# version 25 (final)
# baseline (speedup 1.0000x reference)
import contextlib
import numpy as np
import ml_dtypes

import concourse.bass as bass
import concourse.mybir as mybir
from concourse.bass_utils import run_bass_kernel_spmd

F32 = mybir.dt.float32
BF16 = mybir.dt.bfloat16
AF = mybir.ActivationFunctionType
ALU = mybir.AluOpType
AX = mybir.AxisListType

B, S, D = 4, 2048, 2048
H, HD, RD, ND = 16, 128, 64, 64
QR, KVR = 1024, 512
NE, FF = 8, 1024
NEG = -1.0e30
SCALE = HD ** -0.5
EPS = 1e-5
TILES = ([0, 3, 4, 7, 8, 11, 12, 15], [1, 2, 5, 6, 9, 10, 13, 14])
EXT = [2 * j + 2 for j in range(8)]
TOK = 1024
ENGINES = ("tensor", "vector", "scalar", "gpsimd", "sync")

_cache = {}
DBG_SC = True
DBG_ST = True
DBG_AT = True
WAITMAP = {}
REGION = ["init"]



def build_program(debug=False):
    nc = bass.Bass("TRN2", target_bir_lowering=False)

    def din(name, shape, dt=F32):
        return nc.dram_tensor(name, shape, dt, kind="ExternalInput").ap()

    x_kv = din("x_kv", [S, D])
    x_own = din("x_own", [TOK, D])
    Wqd = din("Wq_down", [128, 16 * QR])
    Wqu = din("Wq_up", [128, 8 * H * ND])
    Wqr = din("Wq_rope", [128, 8 * H * RD])
    Wkv = din("Wkv_down", [128, 16 * KVR])
    Wku = din("Wk_up", [8, 128, 4 * 128])
    Wkr = din("Wk_rope", [2, 128, 16 * 512])
    Wvu = din("Wv_up", [8, 128, 4 * 256])
    Wo = din("Wo", [2, 128, 8 * D])
    RW = din("router_w", [128, 16 * NE])
    WG = din("gate_proj", [NE, 4, 128, 16 * 256])
    WU = din("up_proj", [NE, 4, 128, 16 * 256])
    WD = din("down_proj", [NE, 8, 128, 8 * 256])
    identb = din("identb", [128, 128], BF16)
    identf = din("identf", [128, 128], F32)
    maskb = din("maskb", [8, 128, 256], BF16)
    cosq = din("cosq", [128, TOK])
    sinq = din("sinq", [128, TOK])
    cosk = din("cosk", [128, S])
    sink = din("sink", [128, S])
    iotatok = din("iotatok", [128, 8], mybir.dt.int32)
    ebase = din("ebase", [8, 1])
    oobfill = din("oobfill", [NE, 512], mybir.dt.int32)
    h1_dram = nc.dram_tensor("h1_dram", [TOK, D], BF16, kind="Internal").ap()
    h2_dram = nc.dram_tensor("h2_dram", [TOK, D], F32, kind="Internal").ap()
    gidx_dram = nc.dram_tensor("gidx_dram", [NE, 512], mybir.dt.int32, kind="Internal").ap()
    cw_dram = nc.dram_tensor("cw_dram", [TOK, NE], F32, kind="Internal").ap()
    WGb = nc.dram_tensor("wgb_d", [NE, 4, 128, 16 * 256], BF16, kind="Internal").ap()
    WUb = nc.dram_tensor("wub_d", [NE, 4, 128, 16 * 256], BF16, kind="Internal").ap()
    WDb = nc.dram_tensor("wdb_d", [NE, 8, 128, 8 * 256], BF16, kind="Internal").ap()
    out = nc.dram_tensor("out", [TOK, D], F32, kind="ExternalOutput").ap()
    dbg = {}
    if debug:
        for nm, shp, dt in (
            ("d_qcomb", [128, H * TOK], BF16), ("d_kcomb", [128, 2 * S], BF16),
            ("d_avT", [128, H * TOK], BF16), ("d_h1", [128, 8 * D], BF16),
            ("d_h2", [128, 8 * D], F32), ("d_cw", [128, 64], F32),
            ("d_kvlat", [128, 4 * S], BF16), ("d_kpe", [128, 8 * S], BF16),
            ("d_v2", [128, 16 * 256], BF16), ("d_attn", [128, 2048], BF16),
            ("d_scores", [128, 2048], F32), ("d_stats", [128, 8], F32),
            ("d_maskb", [128, 2048], BF16),
            ("d_stat12", [128, 12], F32), ("d_hpre", [128, 2048], F32),
            ("d_attnT", [128, 2048], BF16), ("d_av", [128, 128], F32),
        ):
            dbg[nm] = nc.dram_tensor(nm, shp, dt, kind="ExternalOutput").ap()

    ops = {e: [] for e in ENGINES}
    semh, cnt = {}, {}
    waited = {e: {} for e in ENGINES}
    es = contextlib.ExitStack()
    live = {}

    def mksem(*names):
        for n in names:
            semh[n] = es.enter_context(nc.semaphore(n))
            cnt[n] = 0

    def wait(eng, sname, val=None):
        v = cnt[sname] if val is None else val
        if v <= 0 or waited[eng].get(sname, -1) >= v:
            return
        waited[eng][sname] = v
        reg = REGION[0]

        def run(h, s=semh[sname], vv=v, eng=eng, sname=sname, reg=reg):
            ins = h.wait_ge(s, vv)
            try:
                WAITMAP[ins.ins.name] = (eng, sname, vv, reg)
            except Exception:
                pass
        ops[eng].append(run)

    def waitd(eng, deps):
        best = {}
        for d in deps:
            if d is None:
                continue
            s, v = d
            if best.get(s, -1) < v:
                best[s] = v
        for s, v in best.items():
            wait(eng, s, v)

    def emit(eng, fn, inc=None, by=1):
        tgt = None
        if inc is not None:
            cnt[inc] += by
            tgt = cnt[inc]

        def run(h, fn=fn, inc=inc, by=by):
            r = fn(h)
            if inc is not None:
                r.then_inc(semh[inc], by)
        ops[eng].append(run)
        return (inc, tgt) if inc else None

    def dma(eng, out_ap, in_ap, s):
        return emit(eng, lambda h, o=out_ap, i=in_ap: h.dma_start(out=o, in_=i), inc=s, by=16)

    SB_LO, SB_HI = 16512, 229344
    free_list = [[SB_LO, SB_HI]]
    sizes = {}

    def sb(name, shape, dtype):
        nbytes = shape[1] * (2 if dtype == BF16 else 4)
        nbytes = (nbytes + 255) & ~255
        for blk in free_list:
            if blk[1] - blk[0] >= nbytes:
                off = blk[0]
                blk[0] += nbytes
                sizes[name] = (off, nbytes)
                return nc.alloc_sbuf_tensor_at(name, shape, dtype, offset=off)
        raise RuntimeError(f"SBUF OOM for {name} ({nbytes}B); free={free_list}")

    def psT(name, shape, dtype=F32):
        g = nc.psum_tensor(name, shape, dtype)
        h = g.__enter__()
        live[name] = g
        return h

    def free(*names):
        for n in names:
            if n in live:
                live.pop(n).__exit__(None, None, None)
                continue
            off, nbytes = sizes.pop(n)
            free_list.append([off, off + nbytes])
            free_list.sort()
            merged = [free_list[0]]
            for lo, hi in free_list[1:]:
                if lo == merged[-1][1]:
                    merged[-1][1] = hi
                else:
                    merged.append([lo, hi])
            free_list[:] = merged

    def vsync():
        tok = emit("vector", lambda h: h.engine_nop(), inc="tVE")
        wait("vector", "tVE", cnt["tVE"])
        return tok

    def ssync():
        tok = emit("scalar", lambda h: h.activation(ssync_sb[:], identb_sb[:, 0:1], AF.Copy), inc="tSC")
        wait("scalar", "tSC", cnt["tSC"])
        return tok

    _regc = {}

    def bcreg(h, v):
        if v not in _regc:
            _regc[v] = h.to_reg(v)
        return _regc[v]

    def fence(engines=ENGINES):
        for e in engines:
            for s in ("tPE", "tVE", "tSC", "tGP"):
                wait(e, s, cnt[s])

    mksem("tPE", "tVE", "tSC", "tGP",
          "dINIT", "dWkv", "dWkr0", "dWkr1", "dXSa", "dXSb",
          "dXSc", "dXSd", "dXSe", "dXSf", "dXSg", "dXSh",
          "dWqd", "dWqu", "dWqr", "dWku", "dWvu", "dWo", "dRW",
          "dXO0", "dXO1", "dWg0", "dWg1", "dWu0", "dWu1", "dWd0", "dWd1", "dOUT",
          "dIDB", "dH1S", "dCWS", "dGPF", "dGIX", "dGA0", "dGA1", "dGI0", "dGI1",
          "dCG0", "dCG1", "dSCT", "dH2Z", "dH2L0", "dH2L1", "dIOT", "dPC")

    EV = {"vector": "tVE", "scalar": "tSC", "gpsimd": "tGP"}

    def copy_on(eng, out_ap, in_ap):
        if eng == "scalar":
            return emit(eng, lambda h, o=out_ap, i=in_ap: h.copy(o, i), inc=EV[eng])
        return emit(eng, lambda h, o=out_ap, i=in_ap: h.tensor_copy(o, i), inc=EV[eng])

    def act_copy(out_ap, in_ap, scale=1.0):
        return emit("scalar", lambda h, o=out_ap, i=in_ap, sc=scale:
                    h.activation(o, i, AF.Copy, scale=sc), inc="tSC")

    def mm(psl, wl, rl, stl, spl):
        n = len(psl)
        tok = None
        for i in range(n):
            if i == n - 1:
                tok = emit("tensor", lambda h, p=psl[i], w=wl[i], r=rl[i], st=stl[i], sp=spl[i]:
                           h.matmul(p, w, r, start=st, stop=sp), inc="tPE")
            else:
                emit("tensor", lambda h, p=psl[i], w=wl[i], r=rl[i], st=stl[i], sp=spl[i]:
                     h.matmul(p, w, r, start=st, stop=sp))
        return tok

    ssync_sb = sb("ssync_sb", [128, 1], F32)
    identb_sb = sb("identb_sb", [128, 128], BF16)
    identf_sb = sb("identf_sb", [128, 128], F32)
    maskb_sb = sb("maskb_sb", [128, 8 * 256], BF16)
    cosq_sb = sb("cosq_sb", [128, TOK], F32)
    sinq_sb = sb("sinq_sb", [128, TOK], F32)
    cosk_sb = sb("cosk_sb", [128, S], F32)
    sink_sb = sb("sink_sb", [128, S], F32)
    dma("sync", identb_sb[:], identb[:], "dIDB")
    dma("sync", identf_sb[:], identf[:], "dINIT")
    dma("sync", maskb_sb.rearrange("p (j n) -> p j n", j=8), maskb.rearrange("j p n -> p j n"), "dINIT")
    dma("sync", cosq_sb[:], cosq[:], "dINIT")
    dma("sync", sinq_sb[:], sinq[:], "dINIT")
    dma("sync", cosk_sb[:], cosk[:], "dINIT")
    dma("sync", sink_sb[:], sink[:], "dINIT")
    iot_sb = sb("iot_sb", [128, 8], mybir.dt.int32)
    ebase_sb = sb("ebase_sb", [8, 1], F32)
    dma("sync", iot_sb[:], iotatok[:], "dIOT")
    dma("sync", ebase_sb[:], ebase[:], "dIOT")
    dma("sync", gidx_dram[:], oobfill[:], "dGPF")

    Wkv_sb = sb("Wkv_sb", [128, 16 * KVR], BF16)
    dma("gpsimd", Wkv_sb[:], Wkv[:], "dWkv")
    Wkr_sb = [sb("Wkr_sb0", [128, 16 * 512], BF16), sb("Wkr_sb1", [128, 16 * 512], BF16)]
    for hh in range(2):
        dma("gpsimd", Wkr_sb[hh][:], Wkr[hh], f"dWkr{hh}")

    if debug:
        wait("sync", "dINIT")
        dma("sync", dbg["d_maskb"][:], maskb_sb[:], "dOUT")
    xT_kv = sb("xT_kv", [128, 16 * S], BF16)
    xslab = [sb(f"xslab{i}", [128, D], BF16) for i in range(8)]
    XSEMS = ["dXSa", "dXSb", "dXSc", "dXSd", "dXSe", "dXSf", "dXSg", "dXSh"]
    ptr = [psT("ptr0", [128, 512], BF16), psT("ptr1", [128, 512], BF16)]
    ptr_dep = [[], []]
    slab_dep = [[] for _ in range(8)]
    wait("tensor", "dIDB")

    def transpose_in(n_slab, dst_sb, dst_span, src_dram, sem_ld, wdma=()):
        deps = []
        for gs in range(n_slab // 4):
            for q in range(4):
                s = gs * 4 + q
                buf = s % 8
                waitd("gpsimd", slab_dep[buf])
                dma("gpsimd", xslab[buf][:], src_dram[s * 128:(s + 1) * 128, :], XSEMS[buf])
                if s < len(wdma):
                    wsb, wsrc, wsem = wdma[s]
                    dma("gpsimd", wsb[:], wsrc[:], wsem)
            for q in range(4):
                sem = XSEMS[(gs * 4 + q) % 8]
                wait("tensor", sem, cnt[sem])
            for f in range(16):
                pidx = f % 2
                waitd("tensor", ptr_dep[pidx])
                tp = None
                for q in range(4):
                    buf = (gs * 4 + q) % 8
                    tp = emit("tensor", lambda h, o=ptr[pidx][:, q * 128:(q + 1) * 128],
                              i=xslab[buf][:, f * 128:(f + 1) * 128]:
                              h.transpose(o, i, identb_sb[:]), inc="tPE")
                    if f == 15:
                        slab_dep[buf] = [tp]
                eng = "vector" if f % 2 == 0 else "scalar"
                waitd(eng, [tp])
                ev = copy_on(eng, dst_sb[:, f * dst_span + gs * 512: f * dst_span + gs * 512 + 512],
                             ptr[pidx][:])
                ptr_dep[pidx] = [ev]
                deps.append(ev)
        return deps

    REGION[0] = "A1-xkv"
    xTkv_dep = transpose_in(16, xT_kv, S, x_kv, None)
    free(*[f"xslab{i}" for i in range(8)])

    REGION[0] = "A2-kvlat"
    kv_latT = sb("kv_latT", [128, 4 * S], BF16)
    psA = [psT("psA0", [128, 512], F32), psT("psA1", [128, 512], F32)]
    psA_dep = [[], []]
    kvlat_dep = []
    wait("tensor", "dWkv")
    waitd("tensor", xTkv_dep)
    ti = 0
    for oc in range(4):
        for tg in range(4):
            pidx = ti % 2
            ti += 1
            waitd("tensor", psA_dep[pidx])
            tok = mm([psA[pidx][:]] * 16,
                     [Wkv_sb[:, kc * KVR + oc * 128: kc * KVR + oc * 128 + 128] for kc in range(16)],
                     [xT_kv[:, kc * S + tg * 512: kc * S + tg * 512 + 512] for kc in range(16)],
                     [kc == 0 for kc in range(16)], [kc == 15 for kc in range(16)])
            eng = "vector" if ti % 2 == 0 else "scalar"
            waitd(eng, [tok])
            ev = copy_on(eng, kv_latT[:, oc * S + tg * 512: oc * S + tg * 512 + 512], psA[pidx][:])
            psA_dep[pidx] = [ev]
            kvlat_dep.append(ev)

    REGION[0] = "A3-kpe"
    kpe1 = sb("kpe1", [128, 4 * S], BF16)
    kpe2 = sb("kpe2", [128, 4 * S], BF16)
    tmpA = [sb("tmpA0", [128, 512], F32), sb("tmpA1", [128, 512], F32)]
    tmpB = [sb("tmpB0", [128, 512], F32), sb("tmpB1", [128, 512], F32)]
    tmp_rd = []
    kpe_dep = []
    wait("vector", "dINIT")
    wait("gpsimd", "dINIT")
    psA2 = [psT("psA2", [128, 512], F32), psT("psA3", [128, 512], F32)]
    psA2_dep = [[], []]

    def rope6(ps1, ps2, cos_ap, sin_ap, d1, d2, w):
        waitd("vector", tmp_rd)
        t1c = emit("vector", lambda h, o=tmpA[0][:, 0:w], i=ps1, c=cos_ap: h.tensor_mul(o, i, c), inc="tVE")
        t1s = emit("vector", lambda h, o=tmpA[1][:, 0:w], i=ps1, c=sin_ap: h.tensor_mul(o, i, c), inc="tVE")
        t2c = emit("vector", lambda h, o=tmpB[0][:, 0:w], i=ps2, c=cos_ap: h.tensor_mul(o, i, c), inc="tVE")
        t2s = emit("vector", lambda h, o=tmpB[1][:, 0:w], i=ps2, c=sin_ap: h.tensor_mul(o, i, c), inc="tVE")
        waitd("gpsimd", [t1c, t1s, t2c, t2s])
        o1 = emit("gpsimd", lambda h, o=d1: h.tensor_sub(o, tmpA[0][:, 0:w], tmpB[1][:, 0:w]), inc="tGP")
        o2 = emit("gpsimd", lambda h, o=d2: h.tensor_add(o, tmpA[1][:, 0:w], tmpB[0][:, 0:w]), inc="tGP")
        tmp_rd[:] = [o1, o2]
        return [o1, o2], [t1c, t1s], [t2c, t2s]

    for qd in range(4):
        wait("tensor", f"dWkr{qd // 2}")
        for tg in range(4):
            wb = Wkr_sb[qd // 2]
            woff = (qd % 2) * 256
            waitd("tensor", psA2_dep[0])
            tok1 = mm([psA2[0][:]] * 16,
                      [wb[:, kc * 512 + woff: kc * 512 + woff + 128] for kc in range(16)],
                      [xT_kv[:, kc * S + tg * 512: kc * S + tg * 512 + 512] for kc in range(16)],
                      [kc == 0 for kc in range(16)], [kc == 15 for kc in range(16)])
            waitd("tensor", psA2_dep[1])
            tok2 = mm([psA2[1][:]] * 16,
                      [wb[:, kc * 512 + woff + 128: kc * 512 + woff + 256] for kc in range(16)],
                      [xT_kv[:, kc * S + tg * 512: kc * S + tg * 512 + 512] for kc in range(16)],
                      [kc == 0 for kc in range(16)], [kc == 15 for kc in range(16)])
            waitd("vector", [tok1, tok2])
            col = qd * S + tg * 512
            deps, f1, f2 = rope6(psA2[0][:], psA2[1][:],
                                 cosk_sb[:, tg * 512: tg * 512 + 512],
                                 sink_sb[:, tg * 512: tg * 512 + 512],
                                 kpe1[:, col:col + 512], kpe2[:, col:col + 512], 512)
            psA2_dep[0] = f1
            psA2_dep[1] = f2
            kpe_dep += deps

    free("Wkv_sb", "cosk_sb", "sink_sb", "Wkr_sb0", "Wkr_sb1")

    if debug:
        waitd("sync", kpe_dep)
        dma("sync", dbg["d_kpe"][:, 0:4 * S], kpe1[:], "dOUT")
        dma("sync", dbg["d_kpe"][:, 4 * S:8 * S], kpe2[:], "dOUT")
        waitd("sync", kvlat_dep)
        dma("sync", dbg["d_kvlat"][:], kv_latT[:], "dOUT")

    free("xT_kv")
    fence(("gpsimd", "sync"))
    Wqd_sb = sb("Wqd_sb", [128, 16 * QR], BF16)
    Wqu_sb = sb("Wqu_sb", [128, 8 * H * ND], BF16)
    Wqr_sb = sb("Wqr_sb", [128, 8 * H * RD], BF16)

    xT_own = sb("xT_own", [128, 16 * TOK], BF16)
    wait("vector", "tPE", cnt["tPE"])
    wait("scalar", "tPE", cnt["tPE"])
    REGION[0] = "A5-xown"
    xslab[:] = [sb(f"xslabB{i}", [128, D], BF16) for i in range(8)]
    for i in range(8):
        slab_dep[i] = []
    xTown_dep = transpose_in(8, xT_own, TOK, x_own, None, wdma=[
        (Wqd_sb, Wqd, "dWqd"), (Wqu_sb, Wqu, "dWqu"), (Wqr_sb, Wqr, "dWqr")])

    free(*[f"xslabB{i}" for i in range(8)])
    REGION[0] = "A6-qlat"
    q_latT = sb("q_latT", [128, 8 * TOK], BF16)
    qlat_dep = []
    wait("tensor", "dWqd")
    waitd("tensor", xTown_dep)
    ti = 0
    for oc in range(8):
        for tg in range(2):
            pidx = ti % 2
            ti += 1
            waitd("tensor", psA_dep[pidx])
            tok = mm([psA[pidx][:]] * 16,
                     [Wqd_sb[:, kc * QR + oc * 128: kc * QR + oc * 128 + 128] for kc in range(16)],
                     [xT_own[:, kc * TOK + tg * 512: kc * TOK + tg * 512 + 512] for kc in range(16)],
                     [kc == 0 for kc in range(16)], [kc == 15 for kc in range(16)])
            eng = "vector" if ti % 2 == 0 else "scalar"
            waitd(eng, [tok])
            ev = copy_on(eng, q_latT[:, oc * TOK + tg * 512: oc * TOK + tg * 512 + 512], psA[pidx][:])
            psA_dep[pidx] = [ev]
            qlat_dep.append(ev)

    free("Wqd_sb", "xT_own")
    REGION[0] = "A7-qcomb"
    q_comb = sb("q_comb", [128, H * TOK], BF16)
    qcomb_dep = []
    wait("tensor", "dWqu")
    waitd("tensor", qlat_dep)
    ti = 0
    for hp in range(8):
        for tg in range(2):
            pidx = ti % 2
            ti += 1
            waitd("tensor", psA_dep[pidx])
            tok = mm([psA[pidx][:]] * 8,
                     [Wqu_sb[:, kc * H * ND + hp * 128: kc * H * ND + hp * 128 + 128] for kc in range(8)],
                     [q_latT[:, kc * TOK + tg * 512: kc * TOK + tg * 512 + 512] for kc in range(8)],
                     [kc == 0 for kc in range(8)], [kc == 7 for kc in range(8)])
            waitd("scalar", [tok])
            e1 = act_copy(q_comb[0:64, (2 * hp) * TOK + tg * 512: (2 * hp) * TOK + tg * 512 + 512],
                          psA[pidx][0:64, :], scale=SCALE)
            e2 = act_copy(q_comb[0:64, (2 * hp + 1) * TOK + tg * 512: (2 * hp + 1) * TOK + tg * 512 + 512],
                          psA[pidx][64:128, :], scale=SCALE)
            psA_dep[pidx] = [e2]
            qcomb_dep += [e1, e2]
    wait("tensor", "dWqr")
    qr1 = sb("qr1", [128, 512], BF16)
    qr2 = sb("qr2", [128, 512], BF16)
    qr_rd = []
    for qd in range(4):
        for tg in range(2):
            waitd("tensor", psA2_dep[0])
            tok1 = mm([psA2[0][:]] * 8,
                      [Wqr_sb[:, kc * H * RD + qd * 256: kc * H * RD + qd * 256 + 128] for kc in range(8)],
                      [q_latT[:, kc * TOK + tg * 512: kc * TOK + tg * 512 + 512] for kc in range(8)],
                      [kc == 0 for kc in range(8)], [kc == 7 for kc in range(8)])
            waitd("tensor", psA2_dep[1])
            tok2 = mm([psA2[1][:]] * 8,
                      [Wqr_sb[:, kc * H * RD + qd * 256 + 128: kc * H * RD + qd * 256 + 256] for kc in range(8)],
                      [q_latT[:, kc * TOK + tg * 512: kc * TOK + tg * 512 + 512] for kc in range(8)],
                      [kc == 0 for kc in range(8)], [kc == 7 for kc in range(8)])
            waitd("vector", [tok1, tok2])
            waitd("vector", qr_rd)
            waitd("gpsimd", qr_rd)
            deps, f1, f2 = rope6(psA2[0][:], psA2[1][:],
                                 cosq_sb[:, tg * 512: tg * 512 + 512],
                                 sinq_sb[:, tg * 512: tg * 512 + 512],
                                 qr1[:], qr2[:], 512)
            psA2_dep[0] = f1
            psA2_dep[1] = f2
            qr_rd = []
            for i_ in range(4):
                hh2 = 4 * qd + i_
                eng = "vector" if i_ % 2 == 0 else "scalar"
                waitd(eng, deps)
                c1 = copy_on(eng, q_comb[64:96, hh2 * TOK + tg * 512: hh2 * TOK + tg * 512 + 512],
                             qr1[i_ * 32:(i_ + 1) * 32, :])
                c2 = copy_on(eng, q_comb[96:128, hh2 * TOK + tg * 512: hh2 * TOK + tg * 512 + 512],
                             qr2[i_ * 32:(i_ + 1) * 32, :])
                qr_rd += [c1, c2]
                qcomb_dep += [c1, c2]
    free("qr1", "qr2")
    free("Wqu_sb", "Wqr_sb", "q_latT", "cosq_sb", "sinq_sb",
         "tmpA0", "tmpA1", "tmpB0", "tmpB1")

    if debug:
        waitd("sync", qcomb_dep)
        dma("sync", dbg["d_qcomb"][:], q_comb[:], "dOUT")

    fence(("gpsimd", "sync"))
    Wku_b = [sb("Wku_b0", [128, 512], BF16), sb("Wku_b1", [128, 512], BF16)]
    Wvu_b = [sb("Wvu_b0", [128, 1024], BF16), sb("Wvu_b1", [128, 1024], BF16)]
    wkv_rd = [[], []]
    dma("gpsimd", Wku_b[0][:], Wku[0], "dWku")
    dma("gpsimd", Wvu_b[0][:], Wvu[0], "dWvu")
    Wo_sbA = sb("Wo_sbA", [128, 8 * D], BF16)
    dma("gpsimd", Wo_sbA[:], Wo[0], "dWo")

    free("psA3", "psA2", "psA1", "psA0")
    psS = psT("psS0", [128, 2048], F32)
    psS_dep = []
    pav = [psT("pav0", [128, 128], F32), psT("pav1", [128, 128], F32)]
    pav_dep = [[], []]

    avT = sb("avT", [128, H * TOK], BF16)
    k_comb2 = [sb("k_comb_a", [128, 2 * S], BF16), sb("k_comb_b", [128, 2 * S], BF16)]
    v2b = [sb("v2_a", [128, 16 * 256], BF16), sb("v2_b", [128, 16 * 256], BF16)]
    attn2 = [sb("attn_sb0", [128, 2048], BF16), sb("attn_sb1", [128, 2048], BF16)]
    attnT2 = [sb("attnT_sb0", [128, 2048], BF16), sb("attnT_sb1", [128, 2048], BF16)]
    nmx = sb("nmx", [128, 2], F32)
    den = sb("den", [128, 2], F32)
    rden = sb("rden", [128, 2], F32)
    avT_dep = []
    nmx_r = [[], []]

    slot_ctr = [0]
    bank_pos = [0]
    bank_dep = [[], [], [], []]
    pend = []
    attn_free = [[], []]
    attnT_free = [[], []]

    def flush_slot(p):
        E2, bp2, s2, hh3, j3 = p["E"], p["bp"], p["s_"], p["hh"], p["j"]
        REGION[0] = f"ATT-flush-j{j3}"
        vb = v2b[p["pp"]]
        waitd("tensor", [p["nrm"]])
        cop_toks = []
        ngrp = (E2 + 3) // 4
        for g in range(ngrp):
            c0, c1 = g * 4, min(g * 4 + 4, E2)
            pidx2 = g % 2
            waitd("tensor", ptr_dep[pidx2])
            tp = None
            for c in range(c0, c1):
                tp = emit("tensor", lambda h, o=ptr[pidx2][:, (c - c0) * 128:(c - c0) * 128 + 128],
                          i=attn2[bp2][:, c * 128: c * 128 + 128]:
                          h.transpose(o, i, identb_sb[:]), inc="tPE")
            if g == ngrp - 1:
                attn_free[bp2] = [tp]
            eng = "vector" if g % 2 == 0 else "scalar"
            waitd(eng, [tp])
            waitd(eng, attnT_free[bp2])
            ev = copy_on(eng, attnT2[bp2][:, c0 * 128: c1 * 128], ptr[pidx2][:, 0:(c1 - c0) * 128])
            ptr_dep[pidx2] = [ev]
            cop_toks.append((ev, c0, c1))
        waitd("tensor", pav_dep[bp2])
        av_tok = None
        for ev, c0, c1 in cop_toks:
            waitd("tensor", [ev])
            for c in range(c0, c1):
                av_tok = mm([pav[bp2][:]],
                            [vb[:, c * 256 + s2 * 128: c * 256 + s2 * 128 + 128]],
                            [attnT2[bp2][:, c * 128: c * 128 + 128]],
                            [c == 0], [c == E2 - 1])
        attnT_free[bp2] = [av_tok]
        pair_av[p["pp"]] = pair_av[p["pp"]] + [av_tok]
        eng = "vector" if j3 % 2 == 0 else "scalar"
        waitd(eng, [av_tok])
        ev = copy_on(eng, avT[:, hh3 * TOK + j3 * 128: hh3 * TOK + j3 * 128 + 128], pav[bp2][:])
        pav_dep[bp2] = [ev]
        avT_dep.append(ev)

    waitd("tensor", kvlat_dep)
    ti = 0
    pair_av = [[], []]
    kc_ready_p = [[], []]
    REGION[0] = "ATT"
    pctasks = []
    for e in range(NE):
        for sl4 in range(4):
            pctasks.append((WGb[e, sl4], WG[e, sl4]))
            pctasks.append((WUb[e, sl4], WU[e, sl4]))
        for dg in range(8):
            pctasks.append((WDb[e, dg], WD[e, dg]))
    for hp in range(8):
        pp = hp % 2
        k_comb = k_comb2[pp]
        v2 = v2b[pp]
        asm_dep = [[], [], [], []]
        wait("tensor", "dWku")
        wait("tensor", "dWvu")
        for tg in range(4):
            pidx = ti % 4
            ti += 1
            waitd("tensor", bank_dep[pidx])
            waitd("tensor", asm_dep[pidx])
            sl = psS[:, pidx * 512:(pidx + 1) * 512]
            tok = mm([sl] * 4,
                     [Wku_b[pp][:, kc * 128:(kc + 1) * 128] for kc in range(4)],
                     [kv_latT[:, kc * S + tg * 512: kc * S + tg * 512 + 512] for kc in range(4)],
                     [kc == 0 for kc in range(4)], [kc == 3 for kc in range(4)])
            eng = "vector" if tg % 2 == 0 else "scalar"
            waitd(eng, [tok])
            waitd(eng, pair_av[pp])
            e1 = copy_on(eng, k_comb[0:64, 0 * S + tg * 512: 0 * S + tg * 512 + 512], sl[0:64, :])
            e2 = copy_on(eng, k_comb[0:64, 1 * S + tg * 512: 1 * S + tg * 512 + 512], sl[64:128, :])
            asm_dep[pidx] = [e2]
        for s_ in range(2):
            hh2 = 2 * hp + s_
            qd2, i2 = hh2 // 4, hh2 % 4
            eng = "vector" if s_ == 0 else "scalar"
            waitd(eng, kpe_dep)
            waitd(eng, pair_av[pp])
            copy_on(eng, k_comb[64:96, s_ * S: s_ * S + S],
                    kpe1[i2 * 32:(i2 + 1) * 32, qd2 * S: qd2 * S + S])
            copy_on(eng, k_comb[96:128, s_ * S: s_ * S + S],
                    kpe2[i2 * 32:(i2 + 1) * 32, qd2 * S: qd2 * S + S])
        kc_ready = [("tVE", cnt["tVE"]), ("tSC", cnt["tSC"])]
        for tc in range(16):
            pidx = ti % 4
            ti += 1
            waitd("tensor", asm_dep[pidx])
            waitd("tensor", bank_dep[pidx])
            sl = psS[:, pidx * 512: pidx * 512 + 256]
            tok = mm([sl] * 4,
                     [kv_latT[:, kc * S + tc * 128: kc * S + tc * 128 + 128] for kc in range(4)],
                     [Wvu_b[pp][:, kc * 256:(kc + 1) * 256] for kc in range(4)],
                     [kc == 0 for kc in range(4)], [kc == 3 for kc in range(4)])
            eng = "vector" if tc % 2 == 0 else "scalar"
            waitd(eng, [tok])
            waitd(eng, pair_av[pp])
            ev = copy_on(eng, v2[:, tc * 256: tc * 256 + 256], sl)
            asm_dep[pidx] = [ev]
        for b in range(4):
            bank_dep[b] = bank_dep[b] + asm_dep[b]
        bank_pos[0] = 0
        v_ready = [("tVE", cnt["tVE"]), ("tSC", cnt["tSC"])]
        wkv_rd[pp] = [("tPE", cnt["tPE"])]
        if hp < 7:
            waitd("gpsimd", wkv_rd[(hp + 1) % 2])
            dma("gpsimd", Wku_b[(hp + 1) % 2][:], Wku[hp + 1], "dWku")
            dma("gpsimd", Wvu_b[(hp + 1) % 2][:], Wvu[hp + 1], "dWvu")

        if debug and hp == 7:
            waitd("sync", kc_ready)
            dma("sync", dbg["d_kcomb"][:], k_comb[:], "dOUT")
            waitd("sync", v_ready)
            dma("sync", dbg["d_v2"][:], v2[:], "dOUT")

        for j in range(8):
            for s_ in range(2):
                hh = 2 * hp + s_
                E = EXT[j]
                cols = E * 128
                nbank = (cols + 511) // 512
                bp = slot_ctr[0] % 2
                slot_ctr[0] += 1
                if bank_pos[0] + nbank > 4:
                    bank_pos[0] = 0
                b0 = bank_pos[0]
                bank_pos[0] += nbank
                waitd("tensor", sum((bank_dep[b] for b in range(b0, b0 + nbank)), []))
                base = b0 * 512
                qsl = q_comb[:, hh * TOK + j * 128: hh * TOK + j * 128 + 128]
                REGION[0] = f"ATT-score-j{j}"
                waitd("tensor", qcomb_dep)
                waitd("tensor", kc_ready)
                psl, wl, rl, stl, spl = [], [], [], [], []
                off = 0
                while off < cols:
                    n = min(512, cols - off)
                    psl.append(psS[:, base + off:base + off + n])
                    wl.append(qsl)
                    rl.append(k_comb[:, s_ * S + off: s_ * S + off + n])
                    stl.append(True)
                    spl.append(off + n < cols)
                    off += n
                psl.append(psS[:, base + cols - 256:base + cols])
                wl.append(identb_sb[:])
                rl.append(maskb_sb[:, j * 256: j * 256 + 256])
                stl.append(False)
                spl.append(True)
                score_tok = mm(psl, wl, rl, stl, spl)
                if len(pend) >= 2:
                    flush_slot(pend.pop(0))
                REGION[0] = f"ATT-smax-j{j}"
                waitd("vector", [score_tok])
                waitd("vector", nmx_r[bp])
                emit("vector", lambda h, o=nmx[:, bp:bp + 1], i=psS[:, base:base + cols]:
                     h.reduce_max(o, i, axis=AX.X, negate=True), inc="tVE")
                nm_done = [("tVE", cnt["tVE"])]
                waitd("scalar", nm_done)
                waitd("scalar", attn_free[bp])
                exp_tok = emit("scalar",
                    lambda h, o=attn2[bp][:, 0:cols], i=psS[:, base:base + cols], b2=nmx[:, bp:bp + 1],
                    a2=den[:, bp:bp + 1]:
                    h.activation(o, i, AF.Exp, bias=b2, scale=1.0, accum_out=a2), inc="tSC")
                for b in range(b0, b0 + nbank):
                    bank_dep[b] = [exp_tok]
                nmx_r[bp] = [exp_tok]
                waitd("vector", [exp_tok])
                emit("vector", lambda h, o=rden[:, bp:bp + 1], i=den[:, bp:bp + 1]:
                     h.reciprocal(o, i), inc="tVE")
                vsync()
                nrm = emit("vector", lambda h, w=cols, b2=bp: h.tensor_single_scalar(
                    attn2[b2][:, 0:w], attn2[b2][:, 0:w], rden[:, b2:b2 + 1], ALU.mult), inc="tVE")
                pend.append(dict(E=E, cols=cols, s_=s_, hh=hh, j=j, bp=bp, nrm=nrm, pp=pp))
        for o_, i_ in pctasks[hp * 16:(hp + 1) * 16]:
            dma("gpsimd", o_, i_, "dPC")
    while pend:
        flush_slot(pend.pop(0))

    if debug:
        waitd("sync", avT_dep)
        dma("sync", dbg["d_avT"][:], avT[:], "dOUT")


    free("k_comb_a", "k_comb_b", "v2_a", "v2_b", "attn_sb0", "attn_sb1", "attnT_sb0", "attnT_sb1", "q_comb", "kpe1", "kpe2", "kv_latT",
         "Wku_b0", "Wku_b1", "Wvu_b0", "Wvu_b1", "maskb_sb", "nmx", "den", "rden",
         "pav1", "pav0", "psS0")
    fence()
    Wo_sbB = sb("Wo_sbB", [128, 8 * D], BF16)
    dma("gpsimd", Wo_sbB[:], Wo[1], "dWo")
    REGION[0] = "LN1"
    po = psT("po", [128, 2048], F32)
    po_dep = []
    po_dep_a = []
    xo = [sb("xo0", [128, D], F32), sb("xo1", [128, D], F32)]
    xo_dep = [[], []]
    hpre = [sb("hpre0", [128, D], F32), sb("hpre1", [128, D], F32)]
    hpre_dep = [[], []]
    sqs = sb("sqs", [128, 512], F32)
    stat = sb("stat", [128, 12], F32)
    if debug:
        dstat12_sb = sb("dstat12_sb", [128, 12], F32)
        dhpre_sb = sb("dhpre_sb", [128, 2048], F32)
    h1b = sb("h1b", [128, 8 * D], BF16)
    h1T0 = sb("h1T0", [128, 8 * TOK], BF16)
    h1T1 = sb("h1T1", [128, 8 * TOK], BF16)

    def h1T_ap(fc, a, b):
        t = h1T0 if fc < 8 else h1T1
        return t[:, (fc % 8) * TOK + a: (fc % 8) * TOK + b]
    h1b_dep, h1T_dep = [], []
    stat_r, sq_r = [], []

    wait("tensor", "dWo")
    waitd("tensor", avT_dep)

    def ln_stats(bi, rs):
        waitd("scalar", [rs])
        waitd("scalar", sq_r)
        for qq in range(4):
            wait("scalar", "tSC", cnt["tSC"])
            emit("scalar", lambda h, i=hpre[bi][:, qq * 512:(qq + 1) * 512], ac=stat[:, 1 + qq:2 + qq]:
                 h.activation(sqs[:], i, AF.Square, accum_out=ac), inc="tSC")
        ssync()
        sq_done = [("tSC", cnt["tSC"])]
        waitd("vector", sq_done)
        emit("vector", lambda h: h.reduce_sum(stat[:, 5:6], stat[:, 1:5], axis=AX.X), inc="tVE")
        emit("vector", lambda h: h.tensor_scalar_mul(stat[:, 6:7], stat[:, 0:1], 1.0 / D), inc="tVE")
        vsync()
        emit("vector", lambda h: h.tensor_scalar(stat[:, 7:8], stat[:, 6:7], stat[:, 6:7][:, 0:1], EPS,
                                                 ALU.mult, ALU.subtract), inc="tVE")
        vsync()
        emit("vector", lambda h: h.scalar_tensor_tensor(stat[:, 8:9], stat[:, 5:6], 1.0 / D, stat[:, 7:8],
                                                        ALU.mult, ALU.subtract), inc="tVE")
        var_done = [("tVE", cnt["tVE"])]
        waitd("scalar", var_done)
        emit("scalar", lambda h: h.activation(stat[:, 9:10], stat[:, 8:9], AF.Ln), inc="tSC")
        ssync()
        return emit("scalar", lambda h: h.activation(stat[:, 10:11], stat[:, 9:10], AF.Exp, scale=-0.5), inc="tSC")

    for tc in range(8):
        bi = tc % 2
        waitd("sync", xo_dep[bi])
        dma("sync", xo[bi][:], x_own[tc * 128:(tc + 1) * 128, :], f"dXO{bi}")
        tok = tokh = None
        for dq in range(4):
            waitd("tensor", po_dep_a if dq < 2 else po_dep)
            tok = mm([po[:, dq * 512:(dq + 1) * 512]] * 16,
                     [avT[:, kc * TOK + tc * 128: kc * TOK + tc * 128 + 128] for kc in range(16)],
                     [(Wo_sbA if kc < 8 else Wo_sbB)[:, (kc % 8) * D + dq * 512: (kc % 8) * D + dq * 512 + 512] for kc in range(16)],
                     [kc == 0 for kc in range(16)], [kc == 15 for kc in range(16)])
            if dq == 1:
                tokh = tok
        waitd("vector", [tokh])
        wait("vector", f"dXO{bi}")
        waitd("vector", hpre_dep[bi])
        waitd("vector", stat_r)
        rs_a = emit("vector", lambda h, o=hpre[bi][:, 0:1024], p=po[:, 0:1024], x=xo[bi][:, 0:1024]:
                    h.scalar_tensor_tensor(o, p, 1.0, x, ALU.mult, ALU.add, accum_out=stat[:, 0:1]), inc="tVE")
        waitd("vector", [tok])
        rs_b = emit("vector", lambda h, o=hpre[bi][:, 1024:2048], p=po[:, 1024:2048], x=xo[bi][:, 1024:2048]:
                    h.scalar_tensor_tensor(o, p, 1.0, x, ALU.mult, ALU.add, accum_out=stat[:, 11:12]), inc="tVE")
        vsync()
        rs = emit("vector", lambda h: h.tensor_add(stat[:, 0:1], stat[:, 0:1], stat[:, 11:12]), inc="tVE")
        po_dep_a = [rs_a]
        po_dep = [rs_b]
        xo_dep[bi] = [rs_b]
        rstd_tok = ln_stats(bi, rs)
        sq_r = [rstd_tok]
        waitd("vector", [rstd_tok])
        vsync()
        nh = emit("vector", lambda h, o=h1b[:, tc * D: tc * D + D], i=hpre[bi][:]:
                  h.tensor_scalar(o, i, stat[:, 6:7], stat[:, 10:11], ALU.subtract, ALU.mult), inc="tVE")
        hpre_dep[bi] = [nh]
        stat_r = [nh]
        h1b_dep.append(nh)
        waitd("sync", [nh])
        dma("sync", h1_dram[tc * 128:(tc + 1) * 128, :], h1b[:, tc * D: tc * D + D], "dH1S")
        if debug and tc == 7:
            c1 = copy_on("vector", dstat12_sb[:], stat[:])
            c2 = copy_on("vector", dhpre_sb[:], hpre[0][:])
            waitd("sync", [c1, c2])
            dma("sync", dbg["d_stat12"][:], dstat12_sb[:], "dOUT")
            dma("sync", dbg["d_hpre"][:], dhpre_sb[:], "dOUT")
        for f in range(16):
            pidx = f % 2
            waitd("tensor", ptr_dep[pidx])
            waitd("tensor", [nh])
            tp = emit("tensor", lambda h, o=ptr[pidx][:, 0:128], i=h1b[:, tc * D + f * 128: tc * D + f * 128 + 128]:
                      h.transpose(o, i, identb_sb[:]), inc="tPE")
            eng = "vector" if f % 2 == 0 else "scalar"
            waitd(eng, [tp])
            ev = copy_on(eng, h1T_ap(f, tc * 128, tc * 128 + 128), ptr[pidx][:, 0:128])
            ptr_dep[pidx] = [ev]
            h1T_dep.append(ev)

    if debug:
        free("dstat12_sb", "dhpre_sb")
    free("avT", "Wo_sbA", "Wo_sbB", "xo0", "xo1", "po")

    fence(("gpsimd", "sync"))
    REGION[0] = "ROUTER"
    rw_sb = sb("rw_sb", [128, 16 * NE], BF16)
    dma("gpsimd", rw_sb[:], RW[:], "dRW")
    ps_r = psT("ps_r", [8, 512], F32)
    ptrf = psT("ptrf", [128, 128], F32)
    sc_sb = sb("sc_sb", [8, TOK], F32)
    sc_tok = sb("sc_tok", [128, 64], F32)
    p_sb = sb("p_sb", [128, 64], F32)
    t8a = sb("t8a", [128, 64], F32)
    t8b = sb("t8b", [128, 64], F32)
    t8c = sb("t8c", [128, 64], F32)
    rst8 = sb("rst8", [128, 48], F32)
    cw_sb = sb("cw_sb", [128, 8 * NE], F32)
    ind_sb = sb("ind_sb", [128, 8 * NE], F32)
    msk1_sb = sb("msk1_sb", [128, 8 * NE], F32)
    msk2_sb = sb("msk2_sb", [128, 8 * NE], F32)
    oi12 = sb("oi12", [128, 16], mybir.dt.int32)
    oi12f = sb("oi12f", [128, 16], F32)
    tmp16 = sb("tmp16", [128, 16], F32)
    offs_tok = sb("offs_tok", [128, 8 * NE], F32)
    indT = sb("indT", [8, TOK], F32)
    posA = sb("posA", [8, TOK], F32)
    posB = sb("posB", [8, TOK], F32)
    cw_dep = []
    wait("tensor", "dRW")
    waitd("tensor", h1T_dep)
    scsb_dep = []
    prev_ev = None
    for g in range(2):
        if prev_ev is not None:
            waitd("tensor", [prev_ev])
        tok = mm([ps_r[:]] * 16,
                 [rw_sb[:, kc * NE: kc * NE + NE] for kc in range(16)],
                 [h1T_ap(kc, g * 512, g * 512 + 512) for kc in range(16)],
                 [kc == 0 for kc in range(16)], [kc == 15 for kc in range(16)])
        waitd("scalar", [tok])
        prev_ev = act_copy(sc_sb[:, g * 512:(g + 1) * 512], ps_r[:])
        scsb_dep.append(prev_ev)
    ptrf_rd = None
    sc_tokw = sc_tok
    trtoks = []
    for tc in range(8):
        waitd("tensor", scsb_dep)
        if ptrf_rd is not None:
            waitd("tensor", [ptrf_rd])
        tp = emit("tensor", lambda h, o=ptrf[:, 0:8], i=sc_sb[:, tc * 128: tc * 128 + 128]:
                  h.transpose(o, i, identf_sb[0:8, 0:8]), inc="tPE")
        waitd("vector", [tp])
        ptrf_rd = emit("vector", lambda h, o=sc_tokw[:, tc * 8:(tc + 1) * 8]:
                       h.tensor_copy(o, ptrf[:, 0:8]), inc="tVE")
        trtoks.append(ptrf_rd)

    def step(fn_of_tc, eng="vector"):
        vsync()
        for tc in range(8):
            emit(eng, fn_of_tc(tc), inc=EV[eng])

    def SL(t, tc):
        return t[:, tc * 8:(tc + 1) * 8]

    def RC(tc, i):
        return rst8[:, tc * 6 + i: tc * 6 + i + 1]

    waitd("vector", trtoks)
    step(lambda tc: (lambda h: h.reduce_max(RC(tc, 0), SL(sc_tokw, tc), axis=AX.X, negate=True)))
    sm_tok = [("tVE", cnt["tVE"])]
    waitd("scalar", sm_tok)
    e1 = None
    for tc in range(8):
        e1 = emit("scalar", lambda h, o=SL(p_sb, tc), i=SL(sc_tokw, tc), b=RC(tc, 0):
                  h.activation(o, i, AF.Exp, bias=b), inc="tSC")
    waitd("vector", [e1])
    step(lambda tc: (lambda h: h.reduce_max(RC(tc, 1), SL(p_sb, tc), axis=AX.X)))
    step(lambda tc: (lambda h: h.tensor_single_scalar(SL(t8a, tc), SL(p_sb, tc), RC(tc, 1), ALU.is_ge)))
    step(lambda tc: (lambda h: h.tensor_copy(SL(t8b, tc), SL(t8a, tc))))
    step(lambda tc: (lambda h: h.tensor_add(t8b[:, tc * 8 + 1:tc * 8 + 8], t8a[:, tc * 8 + 1:tc * 8 + 8], t8a[:, tc * 8:tc * 8 + 7])))
    step(lambda tc: (lambda h: h.tensor_copy(SL(t8c, tc), SL(t8b, tc))))
    step(lambda tc: (lambda h: h.tensor_add(t8c[:, tc * 8 + 2:tc * 8 + 8], t8b[:, tc * 8 + 2:tc * 8 + 8], t8b[:, tc * 8:tc * 8 + 6])))
    step(lambda tc: (lambda h: h.tensor_copy(SL(t8b, tc), SL(t8c, tc))))
    step(lambda tc: (lambda h: h.tensor_add(t8b[:, tc * 8 + 4:tc * 8 + 8], t8c[:, tc * 8 + 4:tc * 8 + 8], t8c[:, tc * 8:tc * 8 + 4])))
    step(lambda tc: (lambda h: h.tensor_single_scalar(SL(t8c, tc), SL(t8b, tc), 1.5, ALU.is_le)))
    step(lambda tc: (lambda h: h.tensor_mul(SL(t8c, tc), SL(t8c, tc), SL(t8a, tc))))
    step(lambda tc: (lambda h: h.tensor_scalar(SL(t8a, tc), SL(t8c, tc), -1.0, 1.0, ALU.mult, ALU.add)))
    step(lambda tc: (lambda h: h.tensor_mul(SL(t8a, tc), SL(t8a, tc), SL(p_sb, tc))))
    step(lambda tc: (lambda h: h.reduce_max(RC(tc, 2), SL(t8a, tc), axis=AX.X)))
    step(lambda tc: (lambda h: h.tensor_single_scalar(SL(t8b, tc), SL(t8a, tc), RC(tc, 2), ALU.is_ge)))
    step(lambda tc: (lambda h: h.tensor_copy(SL(sc_tokw, tc), SL(t8b, tc))))
    step(lambda tc: (lambda h: h.tensor_add(sc_tokw[:, tc * 8 + 1:tc * 8 + 8], t8b[:, tc * 8 + 1:tc * 8 + 8], t8b[:, tc * 8:tc * 8 + 7])))
    step(lambda tc: (lambda h: h.tensor_copy(SL(t8a, tc), SL(sc_tokw, tc))))
    step(lambda tc: (lambda h: h.tensor_add(t8a[:, tc * 8 + 2:tc * 8 + 8], sc_tokw[:, tc * 8 + 2:tc * 8 + 8], sc_tokw[:, tc * 8:tc * 8 + 6])))
    step(lambda tc: (lambda h: h.tensor_copy(SL(sc_tokw, tc), SL(t8a, tc))))
    step(lambda tc: (lambda h: h.tensor_add(sc_tokw[:, tc * 8 + 4:tc * 8 + 8], t8a[:, tc * 8 + 4:tc * 8 + 8], t8a[:, tc * 8:tc * 8 + 4])))
    step(lambda tc: (lambda h: h.tensor_single_scalar(SL(t8a, tc), SL(sc_tokw, tc), 1.5, ALU.is_le)))
    step(lambda tc: (lambda h: h.tensor_mul(SL(t8b, tc), SL(t8a, tc), SL(t8b, tc))))
    step(lambda tc: (lambda h: h.tensor_add(SL(ind_sb, tc), SL(t8c, tc), SL(t8b, tc))))
    step(lambda tc: (lambda h: h.tensor_copy(SL(msk1_sb, tc), SL(t8c, tc))))
    step(lambda tc: (lambda h: h.tensor_copy(SL(msk2_sb, tc), SL(t8b, tc))))
    step(lambda tc: (lambda h: h.tensor_add(RC(tc, 3), RC(tc, 1), RC(tc, 2))))
    step(lambda tc: (lambda h: h.reciprocal(RC(tc, 4), RC(tc, 3))))
    step(lambda tc: (lambda h: h.tensor_mul(RC(tc, 1), RC(tc, 1), RC(tc, 4))))
    step(lambda tc: (lambda h: h.tensor_mul(RC(tc, 2), RC(tc, 2), RC(tc, 4))))
    step(lambda tc: (lambda h: h.tensor_single_scalar(SL(t8c, tc), SL(t8c, tc), RC(tc, 1), ALU.mult)))
    step(lambda tc: (lambda h: h.tensor_single_scalar(SL(t8b, tc), SL(t8b, tc), RC(tc, 2), ALU.mult)))
    vsync()
    cw_dep = []
    for tc in range(8):
        cwt = emit("vector", lambda h, o=SL(cw_sb, tc), a=SL(t8c, tc), b=SL(t8b, tc):
                   h.tensor_add(o, a, b), inc="tVE")
        cw_dep.append(cwt)
        waitd("sync", [cwt])
        dma("sync", cw_dram[tc * 128:(tc + 1) * 128, :], cw_sb[:, tc * NE: tc * NE + NE], "dCWS")

    BIGV = 1.0e6
    trT = []
    for tc in range(8):
        waitd("tensor", cw_dep)
        if trT:
            waitd("tensor", trT[-1:])
        tp = emit("tensor", lambda h, i=ind_sb[:, tc * NE: tc * NE + NE]:
                  h.transpose(ptrf[0:8, 0:128], i, identf_sb[:]), inc="tPE")
        waitd("vector", [tp])
        trT.append(emit("vector", lambda h, o=indT[:, tc * 128:(tc + 1) * 128]:
                        h.tensor_copy(o, ptrf[0:8, 0:128]), inc="tVE"))
    cur, oth = posA, posB
    waitd("vector", trT)
    emit("vector", lambda h: h.tensor_copy(cur[:], indT[:]), inc="tVE")
    sh = 1
    while sh < TOK:
        vsync()
        emit("vector", lambda h, a=cur, b=oth, s=sh: h.tensor_copy(b[:, 0:s], a[:, 0:s]), inc="tVE")
        emit("vector", lambda h, a=cur, b=oth, s=sh: h.tensor_add(b[:, s:TOK], a[:, s:TOK], a[:, 0:TOK - s]), inc="tVE")
        cur, oth = oth, cur
        sh *= 2
    vsync()
    emit("vector", lambda h, a=cur, b=oth: h.tensor_mul(b[:], a[:], indT[:]), inc="tVE")
    vsync()
    emit("vector", lambda h, b=oth: h.tensor_scalar(indT[:], indT[:], -BIGV, BIGV - 1.0, ALU.mult, ALU.add), inc="tVE")
    vsync()
    emit("vector", lambda h, b=oth: h.tensor_add(b[:], b[:], indT[:]), inc="tVE")
    vsync()
    wait("vector", "dIOT")
    emit("vector", lambda h, b=oth: h.tensor_single_scalar(b[:], b[:], ebase_sb[:, 0:1], ALU.add), inc="tVE")
    offs_done = [("tVE", cnt["tVE"])]
    trT2 = []
    for tc in range(8):
        waitd("tensor", offs_done)
        if trT2:
            waitd("tensor", trT2[-1:])
        tp = emit("tensor", lambda h, i=oth[0:8, tc * 128:(tc + 1) * 128]: h.transpose(
            ptrf[:, 8:16], i, identf_sb[0:8, 0:8]), inc="tPE")
        waitd("vector", [tp])
        trT2.append(emit("vector", lambda h, o=offs_tok[:, tc * NE: tc * NE + NE]:
                         h.tensor_copy(o, ptrf[:, 8:16]), inc="tVE"))
    waitd("vector", trT2)
    for tc in range(8):
        for pk in range(2):
            mk = msk1_sb if pk == 0 else msk2_sb
            vsync()
            emit("vector", lambda h, a=offs_tok[:, tc * NE: tc * NE + NE],
                 b=mk[:, tc * NE: tc * NE + NE]: h.tensor_mul(tmp16[:, 0:NE], a, b), inc="tVE")
            vsync()
            emit("vector", lambda h, o=oi12f[:, tc * 2 + pk: tc * 2 + pk + 1]:
                 h.reduce_sum(o, tmp16[:, 0:NE], axis=AX.X), inc="tVE")
    vsync()
    ocast = emit("vector", lambda h: h.tensor_copy(oi12[:], oi12f[:]), inc="tVE")
    wait("gpsimd", "dGPF")
    wait("gpsimd", "dIOT")
    waitd("gpsimd", [ocast])
    for tc in range(8):
        for pk in range(2):
            emit("gpsimd", lambda h, tc=tc, pk=pk: h.indirect_dma_start(
                gidx_dram.rearrange("a b -> (a b)").rearrange("(a b) -> a b", b=1),
                bass.IndirectOffsetOnAxis(oi12[:, tc * 2 + pk: tc * 2 + pk + 1], 0),
                iot_sb[:, tc:tc + 1], None,
                bounds_check=bcreg(h, NE * 512 - 1), oob_is_err=False), inc="dGIX", by=16)
    free("sc_sb", "rw_sb", "sc_tok", "p_sb", "t8a", "t8b", "t8c", "rst8",
         "ind_sb", "indT", "posA", "posB", "msk1_sb", "msk2_sb", "offs_tok",
         "oi12f", "tmp16", "identf_sb", "ptrf", "ps_r")

    if debug:
        waitd("sync", cw_dep)
        dma("sync", dbg["d_cw"][:], cw_sb[:], "dOUT")
        waitd("sync", h1b_dep)
        dma("sync", dbg["d_h1"][:], h1b[:], "dOUT")

    fence(("gpsimd", "sync"))
    free("h1T0", "h1T1")
    C = 384
    REGION[0] = "FFN"
    psG = psT("psG", [128, C], F32)
    psU = psT("psU", [128, C], F32)
    psD = [psT("psD0", [128, 256], F32), psT("psD1", [128, 256], F32)]
    ptr2 = [ptr[0], ptr[1]]
    ptr2_dep = [ptr_dep[0], ptr_dep[1]]
    psD_dep = [[], []]
    sct = [sb("sct0", [128, 3 * D], F32), sb("sct1", [128, 3 * D], F32)]
    gathT = [sb("gathT0", [128, 16 * C], BF16), sb("gathT1", [128, 16 * C], BF16)]
    graw = [sb(f"graw{i}", [128, D], BF16) for i in range(4)]
    Wg_buf = [sb("Wg_buf0", [128, 16 * 256], BF16), sb("Wg_buf1", [128, 16 * 256], BF16)]
    Wu_buf = [sb("Wu_buf0", [128, 16 * 256], BF16), sb("Wu_buf1", [128, 16 * 256], BF16)]
    Wd_buf = [sb(f"Wd_buf{i}", [128, 8 * 256], BF16) for i in range(4)]
    sg_bf = sb("sg_bf", [128, C], BF16)
    actT = [sb("actT0", [128, 8 * C], BF16), sb("actT1", [128, 8 * C], BF16)]
    cwg = [sb("cwg0", [128, 24], F32), sb("cwg1", [128, 24], F32)]
    gidxs = [sb("gidxs0", [128, 3], mybir.dt.int32), sb("gidxs1", [128, 3], mybir.dt.int32)]
    emit("vector", lambda h: h.memzero(sct[0][:, 0:D]), inc="tVE")
    for gi in range(4):
        emit("vector", lambda h, gi=gi: h.memzero(graw[gi][:]), inc="tVE")
    zv = ("tVE", cnt["tVE"])
    waitd("sync", [zv])
    for tc in range(8):
        dma("sync", h2_dram[tc * 128:(tc + 1) * 128, :], sct[0][:, 0:D], "dH2Z")

    wg_rd = [[], []]
    wu_rd = [[], []]
    wd_rd = [[], [], [], []]
    psG_dep, psU_dep = [], []
    sg_dep = []
    gidx_rd = [[], []]
    cwg_rd = [[], []]
    graw_rd = [[] for _ in range(4)]
    gathT_rd = [[], []]
    actT_rd = [[], []]
    sct_rd = [[("dH2Z", 128)], []]
    nsct = [0]

    def emit_gather_dma(e):
        be = e % 2
        wait("gpsimd", "dGIX")
        wait("gpsimd", "dH1S")
        wait("gpsimd", "dCWS")
        waitd("gpsimd", gidx_rd[be])
        for k in range(3):
            dma("gpsimd", gidxs[be][:, k:k + 1],
                gidx_dram[e][k * 128:(k + 1) * 128].rearrange("(a b) -> a b", b=1), f"dGI{be}")
        wait("gpsimd", f"dGI{be}")
        waitd("gpsimd", cwg_rd[be])
        for k in range(3):
            gslot = (e * 3 + k) % 4
            waitd("gpsimd", graw_rd[gslot])
            emit("gpsimd", lambda h, gslot=gslot, be=be, k=k: h.indirect_dma_start(
                graw[gslot][:], None, h1_dram[:],
                bass.IndirectOffsetOnAxis(gidxs[be][:, k:k + 1], 0),
                bounds_check=bcreg(h, TOK - 1), oob_is_err=False), inc=f"dGA{be}", by=16)
            emit("gpsimd", lambda h, be=be, k=k: h.indirect_dma_start(
                cwg[be][:, k * 8:(k + 1) * 8], None, cw_dram[:],
                bass.IndirectOffsetOnAxis(gidxs[be][:, k:k + 1], 0),
                bounds_check=bcreg(h, TOK - 1), oob_is_err=False), inc=f"dCG{be}", by=16)
        gidx_rd[be] = [(f"dGA{be}", cnt[f"dGA{be}"])]

    def emit_gather_tp(e):
        REGION[0] = "FFN-gathtp"
        be = e % 2
        wait("tensor", f"dGA{be}", cnt[f"dGA{be}"])
        tptoks = []
        lasttp = {}
        for kc in range(16):
            pidx = kc % 2
            waitd("tensor", ptr2_dep[pidx])
            tp = None
            for k in range(3):
                gslot = (e * 3 + k) % 4
                tp = emit("tensor", lambda h, o=ptr2[pidx][:, k * 128:(k + 1) * 128], gslot=gslot, kc=kc:
                          h.transpose(o, graw[gslot][:, kc * 128:(kc + 1) * 128], identb_sb[:]), inc="tPE")
                if kc == 15:
                    lasttp[gslot] = tp
            eng = "vector" if kc % 2 == 0 else "scalar"
            waitd(eng, [tp])
            waitd(eng, gathT_rd[be])
            ev = copy_on(eng, gathT[be][:, kc * C: kc * C + C], ptr2[pidx][:, 0:C])
            ptr2_dep[pidx] = [ev]
            tptoks.append(ev)
        for k in range(3):
            gslot = (e * 3 + k) % 4
            graw_rd[gslot] = [lasttp[gslot]]
        return tptoks[-2:] + [("tVE", cnt["tVE"]), ("tSC", cnt["tSC"])]

    def flush_scatter(e2, sc_toks2):
        be2 = e2 % 2
        wait("gpsimd", "dH2Z")
        wait("gpsimd", "dSCT", nsct[0] * 16)
        waitd("gpsimd", sc_toks2)
        for tk in range(3):
            emit("gpsimd", lambda h, be=be2, tk=tk: h.indirect_dma_start(
                h2_dram[:], bass.IndirectOffsetOnAxis(gidxs[be][:, tk:tk + 1], 0),
                sct[be][:, tk * D:(tk + 1) * D], None,
                bounds_check=bcreg(h, TOK - 1), oob_is_err=False,
                compute_op=ALU.add), inc="dSCT", by=16)
            nsct[0] += 1
        sct_rd[be2] = [("dSCT", nsct[0] * 16)]

    pend_sct = [None]
    gath_tp_next = [None]
    emit_gather_dma(0)
    for e in range(NE):
        be = e % 2
        gath_done = emit_gather_tp(e)
        REGION[0] = "FFN-gateup"
        if e == 0:
            wait("sync", "dPC", cnt["dPC"])
        for sl in range(4):
            gsem, usem = f"dWg{sl % 2}", f"dWu{sl % 2}"
            waitd("sync", wg_rd[sl % 2])
            dma("sync", Wg_buf[sl % 2][:], WGb[e, sl], gsem)
            waitd("sync", wu_rd[sl % 2])
            dma("sync", Wu_buf[sl % 2][:], WUb[e, sl], usem)
            for fc2 in range(2):
                fc = sl * 2 + fc2
                wait("tensor", gsem)
                wait("tensor", usem)
                waitd("tensor", gath_done)
                waitd("tensor", psG_dep)
                gmm = mm([psG[:]] * 16,
                         [Wg_buf[sl % 2][:, kc * 256 + fc2 * 128: kc * 256 + fc2 * 128 + 128] for kc in range(16)],
                         [gathT[be][:, kc * C: kc * C + C] for kc in range(16)],
                         [kc == 0 for kc in range(16)], [kc == 15 for kc in range(16)])
                waitd("tensor", psU_dep)
                umm = mm([psU[:]] * 16,
                         [Wu_buf[sl % 2][:, kc * 256 + fc2 * 128: kc * 256 + fc2 * 128 + 128] for kc in range(16)],
                         [gathT[be][:, kc * C: kc * C + C] for kc in range(16)],
                         [kc == 0 for kc in range(16)], [kc == 15 for kc in range(16)])
                wg_rd[sl % 2] = [gmm]
                wu_rd[sl % 2] = [umm]
                waitd("scalar", [gmm])
                waitd("scalar", sg_dep)
                sg = emit("scalar", lambda h: h.activation(sg_bf[:], psG[:], AF.Silu), inc="tSC")
                psG_dep = [sg]
                waitd("vector", [sg, umm])
                waitd("vector", actT_rd[be])
                aw = emit("vector", lambda h, o=actT[be][:, fc * C:(fc + 1) * C]:
                          h.tensor_mul(o, sg_bf[:], psU[:]), inc="tVE")
                psU_dep = [aw]
                sg_dep = [aw]
        act_done = [("tVE", cnt["tVE"])]
        gathT_rd[be] = [("tPE", cnt["tPE"])]
        if e < NE - 1:
            emit_gather_dma(e + 1)
        REGION[0] = "FFN-down"
        sc_toks = []
        dmm = None
        wd_tok = {}
        for dg in range(2):
            waitd("sync", wd_rd[dg % 4])
            wd_tok[dg] = dma("sync", Wd_buf[dg % 4][:], WDb[e, dg], f"dWd{dg % 2}")
        for dg in range(8):
            if dg + 2 < 8:
                waitd("sync", wd_rd[(dg + 2) % 4])
                wd_tok[dg + 2] = dma("sync", Wd_buf[(dg + 2) % 4][:], WDb[e, dg + 2], f"dWd{(dg + 2) % 2}")
            for tk in range(3):
                di = (dg * 3 + tk) % 2
                waitd("tensor", [wd_tok[dg]])
                waitd("tensor", act_done)
                waitd("tensor", psD_dep[di])
                dmm = mm([psD[di][:]] * 8,
                         [actT[be][:, kc * C + tk * 128: kc * C + tk * 128 + 128] for kc in range(8)],
                         [Wd_buf[dg % 4][:, kc * 256:(kc + 1) * 256] for kc in range(8)],
                         [kc == 0 for kc in range(8)], [kc == 7 for kc in range(8)])
                wd_rd[dg % 4] = [dmm]
                eng = "vector" if (dg + tk) % 2 == 0 else "scalar"
                waitd(eng, [dmm])
                wait(eng, f"dCG{be}", cnt[f"dCG{be}"])
                waitd(eng, sct_rd[be])
                o = sct[be][:, tk * D + dg * 256: tk * D + dg * 256 + 256]
                cwc = cwg[be][:, k * 8 + e: k * 8 + e + 1] if False else cwg[be][:, tk * 8 + e: tk * 8 + e + 1]
                if eng == "vector":
                    ev = emit(eng, lambda h, o=o, c=cwc, di=di:
                              h.tensor_single_scalar(o, psD[di][:], c, ALU.mult), inc="tVE")
                else:
                    ev = emit(eng, lambda h, o=o, c=cwc, di=di:
                              h.activation(o, psD[di][:], AF.Copy, scale=c), inc="tSC")
                psD_dep[di] = [ev]
                sc_toks.append(ev)
        actT_rd[be] = [("tPE", cnt["tPE"])]
        cwg_rd[be] = sc_toks[-2:]
        flush_scatter(e, sc_toks)

    free("Wg_buf0", "Wg_buf1", "Wu_buf0", "Wu_buf1", "Wd_buf0", "Wd_buf1",
         "sg_bf", "gathT0", "gathT1", "actT0", "actT1",
         "graw0", "graw1", "graw2", "graw3",
         "sct0", "sct1", "cwg0", "cwg1", "gidxs0", "gidxs1")
    REGION[0] = "LN2"
    ob = [sb("ob0", [128, D], F32), sb("ob1", [128, D], F32)]
    h2t = [sb("h2t0", [128, D], F32), sb("h2t1", [128, D], F32)]
    h2t_rd = [[], []]
    ob_dep = [[], []]
    stat_r, sq_r = [], []
    wait("sync", "dSCT", nsct[0] * 16)
    for tc in range(8):
        bi = tc % 2
        waitd("sync", h2t_rd[bi])
        dma("sync", h2t[bi][:], h2_dram[tc * 128:(tc + 1) * 128, :], f"dH2L{bi}")
        wait("vector", f"dH2L{bi}", cnt[f"dH2L{bi}"])
        waitd("vector", h1b_dep)
        waitd("vector", hpre_dep[bi])
        waitd("vector", stat_r)
        rs = emit("vector", lambda h, o=hpre[bi][:], a=h1b[:, tc * D: tc * D + D],
                  b=h2t[bi][:]:
                  h.scalar_tensor_tensor(o, a, 1.0, b, ALU.mult, ALU.add, accum_out=stat[:, 0:1]), inc="tVE")
        h2t_rd[bi] = [rs]
        rstd_tok = ln_stats(bi, rs)
        sq_r = [rstd_tok]
        waitd("vector", [rstd_tok])
        waitd("vector", ob_dep[bi])
        vsync()
        nh = emit("vector", lambda h, o=ob[bi][:], i=hpre[bi][:]:
                  h.tensor_scalar(o, i, stat[:, 6:7], stat[:, 10:11], ALU.subtract, ALU.mult), inc="tVE")
        hpre_dep[bi] = [nh]
        stat_r = [nh]
        waitd("sync", [nh])
        st_tok = dma("sync", out[tc * 128:(tc + 1) * 128, :], ob[bi][:], "dOUT")
        ob_dep[bi] = [st_tok]

    wait("sync", "dOUT")

    with es:
        block = es.enter_context(nc.Block())
        for eng in ENGINES:
            getattr(block, eng)(lambda h, lst=ops[eng]: [f(h) for f in lst])
    return nc


def _sw(W, k):
    K2, n = W.shape
    assert K2 == k * 128
    return np.ascontiguousarray(
        W.reshape(k, 128, n).transpose(1, 0, 2).reshape(128, k * n))


def _host_weights(inputs):
    H_ = H
    perm = []
    for qd in range(H_ // 4):
        for b2 in (0, 1):
            for i_ in range(4):
                perm.extend((4 * qd + i_) * 64 + b2 * 32 + np.arange(32))
    perm = np.asarray(perm)
    Wqr = np.asarray(inputs["L0_attn_Wq_rope"])[:, perm]
    Wkr = np.asarray(inputs["L0_attn_Wk_rope"])[:, perm]
    Wo_ = np.asarray(inputs["L0_attn_Wo"])
    G = np.asarray(inputs["L0_ffn_gate_proj"])
    U = np.asarray(inputs["L0_ffn_up_proj"])
    Dn = np.asarray(inputs["L0_ffn_down_proj"])
    d = {
        "Wq_down": _sw(np.asarray(inputs["L0_attn_Wq_down"]), 16),
        "Wq_up": _sw(np.asarray(inputs["L0_attn_Wq_up"]), 8),
        "Wq_rope": _sw(Wqr, 8),
        "Wkv_down": _sw(np.asarray(inputs["L0_attn_Wkv_down"]), 16),
        "Wk_up": np.stack([_sw(np.ascontiguousarray(
            np.asarray(inputs["L0_attn_Wk_up"])[:, hp * 128:(hp + 1) * 128]), 4) for hp in range(8)]),
        "Wk_rope": np.stack([_sw(np.ascontiguousarray(Wkr[:, hh * 512:(hh + 1) * 512]), 16)
                             for hh in range(2)]),
        "Wv_up": np.stack([_sw(np.ascontiguousarray(
            np.asarray(inputs["L0_attn_Wv_up"])[:, hp * 256:(hp + 1) * 256]), 4) for hp in range(8)]),
        "Wo": np.stack([_sw(np.ascontiguousarray(Wo_[hh * 1024:(hh + 1) * 1024, :]), 8)
                        for hh in range(2)]),
        "router_w": _sw(np.asarray(inputs["L0_ffn_router_w"]), 16),
        "gate_proj": np.stack([np.stack([_sw(np.ascontiguousarray(G[e][:, sl * 256:(sl + 1) * 256]), 16)
                                         for sl in range(4)]) for e in range(NE)]),
        "up_proj": np.stack([np.stack([_sw(np.ascontiguousarray(U[e][:, sl * 256:(sl + 1) * 256]), 16)
                                       for sl in range(4)]) for e in range(NE)]),
        "down_proj": np.stack([np.stack([_sw(np.ascontiguousarray(Dn[e][:, dg * 256:(dg + 1) * 256]), 8)
                                         for dg in range(8)]) for e in range(NE)]),
        "identb": np.eye(128, dtype=ml_dtypes.bfloat16),
        "identf": np.eye(128, dtype=np.float32),
        "iotatok": (np.arange(128)[:, None] + 128 * np.arange(8)[None, :]).astype(np.int32),
        "oobfill": np.full((NE, 512), 10 ** 6, np.int32),
        "ebase": (np.arange(8)[:, None] * 512).astype(np.float32),
    }
    return d


def _host_inputs(inputs, core):
    b, sub = core // 2, core % 2
    tiles = TILES[sub]
    x = np.asarray(inputs["L0_attn_x"])
    x_kv = np.ascontiguousarray(x[b])
    x_own = np.ascontiguousarray(
        np.concatenate([x[b, g * 128:(g + 1) * 128, :] for g in tiles], axis=0))
    m = np.zeros((8, 128, 256), np.float32)
    for j, g in enumerate(tiles):
        E = EXT[j]
        rows = g * 128 + np.arange(128)
        cols = (E - 2) * 128 + np.arange(256)
        m[j] = np.where(cols[None, :] <= rows[:, None], 0.0, NEG)
    half = RD // 2
    inv = 1.0 / (10000.0 ** (np.arange(half, dtype=np.float64) / half))

    def tables(pos, scale):
        ang = pos[None, :] * inv[:, None]
        c = (np.cos(ang) * scale).astype(np.float32)
        s = (np.sin(ang) * scale).astype(np.float32)
        return np.tile(c, (4, 1)), np.tile(s, (4, 1))

    posq = np.concatenate([g * 128 + np.arange(128) for g in tiles]).astype(np.float64)
    posk = np.arange(S, dtype=np.float64)
    cq, sq = tables(posq, SCALE)
    ck, sk = tables(posk, 1.0)
    return {
        "x_kv": x_kv, "x_own": x_own,
        "maskb": m.astype(ml_dtypes.bfloat16),
        "cosq": cq, "sinq": sq, "cosk": ck, "sink": sk,
    }


def kernel(**inputs):
    if "nc" not in _cache:
        _cache["nc"] = build_program(debug=False)
    nc = _cache["nc"]
    shared = _host_weights(inputs)
    in_maps = []
    for c in range(8):
        d = dict(shared)
        d.update(_host_inputs(inputs, c))
        in_maps.append(d)
    res = run_bass_kernel_spmd(nc, in_maps, core_ids=list(range(8)))
    y = np.empty((B, S, D), np.float32)
    for c in range(8):
        b, sub = c // 2, c % 2
        o = res.results[c]["out"]
        for j, g in enumerate(TILES[sub]):
            y[b, g * 128:(g + 1) * 128, :] = o[j * 128:(j + 1) * 128, :]
    return y


# version 26
# speedup vs baseline: 1.1820x; 1.1820x over previous
import contextlib
import numpy as np
import ml_dtypes

import concourse.bass as bass
import concourse.mybir as mybir
from concourse.bass_utils import run_bass_kernel_spmd

F32 = mybir.dt.float32
BF16 = mybir.dt.bfloat16
AF = mybir.ActivationFunctionType
ALU = mybir.AluOpType
AX = mybir.AxisListType

B, S, D = 4, 2048, 2048
H, HD, RD, ND = 16, 128, 64, 64
QR, KVR = 1024, 512
NE, FF = 8, 1024
NEG = -1.0e30
SCALE = HD ** -0.5
EPS = 1e-5
TILES = ([0, 3, 4, 7, 8, 11, 12, 15], [1, 2, 5, 6, 9, 10, 13, 14])
EXT = [2 * j + 2 for j in range(8)]
TOK = 1024
ENGINES = ("tensor", "vector", "scalar", "gpsimd", "sync")

_cache = {}
DBG_SC = True
DBG_ST = True
DBG_AT = True
WAITMAP = {}
REGION = ["init"]



def build_program(debug=False):
    nc = bass.Bass("TRN2", target_bir_lowering=False)

    def din(name, shape, dt=F32):
        return nc.dram_tensor(name, shape, dt, kind="ExternalInput").ap()

    x_kv = din("x_kv", [S, D])
    x_own = din("x_own", [TOK, D])
    Wqd = din("Wq_down", [128, 16 * QR])
    Wqu = din("Wq_up", [128, 8 * H * ND])
    Wqr = din("Wq_rope", [128, 8 * H * RD])
    Wkv = din("Wkv_down", [128, 16 * KVR])
    Wku = din("Wk_up", [8, 128, 4 * 128])
    Wkr = din("Wk_rope", [2, 128, 16 * 512])
    Wvu = din("Wv_up", [8, 128, 4 * 256])
    Wo = din("Wo", [2, 128, 8 * D])
    RW = din("router_w", [128, 16 * NE])
    WG = din("gate_proj", [NE, 4, 128, 16 * 256])
    WU = din("up_proj", [NE, 4, 128, 16 * 256])
    WD = din("down_proj", [NE, 8, 128, 8 * 256])
    identb = din("identb", [128, 128], BF16)
    identf = din("identf", [128, 128], F32)
    maskb = din("maskb", [8, 128, 256], BF16)
    cosq = din("cosq", [128, TOK])
    sinq = din("sinq", [128, TOK])
    cosk = din("cosk", [128, S])
    sink = din("sink", [128, S])
    iotatok = din("iotatok", [128, 8], mybir.dt.int32)
    ebase = din("ebase", [8, 1])
    oobfill = din("oobfill", [NE, 512], mybir.dt.int32)
    h1_dram = nc.dram_tensor("h1_dram", [TOK, D], BF16, kind="Internal").ap()
    h2_dram = nc.dram_tensor("h2_dram", [TOK, D], F32, kind="Internal").ap()
    gidx_dram = nc.dram_tensor("gidx_dram", [NE, 512], mybir.dt.int32, kind="Internal").ap()
    cw_dram = nc.dram_tensor("cw_dram", [TOK, NE], F32, kind="Internal").ap()
    WGb = nc.dram_tensor("wgb_d", [NE, 4, 128, 16 * 256], BF16, kind="Internal").ap()
    WUb = nc.dram_tensor("wub_d", [NE, 4, 128, 16 * 256], BF16, kind="Internal").ap()
    WDb = nc.dram_tensor("wdb_d", [NE, 8, 128, 8 * 256], BF16, kind="Internal").ap()
    out = nc.dram_tensor("out", [TOK, D], F32, kind="ExternalOutput").ap()
    dbg = {}
    if debug:
        for nm, shp, dt in (
            ("d_qcomb", [128, H * TOK], BF16), ("d_kcomb", [128, 2 * S], BF16),
            ("d_avT", [128, H * TOK], BF16), ("d_h1", [128, 8 * D], BF16),
            ("d_h2", [128, 8 * D], F32), ("d_cw", [128, 64], F32),
            ("d_kvlat", [128, 4 * S], BF16), ("d_kpe", [128, 8 * S], BF16),
            ("d_v2", [128, 16 * 256], BF16), ("d_attn", [128, 2048], BF16),
            ("d_scores", [128, 2048], F32), ("d_stats", [128, 8], F32),
            ("d_maskb", [128, 2048], BF16),
            ("d_stat12", [128, 12], F32), ("d_hpre", [128, 2048], F32),
            ("d_attnT", [128, 2048], BF16), ("d_av", [128, 128], F32),
        ):
            dbg[nm] = nc.dram_tensor(nm, shp, dt, kind="ExternalOutput").ap()

    ops = {e: [] for e in ENGINES}
    semh, cnt = {}, {}
    waited = {e: {} for e in ENGINES}
    es = contextlib.ExitStack()
    live = {}

    def mksem(*names):
        for n in names:
            semh[n] = es.enter_context(nc.semaphore(n))
            cnt[n] = 0

    def wait(eng, sname, val=None):
        v = cnt[sname] if val is None else val
        if v <= 0 or waited[eng].get(sname, -1) >= v:
            return
        waited[eng][sname] = v
        reg = REGION[0]

        def run(h, s=semh[sname], vv=v, eng=eng, sname=sname, reg=reg):
            ins = h.wait_ge(s, vv)
            try:
                WAITMAP[ins.ins.name] = (eng, sname, vv, reg)
            except Exception:
                pass
        ops[eng].append(run)

    def waitd(eng, deps):
        best = {}
        for d in deps:
            if d is None:
                continue
            s, v = d
            if best.get(s, -1) < v:
                best[s] = v
        for s, v in best.items():
            wait(eng, s, v)

    def emit(eng, fn, inc=None, by=1):
        tgt = None
        if inc is not None:
            cnt[inc] += by
            tgt = cnt[inc]

        def run(h, fn=fn, inc=inc, by=by):
            r = fn(h)
            if inc is not None:
                r.then_inc(semh[inc], by)
        ops[eng].append(run)
        return (inc, tgt) if inc else None

    def dma(eng, out_ap, in_ap, s):
        return emit(eng, lambda h, o=out_ap, i=in_ap: h.dma_start(out=o, in_=i), inc=s, by=16)

    SB_LO, SB_HI = 16512, 229344
    free_list = [[SB_LO, SB_HI]]
    sizes = {}

    def sb(name, shape, dtype):
        nbytes = shape[1] * (2 if dtype == BF16 else 4)
        nbytes = (nbytes + 255) & ~255
        for blk in free_list:
            if blk[1] - blk[0] >= nbytes:
                off = blk[0]
                blk[0] += nbytes
                sizes[name] = (off, nbytes)
                return nc.alloc_sbuf_tensor_at(name, shape, dtype, offset=off)
        raise RuntimeError(f"SBUF OOM for {name} ({nbytes}B); free={free_list}")

    def psT(name, shape, dtype=F32):
        g = nc.psum_tensor(name, shape, dtype)
        h = g.__enter__()
        live[name] = g
        return h

    def free(*names):
        for n in names:
            if n in live:
                live.pop(n).__exit__(None, None, None)
                continue
            off, nbytes = sizes.pop(n)
            free_list.append([off, off + nbytes])
            free_list.sort()
            merged = [free_list[0]]
            for lo, hi in free_list[1:]:
                if lo == merged[-1][1]:
                    merged[-1][1] = hi
                else:
                    merged.append([lo, hi])
            free_list[:] = merged

    def vsync():
        tok = emit("vector", lambda h: h.engine_nop(), inc="tVE")
        wait("vector", "tVE", cnt["tVE"])
        return tok

    def ssync():
        tok = emit("scalar", lambda h: h.activation(ssync_sb[:], identb_sb[:, 0:1], AF.Copy), inc="tSC")
        wait("scalar", "tSC", cnt["tSC"])
        return tok

    _regc = {}

    def bcreg(h, v):
        if v not in _regc:
            _regc[v] = h.to_reg(v)
        return _regc[v]

    def fence(engines=ENGINES):
        for e in engines:
            for s in ("tPE", "tVE", "tSC", "tGP"):
                wait(e, s, cnt[s])

    mksem("tPE", "tVE", "tSC", "tGP",
          "dINIT", "dWkv", "dWkr0", "dWkr1", "dXSa", "dXSb",
          "dXSc", "dXSd", "dXSe", "dXSf", "dXSg", "dXSh",
          "dWqd", "dWqu", "dWqr", "dWku", "dWvu", "dWo", "dRW",
          "dXO0", "dXO1", "dWg0", "dWg1", "dWu0", "dWu1", "dWd0", "dWd1", "dOUT",
          "dIDB", "dH1S", "dCWS", "dGPF", "dGIX", "dGA0", "dGA1", "dGI0", "dGI1",
          "dCG0", "dCG1", "dSCT", "dH2Z", "dH2L0", "dH2L1", "dIOT", "dPC")

    EV = {"vector": "tVE", "scalar": "tSC", "gpsimd": "tGP"}

    def copy_on(eng, out_ap, in_ap):
        if eng == "scalar":
            return emit(eng, lambda h, o=out_ap, i=in_ap: h.copy(o, i), inc=EV[eng])
        return emit(eng, lambda h, o=out_ap, i=in_ap: h.tensor_copy(o, i), inc=EV[eng])

    def act_copy(out_ap, in_ap, scale=1.0):
        return emit("scalar", lambda h, o=out_ap, i=in_ap, sc=scale:
                    h.activation(o, i, AF.Copy, scale=sc), inc="tSC")

    def mm(psl, wl, rl, stl, spl):
        n = len(psl)
        tok = None
        for i in range(n):
            if i == n - 1:
                tok = emit("tensor", lambda h, p=psl[i], w=wl[i], r=rl[i], st=stl[i], sp=spl[i]:
                           h.matmul(p, w, r, start=st, stop=sp), inc="tPE")
            else:
                emit("tensor", lambda h, p=psl[i], w=wl[i], r=rl[i], st=stl[i], sp=spl[i]:
                     h.matmul(p, w, r, start=st, stop=sp))
        return tok

    ssync_sb = sb("ssync_sb", [128, 1], F32)
    identb_sb = sb("identb_sb", [128, 128], BF16)
    identf_sb = sb("identf_sb", [128, 128], F32)
    maskb_sb = sb("maskb_sb", [128, 8 * 256], BF16)
    cosq_sb = sb("cosq_sb", [128, TOK], F32)
    sinq_sb = sb("sinq_sb", [128, TOK], F32)
    cosk_sb = sb("cosk_sb", [128, S], F32)
    sink_sb = sb("sink_sb", [128, S], F32)
    dma("sync", identb_sb[:], identb[:], "dIDB")
    dma("sync", identf_sb[:], identf[:], "dINIT")
    dma("sync", maskb_sb.rearrange("p (j n) -> p j n", j=8), maskb.rearrange("j p n -> p j n"), "dINIT")
    dma("sync", cosq_sb[:], cosq[:], "dINIT")
    dma("sync", sinq_sb[:], sinq[:], "dINIT")
    dma("sync", cosk_sb[:], cosk[:], "dINIT")
    dma("sync", sink_sb[:], sink[:], "dINIT")
    iot_sb = sb("iot_sb", [128, 8], mybir.dt.int32)
    ebase_sb = sb("ebase_sb", [8, 1], F32)
    dma("sync", iot_sb[:], iotatok[:], "dIOT")
    dma("sync", ebase_sb[:], ebase[:], "dIOT")
    dma("sync", gidx_dram[:], oobfill[:], "dGPF")

    Wkv_sb = sb("Wkv_sb", [128, 16 * KVR], BF16)
    dma("gpsimd", Wkv_sb[:], Wkv[:], "dWkv")
    Wkr_sb = [sb("Wkr_sb0", [128, 16 * 512], BF16), sb("Wkr_sb1", [128, 16 * 512], BF16)]
    for hh in range(2):
        dma("gpsimd", Wkr_sb[hh][:], Wkr[hh], f"dWkr{hh}")

    if debug:
        wait("sync", "dINIT")
        dma("sync", dbg["d_maskb"][:], maskb_sb[:], "dOUT")
    xT_kv = sb("xT_kv", [128, 16 * S], BF16)
    xslab = [sb(f"xslab{i}", [128, D], BF16) for i in range(8)]
    XSEMS = ["dXSa", "dXSb", "dXSc", "dXSd", "dXSe", "dXSf", "dXSg", "dXSh"]
    ptr = [psT("ptr0", [128, 512], BF16), psT("ptr1", [128, 512], BF16)]
    ptr_dep = [[], []]
    slab_dep = [[] for _ in range(8)]
    wait("tensor", "dIDB")

    def transpose_in(n_slab, dst_sb, dst_span, src_dram, sem_ld, wdma=()):
        deps = []
        for gs in range(n_slab // 4):
            for q in range(4):
                s = gs * 4 + q
                buf = s % 8
                waitd("gpsimd", slab_dep[buf])
                dma("gpsimd", xslab[buf][:], src_dram[s * 128:(s + 1) * 128, :], XSEMS[buf])
                if s < len(wdma):
                    wsb, wsrc, wsem = wdma[s]
                    dma("gpsimd", wsb[:], wsrc[:], wsem)
            for q in range(4):
                sem = XSEMS[(gs * 4 + q) % 8]
                wait("tensor", sem, cnt[sem])
            for f in range(16):
                pidx = f % 2
                waitd("tensor", ptr_dep[pidx])
                tp = None
                for q in range(4):
                    buf = (gs * 4 + q) % 8
                    tp = emit("tensor", lambda h, o=ptr[pidx][:, q * 128:(q + 1) * 128],
                              i=xslab[buf][:, f * 128:(f + 1) * 128]:
                              h.transpose(o, i, identb_sb[:]), inc="tPE")
                    if f == 15:
                        slab_dep[buf] = [tp]
                eng = "vector" if f % 2 == 0 else "scalar"
                waitd(eng, [tp])
                ev = copy_on(eng, dst_sb[:, f * dst_span + gs * 512: f * dst_span + gs * 512 + 512],
                             ptr[pidx][:])
                ptr_dep[pidx] = [ev]
                deps.append(ev)
        return deps

    REGION[0] = "A1-xkv"
    xTkv_dep = transpose_in(16, xT_kv, S, x_kv, None)
    free(*[f"xslab{i}" for i in range(8)])

    REGION[0] = "A2-kvlat"
    kv_latT = sb("kv_latT", [128, 4 * S], BF16)
    psA = [psT("psA0", [128, 512], F32), psT("psA1", [128, 512], F32)]
    psA_dep = [[], []]
    kvlat_dep = []
    wait("tensor", "dWkv")
    waitd("tensor", xTkv_dep)
    ti = 0
    for oc in range(4):
        for tg in range(4):
            pidx = ti % 2
            ti += 1
            waitd("tensor", psA_dep[pidx])
            tok = mm([psA[pidx][:]] * 16,
                     [Wkv_sb[:, kc * KVR + oc * 128: kc * KVR + oc * 128 + 128] for kc in range(16)],
                     [xT_kv[:, kc * S + tg * 512: kc * S + tg * 512 + 512] for kc in range(16)],
                     [kc == 0 for kc in range(16)], [kc == 15 for kc in range(16)])
            eng = "vector" if ti % 2 == 0 else "scalar"
            waitd(eng, [tok])
            ev = copy_on(eng, kv_latT[:, oc * S + tg * 512: oc * S + tg * 512 + 512], psA[pidx][:])
            psA_dep[pidx] = [ev]
            kvlat_dep.append(ev)

    REGION[0] = "A3-kpe"
    kpe1 = sb("kpe1", [128, 4 * S], BF16)
    kpe2 = sb("kpe2", [128, 4 * S], BF16)
    tmpA = [sb("tmpA0", [128, 512], F32), sb("tmpA1", [128, 512], F32)]
    tmpB = [sb("tmpB0", [128, 512], F32), sb("tmpB1", [128, 512], F32)]
    tmp_rd = []
    kpe_dep = []
    wait("vector", "dINIT")
    wait("gpsimd", "dINIT")
    psA2 = [psT("psA2", [128, 512], F32), psT("psA3", [128, 512], F32)]
    psA2_dep = [[], []]

    def rope6(ps1, ps2, cos_ap, sin_ap, d1, d2, w):
        waitd("vector", tmp_rd)
        t1c = emit("vector", lambda h, o=tmpA[0][:, 0:w], i=ps1, c=cos_ap: h.tensor_mul(o, i, c), inc="tVE")
        t1s = emit("vector", lambda h, o=tmpA[1][:, 0:w], i=ps1, c=sin_ap: h.tensor_mul(o, i, c), inc="tVE")
        t2c = emit("vector", lambda h, o=tmpB[0][:, 0:w], i=ps2, c=cos_ap: h.tensor_mul(o, i, c), inc="tVE")
        t2s = emit("vector", lambda h, o=tmpB[1][:, 0:w], i=ps2, c=sin_ap: h.tensor_mul(o, i, c), inc="tVE")
        waitd("gpsimd", [t1c, t1s, t2c, t2s])
        o1 = emit("gpsimd", lambda h, o=d1: h.tensor_sub(o, tmpA[0][:, 0:w], tmpB[1][:, 0:w]), inc="tGP")
        o2 = emit("gpsimd", lambda h, o=d2: h.tensor_add(o, tmpA[1][:, 0:w], tmpB[0][:, 0:w]), inc="tGP")
        tmp_rd[:] = [o1, o2]
        return [o1, o2], [t1c, t1s], [t2c, t2s]

    for qd in range(4):
        wait("tensor", f"dWkr{qd // 2}")
        for tg in range(4):
            wb = Wkr_sb[qd // 2]
            woff = (qd % 2) * 256
            waitd("tensor", psA2_dep[0])
            tok1 = mm([psA2[0][:]] * 16,
                      [wb[:, kc * 512 + woff: kc * 512 + woff + 128] for kc in range(16)],
                      [xT_kv[:, kc * S + tg * 512: kc * S + tg * 512 + 512] for kc in range(16)],
                      [kc == 0 for kc in range(16)], [kc == 15 for kc in range(16)])
            waitd("tensor", psA2_dep[1])
            tok2 = mm([psA2[1][:]] * 16,
                      [wb[:, kc * 512 + woff + 128: kc * 512 + woff + 256] for kc in range(16)],
                      [xT_kv[:, kc * S + tg * 512: kc * S + tg * 512 + 512] for kc in range(16)],
                      [kc == 0 for kc in range(16)], [kc == 15 for kc in range(16)])
            waitd("vector", [tok1, tok2])
            col = qd * S + tg * 512
            deps, f1, f2 = rope6(psA2[0][:], psA2[1][:],
                                 cosk_sb[:, tg * 512: tg * 512 + 512],
                                 sink_sb[:, tg * 512: tg * 512 + 512],
                                 kpe1[:, col:col + 512], kpe2[:, col:col + 512], 512)
            psA2_dep[0] = f1
            psA2_dep[1] = f2
            kpe_dep += deps

    free("Wkv_sb", "cosk_sb", "sink_sb", "Wkr_sb0", "Wkr_sb1")

    if debug:
        waitd("sync", kpe_dep)
        dma("sync", dbg["d_kpe"][:, 0:4 * S], kpe1[:], "dOUT")
        dma("sync", dbg["d_kpe"][:, 4 * S:8 * S], kpe2[:], "dOUT")
        waitd("sync", kvlat_dep)
        dma("sync", dbg["d_kvlat"][:], kv_latT[:], "dOUT")

    free("xT_kv")
    fence(("gpsimd", "sync"))
    Wqd_sb = sb("Wqd_sb", [128, 16 * QR], BF16)
    Wqu_sb = sb("Wqu_sb", [128, 8 * H * ND], BF16)
    Wqr_sb = sb("Wqr_sb", [128, 8 * H * RD], BF16)

    xT_own = sb("xT_own", [128, 16 * TOK], BF16)
    wait("vector", "tPE", cnt["tPE"])
    wait("scalar", "tPE", cnt["tPE"])
    REGION[0] = "A5-xown"
    xslab[:] = [sb(f"xslabB{i}", [128, D], BF16) for i in range(8)]
    for i in range(8):
        slab_dep[i] = []
    xTown_dep = transpose_in(8, xT_own, TOK, x_own, None, wdma=[
        (Wqd_sb, Wqd, "dWqd"), (Wqu_sb, Wqu, "dWqu"), (Wqr_sb, Wqr, "dWqr")])

    free(*[f"xslabB{i}" for i in range(8)])
    REGION[0] = "A6-qlat"
    q_latT = sb("q_latT", [128, 8 * TOK], BF16)
    qlat_dep = []
    wait("tensor", "dWqd")
    waitd("tensor", xTown_dep)
    ti = 0
    for oc in range(8):
        for tg in range(2):
            pidx = ti % 2
            ti += 1
            waitd("tensor", psA_dep[pidx])
            tok = mm([psA[pidx][:]] * 16,
                     [Wqd_sb[:, kc * QR + oc * 128: kc * QR + oc * 128 + 128] for kc in range(16)],
                     [xT_own[:, kc * TOK + tg * 512: kc * TOK + tg * 512 + 512] for kc in range(16)],
                     [kc == 0 for kc in range(16)], [kc == 15 for kc in range(16)])
            eng = "vector" if ti % 2 == 0 else "scalar"
            waitd(eng, [tok])
            ev = copy_on(eng, q_latT[:, oc * TOK + tg * 512: oc * TOK + tg * 512 + 512], psA[pidx][:])
            psA_dep[pidx] = [ev]
            qlat_dep.append(ev)

    free("Wqd_sb", "xT_own")
    REGION[0] = "A7-qcomb"
    q_comb = sb("q_comb", [128, H * TOK], BF16)
    qcomb_dep = []
    wait("tensor", "dWqu")
    waitd("tensor", qlat_dep)
    ti = 0
    for hp in range(8):
        for tg in range(2):
            pidx = ti % 2
            ti += 1
            waitd("tensor", psA_dep[pidx])
            tok = mm([psA[pidx][:]] * 8,
                     [Wqu_sb[:, kc * H * ND + hp * 128: kc * H * ND + hp * 128 + 128] for kc in range(8)],
                     [q_latT[:, kc * TOK + tg * 512: kc * TOK + tg * 512 + 512] for kc in range(8)],
                     [kc == 0 for kc in range(8)], [kc == 7 for kc in range(8)])
            waitd("scalar", [tok])
            e1 = act_copy(q_comb[0:64, (2 * hp) * TOK + tg * 512: (2 * hp) * TOK + tg * 512 + 512],
                          psA[pidx][0:64, :], scale=SCALE)
            e2 = act_copy(q_comb[0:64, (2 * hp + 1) * TOK + tg * 512: (2 * hp + 1) * TOK + tg * 512 + 512],
                          psA[pidx][64:128, :], scale=SCALE)
            psA_dep[pidx] = [e2]
            qcomb_dep += [e1, e2]
    wait("tensor", "dWqr")
    qr1 = sb("qr1", [128, 512], BF16)
    qr2 = sb("qr2", [128, 512], BF16)
    qr_rd = []
    for qd in range(4):
        for tg in range(2):
            waitd("tensor", psA2_dep[0])
            tok1 = mm([psA2[0][:]] * 8,
                      [Wqr_sb[:, kc * H * RD + qd * 256: kc * H * RD + qd * 256 + 128] for kc in range(8)],
                      [q_latT[:, kc * TOK + tg * 512: kc * TOK + tg * 512 + 512] for kc in range(8)],
                      [kc == 0 for kc in range(8)], [kc == 7 for kc in range(8)])
            waitd("tensor", psA2_dep[1])
            tok2 = mm([psA2[1][:]] * 8,
                      [Wqr_sb[:, kc * H * RD + qd * 256 + 128: kc * H * RD + qd * 256 + 256] for kc in range(8)],
                      [q_latT[:, kc * TOK + tg * 512: kc * TOK + tg * 512 + 512] for kc in range(8)],
                      [kc == 0 for kc in range(8)], [kc == 7 for kc in range(8)])
            waitd("vector", [tok1, tok2])
            waitd("vector", qr_rd)
            waitd("gpsimd", qr_rd)
            deps, f1, f2 = rope6(psA2[0][:], psA2[1][:],
                                 cosq_sb[:, tg * 512: tg * 512 + 512],
                                 sinq_sb[:, tg * 512: tg * 512 + 512],
                                 qr1[:], qr2[:], 512)
            psA2_dep[0] = f1
            psA2_dep[1] = f2
            qr_rd = []
            for i_ in range(4):
                hh2 = 4 * qd + i_
                eng = "vector" if i_ % 2 == 0 else "scalar"
                waitd(eng, deps)
                c1 = copy_on(eng, q_comb[64:96, hh2 * TOK + tg * 512: hh2 * TOK + tg * 512 + 512],
                             qr1[i_ * 32:(i_ + 1) * 32, :])
                c2 = copy_on(eng, q_comb[96:128, hh2 * TOK + tg * 512: hh2 * TOK + tg * 512 + 512],
                             qr2[i_ * 32:(i_ + 1) * 32, :])
                qr_rd += [c1, c2]
                qcomb_dep += [c1, c2]
    free("qr1", "qr2")
    free("Wqu_sb", "Wqr_sb", "q_latT", "cosq_sb", "sinq_sb",
         "tmpA0", "tmpA1", "tmpB0", "tmpB1")

    if debug:
        waitd("sync", qcomb_dep)
        dma("sync", dbg["d_qcomb"][:], q_comb[:], "dOUT")

    fence(("gpsimd", "sync"))
    Wku_b = [sb("Wku_b0", [128, 512], BF16), sb("Wku_b1", [128, 512], BF16)]
    Wvu_b = [sb("Wvu_b0", [128, 1024], BF16), sb("Wvu_b1", [128, 1024], BF16)]
    wkv_rd = [[], []]
    dma("gpsimd", Wku_b[0][:], Wku[0], "dWku")
    dma("gpsimd", Wvu_b[0][:], Wvu[0], "dWvu")
    Wo_sbA = sb("Wo_sbA", [128, 8 * D], BF16)
    dma("gpsimd", Wo_sbA[:], Wo[0], "dWo")

    free("psA3", "psA2", "psA1", "psA0")
    psS = psT("psS0", [128, 2048], F32)
    psS_dep = []
    pav = [psT("pav0", [128, 128], F32), psT("pav1", [128, 128], F32)]
    pav_dep = [[], []]

    avT = sb("avT", [128, H * TOK], BF16)
    k_comb2 = [sb("k_comb_a", [128, 2 * S], BF16), sb("k_comb_b", [128, 2 * S], BF16)]
    v2b = [sb("v2_a", [128, 16 * 256], BF16), sb("v2_b", [128, 16 * 256], BF16)]
    attn2 = [sb("attn_sb0", [128, 2048], BF16), sb("attn_sb1", [128, 2048], BF16)]
    attnT2 = [sb("attnT_sb0", [128, 2048], BF16), sb("attnT_sb1", [128, 2048], BF16)]
    nmx = sb("nmx", [128, 2], F32)
    den = sb("den", [128, 2], F32)
    rden = sb("rden", [128, 2], F32)
    avT_dep = []
    nmx_r = [[], []]

    slot_ctr = [0]
    bank_pos = [0]
    bank_dep = [[], [], [], []]
    pend = []
    attn_free = [[], []]
    attnT_free = [[], []]

    def flush_slot(p):
        E2, bp2, s2, hh3, j3 = p["E"], p["bp"], p["s_"], p["hh"], p["j"]
        REGION[0] = f"ATT-flush-j{j3}"
        vb = v2b[p["pp"]]
        waitd("tensor", [p["nrm"]])
        cop_toks = []
        ngrp = (E2 + 3) // 4
        for g in range(ngrp):
            c0, c1 = g * 4, min(g * 4 + 4, E2)
            pidx2 = g % 2
            waitd("tensor", ptr_dep[pidx2])
            tp = None
            for c in range(c0, c1):
                tp = emit("tensor", lambda h, o=ptr[pidx2][:, (c - c0) * 128:(c - c0) * 128 + 128],
                          i=attn2[bp2][:, c * 128: c * 128 + 128]:
                          h.transpose(o, i, identb_sb[:]), inc="tPE")
            if g == ngrp - 1:
                attn_free[bp2] = [tp]
            eng = "vector" if g % 2 == 0 else "scalar"
            waitd(eng, [tp])
            waitd(eng, attnT_free[bp2])
            ev = copy_on(eng, attnT2[bp2][:, c0 * 128: c1 * 128], ptr[pidx2][:, 0:(c1 - c0) * 128])
            ptr_dep[pidx2] = [ev]
            cop_toks.append((ev, c0, c1))
        waitd("tensor", pav_dep[bp2])
        av_tok = None
        for ev, c0, c1 in cop_toks:
            waitd("tensor", [ev])
            for c in range(c0, c1):
                av_tok = mm([pav[bp2][:]],
                            [vb[:, c * 256 + s2 * 128: c * 256 + s2 * 128 + 128]],
                            [attnT2[bp2][:, c * 128: c * 128 + 128]],
                            [c == 0], [c == E2 - 1])
        attnT_free[bp2] = [av_tok]
        pair_av[p["pp"]] = pair_av[p["pp"]] + [av_tok]
        eng = "vector" if j3 % 2 == 0 else "scalar"
        waitd(eng, [av_tok])
        ev = copy_on(eng, avT[:, hh3 * TOK + j3 * 128: hh3 * TOK + j3 * 128 + 128], pav[bp2][:])
        pav_dep[bp2] = [ev]
        avT_dep.append(ev)

    waitd("tensor", kvlat_dep)
    ti = 0
    pair_av = [[], []]
    kc_ready_p = [[], []]
    REGION[0] = "ATT"
    pctasks = []
    for e in range(NE):
        for sl4 in range(4):
            pctasks.append((WGb[e, sl4], WG[e, sl4]))
            pctasks.append((WUb[e, sl4], WU[e, sl4]))
        for dg in range(8):
            pctasks.append((WDb[e, dg], WD[e, dg]))
    for hp in range(8):
        pp = hp % 2
        k_comb = k_comb2[pp]
        v2 = v2b[pp]
        asm_dep = [[], [], [], []]
        wait("tensor", "dWku")
        wait("tensor", "dWvu")
        for tg in range(4):
            pidx = ti % 4
            ti += 1
            waitd("tensor", bank_dep[pidx])
            waitd("tensor", asm_dep[pidx])
            sl = psS[:, pidx * 512:(pidx + 1) * 512]
            tok = mm([sl] * 4,
                     [Wku_b[pp][:, kc * 128:(kc + 1) * 128] for kc in range(4)],
                     [kv_latT[:, kc * S + tg * 512: kc * S + tg * 512 + 512] for kc in range(4)],
                     [kc == 0 for kc in range(4)], [kc == 3 for kc in range(4)])
            eng = "vector"
            waitd(eng, [tok])
            waitd(eng, pair_av[pp])
            e1 = copy_on(eng, k_comb[0:64, 0 * S + tg * 512: 0 * S + tg * 512 + 512], sl[0:64, :])
            e2 = copy_on(eng, k_comb[0:64, 1 * S + tg * 512: 1 * S + tg * 512 + 512], sl[64:128, :])
            asm_dep[pidx] = [e2]
        for s_ in range(2):
            hh2 = 2 * hp + s_
            qd2, i2 = hh2 // 4, hh2 % 4
            eng = "vector" if s_ == 0 else "scalar"
            waitd(eng, kpe_dep)
            waitd(eng, pair_av[pp])
            copy_on(eng, k_comb[64:96, s_ * S: s_ * S + S],
                    kpe1[i2 * 32:(i2 + 1) * 32, qd2 * S: qd2 * S + S])
            copy_on(eng, k_comb[96:128, s_ * S: s_ * S + S],
                    kpe2[i2 * 32:(i2 + 1) * 32, qd2 * S: qd2 * S + S])
        kc_ready = [("tVE", cnt["tVE"]), ("tSC", cnt["tSC"])]
        for tc in range(16):
            pidx = ti % 4
            ti += 1
            waitd("tensor", asm_dep[pidx])
            waitd("tensor", bank_dep[pidx])
            sl = psS[:, pidx * 512: pidx * 512 + 256]
            tok = mm([sl] * 4,
                     [kv_latT[:, kc * S + tc * 128: kc * S + tc * 128 + 128] for kc in range(4)],
                     [Wvu_b[pp][:, kc * 256:(kc + 1) * 256] for kc in range(4)],
                     [kc == 0 for kc in range(4)], [kc == 3 for kc in range(4)])
            eng = "vector" if tc % 4 != 3 else "scalar"
            waitd(eng, [tok])
            waitd(eng, pair_av[pp])
            ev = copy_on(eng, v2[:, tc * 256: tc * 256 + 256], sl)
            asm_dep[pidx] = [ev]
        for b in range(4):
            bank_dep[b] = bank_dep[b] + asm_dep[b]
        bank_pos[0] = 0
        v_ready = [("tVE", cnt["tVE"]), ("tSC", cnt["tSC"])]
        wkv_rd[pp] = [("tPE", cnt["tPE"])]
        if hp < 7:
            waitd("gpsimd", wkv_rd[(hp + 1) % 2])
            dma("gpsimd", Wku_b[(hp + 1) % 2][:], Wku[hp + 1], "dWku")
            dma("gpsimd", Wvu_b[(hp + 1) % 2][:], Wvu[hp + 1], "dWvu")

        if debug and hp == 7:
            waitd("sync", kc_ready)
            dma("sync", dbg["d_kcomb"][:], k_comb[:], "dOUT")
            waitd("sync", v_ready)
            dma("sync", dbg["d_v2"][:], v2[:], "dOUT")

        for j in range(8):
            for s_ in range(2):
                hh = 2 * hp + s_
                E = EXT[j]
                cols = E * 128
                nbank = (cols + 511) // 512
                bp = slot_ctr[0] % 2
                slot_ctr[0] += 1
                if bank_pos[0] + nbank > 4:
                    bank_pos[0] = 0
                b0 = bank_pos[0]
                bank_pos[0] += nbank
                waitd("tensor", sum((bank_dep[b] for b in range(b0, b0 + nbank)), []))
                base = b0 * 512
                qsl = q_comb[:, hh * TOK + j * 128: hh * TOK + j * 128 + 128]
                REGION[0] = f"ATT-score-j{j}"
                waitd("tensor", qcomb_dep)
                waitd("tensor", kc_ready)
                psl, wl, rl, stl, spl = [], [], [], [], []
                off = 0
                while off < cols:
                    n = min(512, cols - off)
                    psl.append(psS[:, base + off:base + off + n])
                    wl.append(qsl)
                    rl.append(k_comb[:, s_ * S + off: s_ * S + off + n])
                    stl.append(True)
                    spl.append(off + n < cols)
                    off += n
                psl.append(psS[:, base + cols - 256:base + cols])
                wl.append(identb_sb[:])
                rl.append(maskb_sb[:, j * 256: j * 256 + 256])
                stl.append(False)
                spl.append(True)
                score_tok = mm(psl, wl, rl, stl, spl)
                if len(pend) >= 2:
                    flush_slot(pend.pop(0))
                REGION[0] = f"ATT-smax-j{j}"
                waitd("vector", [score_tok])
                waitd("vector", nmx_r[bp])
                emit("vector", lambda h, o=nmx[:, bp:bp + 1], i=psS[:, base:base + cols]:
                     h.reduce_max(o, i, axis=AX.X, negate=True), inc="tVE")
                nm_done = [("tVE", cnt["tVE"])]
                waitd("scalar", nm_done)
                waitd("scalar", attn_free[bp])
                exp_tok = emit("scalar",
                    lambda h, o=attn2[bp][:, 0:cols], i=psS[:, base:base + cols], b2=nmx[:, bp:bp + 1],
                    a2=den[:, bp:bp + 1]:
                    h.activation(o, i, AF.Exp, bias=b2, scale=1.0, accum_out=a2), inc="tSC")
                for b in range(b0, b0 + nbank):
                    bank_dep[b] = [exp_tok]
                nmx_r[bp] = [exp_tok]
                waitd("vector", [exp_tok])
                emit("vector", lambda h, o=rden[:, bp:bp + 1], i=den[:, bp:bp + 1]:
                     h.reciprocal(o, i), inc="tVE")
                vsync()
                nrm = emit("vector", lambda h, w=cols, b2=bp: h.tensor_single_scalar(
                    attn2[b2][:, 0:w], attn2[b2][:, 0:w], rden[:, b2:b2 + 1], ALU.mult), inc="tVE")
                pend.append(dict(E=E, cols=cols, s_=s_, hh=hh, j=j, bp=bp, nrm=nrm, pp=pp))
        for o_, i_ in pctasks[hp * 16:(hp + 1) * 16]:
            dma("gpsimd", o_, i_, "dPC")
    while pend:
        flush_slot(pend.pop(0))

    if debug:
        waitd("sync", avT_dep)
        dma("sync", dbg["d_avT"][:], avT[:], "dOUT")


    free("k_comb_a", "k_comb_b", "v2_a", "v2_b", "attn_sb0", "attn_sb1", "attnT_sb0", "attnT_sb1", "q_comb", "kpe1", "kpe2", "kv_latT",
         "Wku_b0", "Wku_b1", "Wvu_b0", "Wvu_b1", "maskb_sb", "nmx", "den", "rden",
         "pav1", "pav0", "psS0")
    fence()
    Wo_sbB = sb("Wo_sbB", [128, 8 * D], BF16)
    dma("gpsimd", Wo_sbB[:], Wo[1], "dWo")
    REGION[0] = "LN1"
    po = psT("po", [128, 2048], F32)
    po_dep = []
    po_dep_a = []
    xo = [sb("xo0", [128, D], F32), sb("xo1", [128, D], F32)]
    xo_dep = [[], []]
    hpre = [sb("hpre0", [128, D], F32), sb("hpre1", [128, D], F32)]
    hpre_dep = [[], []]
    sqs = sb("sqs", [128, 512], F32)
    stat = sb("stat", [128, 12], F32)
    if debug:
        dstat12_sb = sb("dstat12_sb", [128, 12], F32)
        dhpre_sb = sb("dhpre_sb", [128, 2048], F32)
    h1b = sb("h1b", [128, 8 * D], BF16)
    h1T0 = sb("h1T0", [128, 8 * TOK], BF16)
    h1T1 = sb("h1T1", [128, 8 * TOK], BF16)

    def h1T_ap(fc, a, b):
        t = h1T0 if fc < 8 else h1T1
        return t[:, (fc % 8) * TOK + a: (fc % 8) * TOK + b]
    h1b_dep, h1T_dep = [], []
    stat_r, sq_r = [], []

    wait("tensor", "dWo")
    waitd("tensor", avT_dep)

    def ln_stats(bi, rs):
        waitd("scalar", [rs])
        waitd("scalar", sq_r)
        for qq in range(4):
            wait("scalar", "tSC", cnt["tSC"])
            emit("scalar", lambda h, i=hpre[bi][:, qq * 512:(qq + 1) * 512], ac=stat[:, 1 + qq:2 + qq]:
                 h.activation(sqs[:], i, AF.Square, accum_out=ac), inc="tSC")
        ssync()
        sq_done = [("tSC", cnt["tSC"])]
        waitd("vector", sq_done)
        emit("vector", lambda h: h.reduce_sum(stat[:, 5:6], stat[:, 1:5], axis=AX.X), inc="tVE")
        emit("vector", lambda h: h.tensor_scalar_mul(stat[:, 6:7], stat[:, 0:1], 1.0 / D), inc="tVE")
        vsync()
        emit("vector", lambda h: h.tensor_scalar(stat[:, 7:8], stat[:, 6:7], stat[:, 6:7][:, 0:1], EPS,
                                                 ALU.mult, ALU.subtract), inc="tVE")
        vsync()
        emit("vector", lambda h: h.scalar_tensor_tensor(stat[:, 8:9], stat[:, 5:6], 1.0 / D, stat[:, 7:8],
                                                        ALU.mult, ALU.subtract), inc="tVE")
        var_done = [("tVE", cnt["tVE"])]
        waitd("scalar", var_done)
        emit("scalar", lambda h: h.activation(stat[:, 9:10], stat[:, 8:9], AF.Ln), inc="tSC")
        ssync()
        return emit("scalar", lambda h: h.activation(stat[:, 10:11], stat[:, 9:10], AF.Exp, scale=-0.5), inc="tSC")

    for tc in range(8):
        bi = tc % 2
        waitd("sync", xo_dep[bi])
        dma("sync", xo[bi][:], x_own[tc * 128:(tc + 1) * 128, :], f"dXO{bi}")
        tok = tokh = None
        for dq in range(4):
            waitd("tensor", po_dep_a if dq < 2 else po_dep)
            tok = mm([po[:, dq * 512:(dq + 1) * 512]] * 16,
                     [avT[:, kc * TOK + tc * 128: kc * TOK + tc * 128 + 128] for kc in range(16)],
                     [(Wo_sbA if kc < 8 else Wo_sbB)[:, (kc % 8) * D + dq * 512: (kc % 8) * D + dq * 512 + 512] for kc in range(16)],
                     [kc == 0 for kc in range(16)], [kc == 15 for kc in range(16)])
            if dq == 1:
                tokh = tok
        waitd("vector", [tokh])
        wait("vector", f"dXO{bi}")
        waitd("vector", hpre_dep[bi])
        waitd("vector", stat_r)
        rs_a = emit("vector", lambda h, o=hpre[bi][:, 0:1024], p=po[:, 0:1024], x=xo[bi][:, 0:1024]:
                    h.scalar_tensor_tensor(o, p, 1.0, x, ALU.mult, ALU.add, accum_out=stat[:, 0:1]), inc="tVE")
        waitd("vector", [tok])
        rs_b = emit("vector", lambda h, o=hpre[bi][:, 1024:2048], p=po[:, 1024:2048], x=xo[bi][:, 1024:2048]:
                    h.scalar_tensor_tensor(o, p, 1.0, x, ALU.mult, ALU.add, accum_out=stat[:, 11:12]), inc="tVE")
        vsync()
        rs = emit("vector", lambda h: h.tensor_add(stat[:, 0:1], stat[:, 0:1], stat[:, 11:12]), inc="tVE")
        po_dep_a = [rs_a]
        po_dep = [rs_b]
        xo_dep[bi] = [rs_b]
        rstd_tok = ln_stats(bi, rs)
        sq_r = [rstd_tok]
        waitd("vector", [rstd_tok])
        vsync()
        nh = emit("vector", lambda h, o=h1b[:, tc * D: tc * D + D], i=hpre[bi][:]:
                  h.tensor_scalar(o, i, stat[:, 6:7], stat[:, 10:11], ALU.subtract, ALU.mult), inc="tVE")
        hpre_dep[bi] = [nh]
        stat_r = [nh]
        h1b_dep.append(nh)
        waitd("sync", [nh])
        dma("sync", h1_dram[tc * 128:(tc + 1) * 128, :], h1b[:, tc * D: tc * D + D], "dH1S")
        if debug and tc == 7:
            c1 = copy_on("vector", dstat12_sb[:], stat[:])
            c2 = copy_on("vector", dhpre_sb[:], hpre[0][:])
            waitd("sync", [c1, c2])
            dma("sync", dbg["d_stat12"][:], dstat12_sb[:], "dOUT")
            dma("sync", dbg["d_hpre"][:], dhpre_sb[:], "dOUT")
        for f in range(16):
            pidx = f % 2
            waitd("tensor", ptr_dep[pidx])
            waitd("tensor", [nh])
            tp = emit("tensor", lambda h, o=ptr[pidx][:, 0:128], i=h1b[:, tc * D + f * 128: tc * D + f * 128 + 128]:
                      h.transpose(o, i, identb_sb[:]), inc="tPE")
            eng = "vector" if f % 2 == 0 else "scalar"
            waitd(eng, [tp])
            ev = copy_on(eng, h1T_ap(f, tc * 128, tc * 128 + 128), ptr[pidx][:, 0:128])
            ptr_dep[pidx] = [ev]
            h1T_dep.append(ev)

    if debug:
        free("dstat12_sb", "dhpre_sb")
    free("avT", "Wo_sbA", "Wo_sbB", "xo0", "xo1", "po")

    fence(("gpsimd", "sync"))
    REGION[0] = "ROUTER"
    rw_sb = sb("rw_sb", [128, 16 * NE], BF16)
    dma("gpsimd", rw_sb[:], RW[:], "dRW")
    ps_r = psT("ps_r", [8, 512], F32)
    ptrf = psT("ptrf", [128, 128], F32)
    sc_sb = sb("sc_sb", [8, TOK], F32)
    sc_tok = sb("sc_tok", [128, 64], F32)
    p_sb = sb("p_sb", [128, 64], F32)
    t8a = sb("t8a", [128, 64], F32)
    t8b = sb("t8b", [128, 64], F32)
    t8c = sb("t8c", [128, 64], F32)
    rst8 = sb("rst8", [128, 48], F32)
    cw_sb = sb("cw_sb", [128, 8 * NE], F32)
    ind_sb = sb("ind_sb", [128, 8 * NE], F32)
    msk1_sb = sb("msk1_sb", [128, 8 * NE], F32)
    msk2_sb = sb("msk2_sb", [128, 8 * NE], F32)
    oi12 = sb("oi12", [128, 16], mybir.dt.int32)
    oi12f = sb("oi12f", [128, 16], F32)
    tmp16 = sb("tmp16", [128, 16], F32)
    offs_tok = sb("offs_tok", [128, 8 * NE], F32)
    indT = sb("indT", [8, TOK], F32)
    posA = sb("posA", [8, TOK], F32)
    posB = sb("posB", [8, TOK], F32)
    cw_dep = []
    wait("tensor", "dRW")
    waitd("tensor", h1T_dep)
    scsb_dep = []
    prev_ev = None
    for g in range(2):
        if prev_ev is not None:
            waitd("tensor", [prev_ev])
        tok = mm([ps_r[:]] * 16,
                 [rw_sb[:, kc * NE: kc * NE + NE] for kc in range(16)],
                 [h1T_ap(kc, g * 512, g * 512 + 512) for kc in range(16)],
                 [kc == 0 for kc in range(16)], [kc == 15 for kc in range(16)])
        waitd("scalar", [tok])
        prev_ev = act_copy(sc_sb[:, g * 512:(g + 1) * 512], ps_r[:])
        scsb_dep.append(prev_ev)
    ptrf_rd = None
    sc_tokw = sc_tok
    trtoks = []
    for tc in range(8):
        waitd("tensor", scsb_dep)
        if ptrf_rd is not None:
            waitd("tensor", [ptrf_rd])
        tp = emit("tensor", lambda h, o=ptrf[:, 0:8], i=sc_sb[:, tc * 128: tc * 128 + 128]:
                  h.transpose(o, i, identf_sb[0:8, 0:8]), inc="tPE")
        waitd("vector", [tp])
        ptrf_rd = emit("vector", lambda h, o=sc_tokw[:, tc * 8:(tc + 1) * 8]:
                       h.tensor_copy(o, ptrf[:, 0:8]), inc="tVE")
        trtoks.append(ptrf_rd)

    def step(fn_of_tc, eng="vector"):
        vsync()
        for tc in range(8):
            emit(eng, fn_of_tc(tc), inc=EV[eng])

    def SL(t, tc):
        return t[:, tc * 8:(tc + 1) * 8]

    def RC(tc, i):
        return rst8[:, tc * 6 + i: tc * 6 + i + 1]

    waitd("vector", trtoks)
    step(lambda tc: (lambda h: h.reduce_max(RC(tc, 0), SL(sc_tokw, tc), axis=AX.X, negate=True)))
    sm_tok = [("tVE", cnt["tVE"])]
    waitd("scalar", sm_tok)
    e1 = None
    for tc in range(8):
        e1 = emit("scalar", lambda h, o=SL(p_sb, tc), i=SL(sc_tokw, tc), b=RC(tc, 0):
                  h.activation(o, i, AF.Exp, bias=b), inc="tSC")
    waitd("vector", [e1])
    step(lambda tc: (lambda h: h.reduce_max(RC(tc, 1), SL(p_sb, tc), axis=AX.X)))
    step(lambda tc: (lambda h: h.tensor_single_scalar(SL(t8a, tc), SL(p_sb, tc), RC(tc, 1), ALU.is_ge)))
    step(lambda tc: (lambda h: h.tensor_copy(SL(t8b, tc), SL(t8a, tc))))
    step(lambda tc: (lambda h: h.tensor_add(t8b[:, tc * 8 + 1:tc * 8 + 8], t8a[:, tc * 8 + 1:tc * 8 + 8], t8a[:, tc * 8:tc * 8 + 7])))
    step(lambda tc: (lambda h: h.tensor_copy(SL(t8c, tc), SL(t8b, tc))))
    step(lambda tc: (lambda h: h.tensor_add(t8c[:, tc * 8 + 2:tc * 8 + 8], t8b[:, tc * 8 + 2:tc * 8 + 8], t8b[:, tc * 8:tc * 8 + 6])))
    step(lambda tc: (lambda h: h.tensor_copy(SL(t8b, tc), SL(t8c, tc))))
    step(lambda tc: (lambda h: h.tensor_add(t8b[:, tc * 8 + 4:tc * 8 + 8], t8c[:, tc * 8 + 4:tc * 8 + 8], t8c[:, tc * 8:tc * 8 + 4])))
    step(lambda tc: (lambda h: h.tensor_single_scalar(SL(t8c, tc), SL(t8b, tc), 1.5, ALU.is_le)))
    step(lambda tc: (lambda h: h.tensor_mul(SL(t8c, tc), SL(t8c, tc), SL(t8a, tc))))
    step(lambda tc: (lambda h: h.tensor_scalar(SL(t8a, tc), SL(t8c, tc), -1.0, 1.0, ALU.mult, ALU.add)))
    step(lambda tc: (lambda h: h.tensor_mul(SL(t8a, tc), SL(t8a, tc), SL(p_sb, tc))))
    step(lambda tc: (lambda h: h.reduce_max(RC(tc, 2), SL(t8a, tc), axis=AX.X)))
    step(lambda tc: (lambda h: h.tensor_single_scalar(SL(t8b, tc), SL(t8a, tc), RC(tc, 2), ALU.is_ge)))
    step(lambda tc: (lambda h: h.tensor_copy(SL(sc_tokw, tc), SL(t8b, tc))))
    step(lambda tc: (lambda h: h.tensor_add(sc_tokw[:, tc * 8 + 1:tc * 8 + 8], t8b[:, tc * 8 + 1:tc * 8 + 8], t8b[:, tc * 8:tc * 8 + 7])))
    step(lambda tc: (lambda h: h.tensor_copy(SL(t8a, tc), SL(sc_tokw, tc))))
    step(lambda tc: (lambda h: h.tensor_add(t8a[:, tc * 8 + 2:tc * 8 + 8], sc_tokw[:, tc * 8 + 2:tc * 8 + 8], sc_tokw[:, tc * 8:tc * 8 + 6])))
    step(lambda tc: (lambda h: h.tensor_copy(SL(sc_tokw, tc), SL(t8a, tc))))
    step(lambda tc: (lambda h: h.tensor_add(sc_tokw[:, tc * 8 + 4:tc * 8 + 8], t8a[:, tc * 8 + 4:tc * 8 + 8], t8a[:, tc * 8:tc * 8 + 4])))
    step(lambda tc: (lambda h: h.tensor_single_scalar(SL(t8a, tc), SL(sc_tokw, tc), 1.5, ALU.is_le)))
    step(lambda tc: (lambda h: h.tensor_mul(SL(t8b, tc), SL(t8a, tc), SL(t8b, tc))))
    step(lambda tc: (lambda h: h.tensor_add(SL(ind_sb, tc), SL(t8c, tc), SL(t8b, tc))))
    step(lambda tc: (lambda h: h.tensor_copy(SL(msk1_sb, tc), SL(t8c, tc))))
    step(lambda tc: (lambda h: h.tensor_copy(SL(msk2_sb, tc), SL(t8b, tc))))
    step(lambda tc: (lambda h: h.tensor_add(RC(tc, 3), RC(tc, 1), RC(tc, 2))))
    step(lambda tc: (lambda h: h.reciprocal(RC(tc, 4), RC(tc, 3))))
    step(lambda tc: (lambda h: h.tensor_mul(RC(tc, 1), RC(tc, 1), RC(tc, 4))))
    step(lambda tc: (lambda h: h.tensor_mul(RC(tc, 2), RC(tc, 2), RC(tc, 4))))
    step(lambda tc: (lambda h: h.tensor_single_scalar(SL(t8c, tc), SL(t8c, tc), RC(tc, 1), ALU.mult)))
    step(lambda tc: (lambda h: h.tensor_single_scalar(SL(t8b, tc), SL(t8b, tc), RC(tc, 2), ALU.mult)))
    vsync()
    cw_dep = []
    for tc in range(8):
        cwt = emit("vector", lambda h, o=SL(cw_sb, tc), a=SL(t8c, tc), b=SL(t8b, tc):
                   h.tensor_add(o, a, b), inc="tVE")
        cw_dep.append(cwt)
        waitd("sync", [cwt])
        dma("sync", cw_dram[tc * 128:(tc + 1) * 128, :], cw_sb[:, tc * NE: tc * NE + NE], "dCWS")

    BIGV = 1.0e6
    trT = []
    for tc in range(8):
        waitd("tensor", cw_dep)
        if trT:
            waitd("tensor", trT[-1:])
        tp = emit("tensor", lambda h, i=ind_sb[:, tc * NE: tc * NE + NE]:
                  h.transpose(ptrf[0:8, 0:128], i, identf_sb[:]), inc="tPE")
        waitd("vector", [tp])
        trT.append(emit("vector", lambda h, o=indT[:, tc * 128:(tc + 1) * 128]:
                        h.tensor_copy(o, ptrf[0:8, 0:128]), inc="tVE"))
    cur, oth = posA, posB
    waitd("vector", trT)
    emit("vector", lambda h: h.tensor_copy(cur[:], indT[:]), inc="tVE")
    sh = 1
    while sh < TOK:
        vsync()
        emit("vector", lambda h, a=cur, b=oth, s=sh: h.tensor_copy(b[:, 0:s], a[:, 0:s]), inc="tVE")
        emit("vector", lambda h, a=cur, b=oth, s=sh: h.tensor_add(b[:, s:TOK], a[:, s:TOK], a[:, 0:TOK - s]), inc="tVE")
        cur, oth = oth, cur
        sh *= 2
    vsync()
    emit("vector", lambda h, a=cur, b=oth: h.tensor_mul(b[:], a[:], indT[:]), inc="tVE")
    vsync()
    emit("vector", lambda h, b=oth: h.tensor_scalar(indT[:], indT[:], -BIGV, BIGV - 1.0, ALU.mult, ALU.add), inc="tVE")
    vsync()
    emit("vector", lambda h, b=oth: h.tensor_add(b[:], b[:], indT[:]), inc="tVE")
    vsync()
    wait("vector", "dIOT")
    emit("vector", lambda h, b=oth: h.tensor_single_scalar(b[:], b[:], ebase_sb[:, 0:1], ALU.add), inc="tVE")
    offs_done = [("tVE", cnt["tVE"])]
    trT2 = []
    for tc in range(8):
        waitd("tensor", offs_done)
        if trT2:
            waitd("tensor", trT2[-1:])
        tp = emit("tensor", lambda h, i=oth[0:8, tc * 128:(tc + 1) * 128]: h.transpose(
            ptrf[:, 8:16], i, identf_sb[0:8, 0:8]), inc="tPE")
        waitd("vector", [tp])
        trT2.append(emit("vector", lambda h, o=offs_tok[:, tc * NE: tc * NE + NE]:
                         h.tensor_copy(o, ptrf[:, 8:16]), inc="tVE"))
    waitd("vector", trT2)
    for tc in range(8):
        for pk in range(2):
            mk = msk1_sb if pk == 0 else msk2_sb
            vsync()
            emit("vector", lambda h, a=offs_tok[:, tc * NE: tc * NE + NE],
                 b=mk[:, tc * NE: tc * NE + NE]: h.tensor_mul(tmp16[:, 0:NE], a, b), inc="tVE")
            vsync()
            emit("vector", lambda h, o=oi12f[:, tc * 2 + pk: tc * 2 + pk + 1]:
                 h.reduce_sum(o, tmp16[:, 0:NE], axis=AX.X), inc="tVE")
    vsync()
    ocast = emit("vector", lambda h: h.tensor_copy(oi12[:], oi12f[:]), inc="tVE")
    wait("gpsimd", "dGPF")
    wait("gpsimd", "dIOT")
    waitd("gpsimd", [ocast])
    for tc in range(8):
        for pk in range(2):
            emit("gpsimd", lambda h, tc=tc, pk=pk: h.indirect_dma_start(
                gidx_dram.rearrange("a b -> (a b)").rearrange("(a b) -> a b", b=1),
                bass.IndirectOffsetOnAxis(oi12[:, tc * 2 + pk: tc * 2 + pk + 1], 0),
                iot_sb[:, tc:tc + 1], None,
                bounds_check=bcreg(h, NE * 512 - 1), oob_is_err=False), inc="dGIX", by=16)
    free("sc_sb", "rw_sb", "sc_tok", "p_sb", "t8a", "t8b", "t8c", "rst8",
         "ind_sb", "indT", "posA", "posB", "msk1_sb", "msk2_sb", "offs_tok",
         "oi12f", "tmp16", "identf_sb", "ptrf", "ps_r")

    if debug:
        waitd("sync", cw_dep)
        dma("sync", dbg["d_cw"][:], cw_sb[:], "dOUT")
        waitd("sync", h1b_dep)
        dma("sync", dbg["d_h1"][:], h1b[:], "dOUT")

    fence(("gpsimd", "sync"))
    free("h1T0", "h1T1")
    C = 384
    REGION[0] = "FFN"
    psG = psT("psG", [128, C], F32)
    psU = psT("psU", [128, C], F32)
    psD = [psT("psD0", [128, 256], F32), psT("psD1", [128, 256], F32)]
    ptr2 = [ptr[0], ptr[1]]
    ptr2_dep = [ptr_dep[0], ptr_dep[1]]
    psD_dep = [[], []]
    sct = [sb("sct0", [128, 3 * D], F32), sb("sct1", [128, 3 * D], F32)]
    gathT = [sb("gathT0", [128, 16 * C], BF16), sb("gathT1", [128, 16 * C], BF16)]
    graw = [sb(f"graw{i}", [128, D], BF16) for i in range(4)]
    Wg_buf = [sb("Wg_buf0", [128, 16 * 256], BF16), sb("Wg_buf1", [128, 16 * 256], BF16)]
    Wu_buf = [sb("Wu_buf0", [128, 16 * 256], BF16), sb("Wu_buf1", [128, 16 * 256], BF16)]
    Wd_buf = [sb(f"Wd_buf{i}", [128, 8 * 256], BF16) for i in range(4)]
    sg_bf = sb("sg_bf", [128, C], BF16)
    actT = [sb("actT0", [128, 8 * C], BF16), sb("actT1", [128, 8 * C], BF16)]
    cwg = [sb("cwg0", [128, 24], F32), sb("cwg1", [128, 24], F32)]
    gidxs = [sb("gidxs0", [128, 3], mybir.dt.int32), sb("gidxs1", [128, 3], mybir.dt.int32)]
    emit("vector", lambda h: h.memzero(sct[0][:, 0:D]), inc="tVE")
    for gi in range(4):
        emit("vector", lambda h, gi=gi: h.memzero(graw[gi][:]), inc="tVE")
    zv = ("tVE", cnt["tVE"])
    waitd("sync", [zv])
    for tc in range(8):
        dma("sync", h2_dram[tc * 128:(tc + 1) * 128, :], sct[0][:, 0:D], "dH2Z")

    wg_rd = [[], []]
    wu_rd = [[], []]
    wd_rd = [[], [], [], []]
    psG_dep, psU_dep = [], []
    sg_dep = []
    gidx_rd = [[], []]
    cwg_rd = [[], []]
    graw_rd = [[] for _ in range(4)]
    gathT_rd = [[], []]
    actT_rd = [[], []]
    sct_rd = [[("dH2Z", 128)], []]
    nsct = [0]

    def emit_gather_dma(e):
        be = e % 2
        wait("gpsimd", "dGIX")
        wait("gpsimd", "dH1S")
        wait("gpsimd", "dCWS")
        waitd("gpsimd", gidx_rd[be])
        for k in range(3):
            dma("gpsimd", gidxs[be][:, k:k + 1],
                gidx_dram[e][k * 128:(k + 1) * 128].rearrange("(a b) -> a b", b=1), f"dGI{be}")
        wait("gpsimd", f"dGI{be}")
        waitd("gpsimd", cwg_rd[be])
        for k in range(3):
            gslot = (e * 3 + k) % 4
            waitd("gpsimd", graw_rd[gslot])
            emit("gpsimd", lambda h, gslot=gslot, be=be, k=k: h.indirect_dma_start(
                graw[gslot][:], None, h1_dram[:],
                bass.IndirectOffsetOnAxis(gidxs[be][:, k:k + 1], 0),
                bounds_check=bcreg(h, TOK - 1), oob_is_err=False), inc=f"dGA{be}", by=16)
            emit("gpsimd", lambda h, be=be, k=k: h.indirect_dma_start(
                cwg[be][:, k * 8:(k + 1) * 8], None, cw_dram[:],
                bass.IndirectOffsetOnAxis(gidxs[be][:, k:k + 1], 0),
                bounds_check=bcreg(h, TOK - 1), oob_is_err=False), inc=f"dCG{be}", by=16)
        gidx_rd[be] = [(f"dGA{be}", cnt[f"dGA{be}"])]

    def emit_gather_tp(e):
        REGION[0] = "FFN-gathtp"
        be = e % 2
        wait("tensor", f"dGA{be}", cnt[f"dGA{be}"])
        tptoks = []
        lasttp = {}
        for kc in range(16):
            pidx = kc % 2
            waitd("tensor", ptr2_dep[pidx])
            tp = None
            for k in range(3):
                gslot = (e * 3 + k) % 4
                tp = emit("tensor", lambda h, o=ptr2[pidx][:, k * 128:(k + 1) * 128], gslot=gslot, kc=kc:
                          h.transpose(o, graw[gslot][:, kc * 128:(kc + 1) * 128], identb_sb[:]), inc="tPE")
                if kc == 15:
                    lasttp[gslot] = tp
            eng = "vector" if kc % 2 == 0 else "scalar"
            waitd(eng, [tp])
            waitd(eng, gathT_rd[be])
            ev = copy_on(eng, gathT[be][:, kc * C: kc * C + C], ptr2[pidx][:, 0:C])
            ptr2_dep[pidx] = [ev]
            tptoks.append(ev)
        for k in range(3):
            gslot = (e * 3 + k) % 4
            graw_rd[gslot] = [lasttp[gslot]]
        return tptoks[-2:] + [("tVE", cnt["tVE"]), ("tSC", cnt["tSC"])]

    def flush_scatter(e2, sc_toks2):
        be2 = e2 % 2
        wait("gpsimd", "dH2Z")
        wait("gpsimd", "dSCT", nsct[0] * 16)
        waitd("gpsimd", sc_toks2)
        for tk in range(3):
            emit("gpsimd", lambda h, be=be2, tk=tk: h.indirect_dma_start(
                h2_dram[:], bass.IndirectOffsetOnAxis(gidxs[be][:, tk:tk + 1], 0),
                sct[be][:, tk * D:(tk + 1) * D], None,
                bounds_check=bcreg(h, TOK - 1), oob_is_err=False,
                compute_op=ALU.add), inc="dSCT", by=16)
            nsct[0] += 1
        sct_rd[be2] = [("dSCT", nsct[0] * 16)]

    pend_sct = [None]
    gath_tp_next = [None]
    emit_gather_dma(0)
    for e in range(NE):
        be = e % 2
        gath_done = emit_gather_tp(e)
        REGION[0] = "FFN-gateup"
        if e == 0:
            wait("sync", "dPC", cnt["dPC"])
        for sl in range(4):
            gsem, usem = f"dWg{sl % 2}", f"dWu{sl % 2}"
            waitd("sync", wg_rd[sl % 2])
            dma("sync", Wg_buf[sl % 2][:], WGb[e, sl], gsem)
            waitd("sync", wu_rd[sl % 2])
            dma("sync", Wu_buf[sl % 2][:], WUb[e, sl], usem)
            for fc2 in range(2):
                fc = sl * 2 + fc2
                wait("tensor", gsem)
                wait("tensor", usem)
                waitd("tensor", gath_done)
                waitd("tensor", psG_dep)
                gmm = mm([psG[:]] * 16,
                         [Wg_buf[sl % 2][:, kc * 256 + fc2 * 128: kc * 256 + fc2 * 128 + 128] for kc in range(16)],
                         [gathT[be][:, kc * C: kc * C + C] for kc in range(16)],
                         [kc == 0 for kc in range(16)], [kc == 15 for kc in range(16)])
                waitd("tensor", psU_dep)
                umm = mm([psU[:]] * 16,
                         [Wu_buf[sl % 2][:, kc * 256 + fc2 * 128: kc * 256 + fc2 * 128 + 128] for kc in range(16)],
                         [gathT[be][:, kc * C: kc * C + C] for kc in range(16)],
                         [kc == 0 for kc in range(16)], [kc == 15 for kc in range(16)])
                wg_rd[sl % 2] = [gmm]
                wu_rd[sl % 2] = [umm]
                waitd("scalar", [gmm])
                waitd("scalar", sg_dep)
                sg = emit("scalar", lambda h: h.activation(sg_bf[:], psG[:], AF.Silu), inc="tSC")
                psG_dep = [sg]
                waitd("vector", [sg, umm])
                waitd("vector", actT_rd[be])
                aw = emit("vector", lambda h, o=actT[be][:, fc * C:(fc + 1) * C]:
                          h.tensor_mul(o, sg_bf[:], psU[:]), inc="tVE")
                psU_dep = [aw]
                sg_dep = [aw]
        act_done = [("tVE", cnt["tVE"])]
        gathT_rd[be] = [("tPE", cnt["tPE"])]
        if e < NE - 1:
            emit_gather_dma(e + 1)
        REGION[0] = "FFN-down"
        sc_toks = []
        dmm = None
        wd_tok = {}
        for dg in range(2):
            waitd("sync", wd_rd[dg % 4])
            wd_tok[dg] = dma("sync", Wd_buf[dg % 4][:], WDb[e, dg], f"dWd{dg % 2}")
        for dg in range(8):
            if dg + 2 < 8:
                waitd("sync", wd_rd[(dg + 2) % 4])
                wd_tok[dg + 2] = dma("sync", Wd_buf[(dg + 2) % 4][:], WDb[e, dg + 2], f"dWd{(dg + 2) % 2}")
            for tk in range(3):
                di = (dg * 3 + tk) % 2
                waitd("tensor", [wd_tok[dg]])
                waitd("tensor", act_done)
                waitd("tensor", psD_dep[di])
                dmm = mm([psD[di][:]] * 8,
                         [actT[be][:, kc * C + tk * 128: kc * C + tk * 128 + 128] for kc in range(8)],
                         [Wd_buf[dg % 4][:, kc * 256:(kc + 1) * 256] for kc in range(8)],
                         [kc == 0 for kc in range(8)], [kc == 7 for kc in range(8)])
                wd_rd[dg % 4] = [dmm]
                eng = "vector" if (dg + tk) % 2 == 0 else "scalar"
                waitd(eng, [dmm])
                wait(eng, f"dCG{be}", cnt[f"dCG{be}"])
                waitd(eng, sct_rd[be])
                o = sct[be][:, tk * D + dg * 256: tk * D + dg * 256 + 256]
                cwc = cwg[be][:, k * 8 + e: k * 8 + e + 1] if False else cwg[be][:, tk * 8 + e: tk * 8 + e + 1]
                if eng == "vector":
                    ev = emit(eng, lambda h, o=o, c=cwc, di=di:
                              h.tensor_single_scalar(o, psD[di][:], c, ALU.mult), inc="tVE")
                else:
                    ev = emit(eng, lambda h, o=o, c=cwc, di=di:
                              h.activation(o, psD[di][:], AF.Copy, scale=c), inc="tSC")
                psD_dep[di] = [ev]
                sc_toks.append(ev)
        actT_rd[be] = [("tPE", cnt["tPE"])]
        cwg_rd[be] = sc_toks[-2:]
        flush_scatter(e, sc_toks)

    free("Wg_buf0", "Wg_buf1", "Wu_buf0", "Wu_buf1", "Wd_buf0", "Wd_buf1",
         "sg_bf", "gathT0", "gathT1", "actT0", "actT1",
         "graw0", "graw1", "graw2", "graw3",
         "sct0", "sct1", "cwg0", "cwg1", "gidxs0", "gidxs1")
    REGION[0] = "LN2"
    ob = [sb("ob0", [128, D], F32), sb("ob1", [128, D], F32)]
    h2t = [sb("h2t0", [128, D], F32), sb("h2t1", [128, D], F32)]
    h2t_rd = [[], []]
    ob_dep = [[], []]
    stat_r, sq_r = [], []
    wait("sync", "dSCT", nsct[0] * 16)
    for tc in range(8):
        bi = tc % 2
        waitd("sync", h2t_rd[bi])
        dma("sync", h2t[bi][:], h2_dram[tc * 128:(tc + 1) * 128, :], f"dH2L{bi}")
        wait("vector", f"dH2L{bi}", cnt[f"dH2L{bi}"])
        waitd("vector", h1b_dep)
        waitd("vector", hpre_dep[bi])
        waitd("vector", stat_r)
        rs = emit("vector", lambda h, o=hpre[bi][:], a=h1b[:, tc * D: tc * D + D],
                  b=h2t[bi][:]:
                  h.scalar_tensor_tensor(o, a, 1.0, b, ALU.mult, ALU.add, accum_out=stat[:, 0:1]), inc="tVE")
        h2t_rd[bi] = [rs]
        rstd_tok = ln_stats(bi, rs)
        sq_r = [rstd_tok]
        waitd("vector", [rstd_tok])
        waitd("vector", ob_dep[bi])
        vsync()
        nh = emit("vector", lambda h, o=ob[bi][:], i=hpre[bi][:]:
                  h.tensor_scalar(o, i, stat[:, 6:7], stat[:, 10:11], ALU.subtract, ALU.mult), inc="tVE")
        hpre_dep[bi] = [nh]
        stat_r = [nh]
        waitd("sync", [nh])
        st_tok = dma("sync", out[tc * 128:(tc + 1) * 128, :], ob[bi][:], "dOUT")
        ob_dep[bi] = [st_tok]

    wait("sync", "dOUT")

    with es:
        block = es.enter_context(nc.Block())
        for eng in ENGINES:
            getattr(block, eng)(lambda h, lst=ops[eng]: [f(h) for f in lst])
    return nc


def _sw(W, k):
    K2, n = W.shape
    assert K2 == k * 128
    return np.ascontiguousarray(
        W.reshape(k, 128, n).transpose(1, 0, 2).reshape(128, k * n))


def _host_weights(inputs):
    H_ = H
    perm = []
    for qd in range(H_ // 4):
        for b2 in (0, 1):
            for i_ in range(4):
                perm.extend((4 * qd + i_) * 64 + b2 * 32 + np.arange(32))
    perm = np.asarray(perm)
    Wqr = np.asarray(inputs["L0_attn_Wq_rope"])[:, perm]
    Wkr = np.asarray(inputs["L0_attn_Wk_rope"])[:, perm]
    Wo_ = np.asarray(inputs["L0_attn_Wo"])
    G = np.asarray(inputs["L0_ffn_gate_proj"])
    U = np.asarray(inputs["L0_ffn_up_proj"])
    Dn = np.asarray(inputs["L0_ffn_down_proj"])
    d = {
        "Wq_down": _sw(np.asarray(inputs["L0_attn_Wq_down"]), 16),
        "Wq_up": _sw(np.asarray(inputs["L0_attn_Wq_up"]), 8),
        "Wq_rope": _sw(Wqr, 8),
        "Wkv_down": _sw(np.asarray(inputs["L0_attn_Wkv_down"]), 16),
        "Wk_up": np.stack([_sw(np.ascontiguousarray(
            np.asarray(inputs["L0_attn_Wk_up"])[:, hp * 128:(hp + 1) * 128]), 4) for hp in range(8)]),
        "Wk_rope": np.stack([_sw(np.ascontiguousarray(Wkr[:, hh * 512:(hh + 1) * 512]), 16)
                             for hh in range(2)]),
        "Wv_up": np.stack([_sw(np.ascontiguousarray(
            np.asarray(inputs["L0_attn_Wv_up"])[:, hp * 256:(hp + 1) * 256]), 4) for hp in range(8)]),
        "Wo": np.stack([_sw(np.ascontiguousarray(Wo_[hh * 1024:(hh + 1) * 1024, :]), 8)
                        for hh in range(2)]),
        "router_w": _sw(np.asarray(inputs["L0_ffn_router_w"]), 16),
        "gate_proj": np.stack([np.stack([_sw(np.ascontiguousarray(G[e][:, sl * 256:(sl + 1) * 256]), 16)
                                         for sl in range(4)]) for e in range(NE)]),
        "up_proj": np.stack([np.stack([_sw(np.ascontiguousarray(U[e][:, sl * 256:(sl + 1) * 256]), 16)
                                       for sl in range(4)]) for e in range(NE)]),
        "down_proj": np.stack([np.stack([_sw(np.ascontiguousarray(Dn[e][:, dg * 256:(dg + 1) * 256]), 8)
                                         for dg in range(8)]) for e in range(NE)]),
        "identb": np.eye(128, dtype=ml_dtypes.bfloat16),
        "identf": np.eye(128, dtype=np.float32),
        "iotatok": (np.arange(128)[:, None] + 128 * np.arange(8)[None, :]).astype(np.int32),
        "oobfill": np.full((NE, 512), 10 ** 6, np.int32),
        "ebase": (np.arange(8)[:, None] * 512).astype(np.float32),
    }
    return d


def _host_inputs(inputs, core):
    b, sub = core // 2, core % 2
    tiles = TILES[sub]
    x = np.asarray(inputs["L0_attn_x"])
    x_kv = np.ascontiguousarray(x[b])
    x_own = np.ascontiguousarray(
        np.concatenate([x[b, g * 128:(g + 1) * 128, :] for g in tiles], axis=0))
    m = np.zeros((8, 128, 256), np.float32)
    for j, g in enumerate(tiles):
        E = EXT[j]
        rows = g * 128 + np.arange(128)
        cols = (E - 2) * 128 + np.arange(256)
        m[j] = np.where(cols[None, :] <= rows[:, None], 0.0, NEG)
    half = RD // 2
    inv = 1.0 / (10000.0 ** (np.arange(half, dtype=np.float64) / half))

    def tables(pos, scale):
        ang = pos[None, :] * inv[:, None]
        c = (np.cos(ang) * scale).astype(np.float32)
        s = (np.sin(ang) * scale).astype(np.float32)
        return np.tile(c, (4, 1)), np.tile(s, (4, 1))

    posq = np.concatenate([g * 128 + np.arange(128) for g in tiles]).astype(np.float64)
    posk = np.arange(S, dtype=np.float64)
    cq, sq = tables(posq, SCALE)
    ck, sk = tables(posk, 1.0)
    return {
        "x_kv": x_kv, "x_own": x_own,
        "maskb": m.astype(ml_dtypes.bfloat16),
        "cosq": cq, "sinq": sq, "cosk": ck, "sink": sk,
    }


def kernel(**inputs):
    if "nc" not in _cache:
        _cache["nc"] = build_program(debug=False)
    nc = _cache["nc"]
    shared = _host_weights(inputs)
    in_maps = []
    for c in range(8):
        d = dict(shared)
        d.update(_host_inputs(inputs, c))
        in_maps.append(d)
    res = run_bass_kernel_spmd(nc, in_maps, core_ids=list(range(8)))
    y = np.empty((B, S, D), np.float32)
    for c in range(8):
        b, sub = c // 2, c % 2
        o = res.results[c]["out"]
        for j, g in enumerate(TILES[sub]):
            y[b, g * 128:(g + 1) * 128, :] = o[j * 128:(j + 1) * 128, :]
    return y


# version 27
# speedup vs baseline: 1.1995x; 1.0148x over previous
import contextlib
import numpy as np
import ml_dtypes

import concourse.bass as bass
import concourse.mybir as mybir
from concourse.bass_utils import run_bass_kernel_spmd

F32 = mybir.dt.float32
BF16 = mybir.dt.bfloat16
AF = mybir.ActivationFunctionType
ALU = mybir.AluOpType
AX = mybir.AxisListType

B, S, D = 4, 2048, 2048
H, HD, RD, ND = 16, 128, 64, 64
QR, KVR = 1024, 512
NE, FF = 8, 1024
NEG = -1.0e30
SCALE = HD ** -0.5
EPS = 1e-5
TILES = ([0, 3, 4, 7, 8, 11, 12, 15], [1, 2, 5, 6, 9, 10, 13, 14])
EXT = [2 * j + 2 for j in range(8)]
TOK = 1024
ENGINES = ("tensor", "vector", "scalar", "gpsimd", "sync")

_cache = {}
DBG_SC = True
DBG_ST = True
DBG_AT = True
WAITMAP = {}
REGION = ["init"]



def build_program(debug=False):
    nc = bass.Bass("TRN2", target_bir_lowering=False)

    def din(name, shape, dt=F32):
        return nc.dram_tensor(name, shape, dt, kind="ExternalInput").ap()

    x_kv = din("x_kv", [S, D])
    x_own = din("x_own", [TOK, D])
    Wqd = din("Wq_down", [128, 16 * QR])
    Wqu = din("Wq_up", [128, 8 * H * ND])
    Wqr = din("Wq_rope", [128, 8 * H * RD])
    Wkv = din("Wkv_down", [128, 16 * KVR])
    Wku = din("Wk_up", [8, 128, 4 * 128])
    Wkr = din("Wk_rope", [2, 128, 16 * 512])
    Wvu = din("Wv_up", [8, 128, 4 * 256])
    Wo = din("Wo", [2, 128, 8 * D])
    RW = din("router_w", [128, 16 * NE])
    WG = din("gate_proj", [NE, 4, 128, 16 * 256])
    WU = din("up_proj", [NE, 4, 128, 16 * 256])
    WD = din("down_proj", [NE, 8, 128, 8 * 256])
    identb = din("identb", [128, 128], BF16)
    identf = din("identf", [128, 128], F32)
    maskb = din("maskb", [8, 128, 256], BF16)
    cosq = din("cosq", [128, TOK])
    sinq = din("sinq", [128, TOK])
    cosk = din("cosk", [128, S])
    sink = din("sink", [128, S])
    iotatok = din("iotatok", [128, 8], mybir.dt.int32)
    ebase = din("ebase", [8, 1])
    oobfill = din("oobfill", [NE, 512], mybir.dt.int32)
    h1_dram = nc.dram_tensor("h1_dram", [TOK, D], BF16, kind="Internal").ap()
    h2_dram = nc.dram_tensor("h2_dram", [TOK, D], F32, kind="Internal").ap()
    gidx_dram = nc.dram_tensor("gidx_dram", [NE, 512], mybir.dt.int32, kind="Internal").ap()
    cw_dram = nc.dram_tensor("cw_dram", [TOK, NE], F32, kind="Internal").ap()
    WGb = nc.dram_tensor("wgb_d", [NE, 4, 128, 16 * 256], BF16, kind="Internal").ap()
    WUb = nc.dram_tensor("wub_d", [NE, 4, 128, 16 * 256], BF16, kind="Internal").ap()
    WDb = nc.dram_tensor("wdb_d", [NE, 8, 128, 8 * 256], BF16, kind="Internal").ap()
    out = nc.dram_tensor("out", [TOK, D], F32, kind="ExternalOutput").ap()
    dbg = {}
    if debug:
        for nm, shp, dt in (
            ("d_qcomb", [128, H * TOK], BF16), ("d_kcomb", [128, 2 * S], BF16),
            ("d_avT", [128, H * TOK], BF16), ("d_h1", [128, 8 * D], BF16),
            ("d_h2", [128, 8 * D], F32), ("d_cw", [128, 64], F32),
            ("d_kvlat", [128, 4 * S], BF16), ("d_kpe", [128, 8 * S], BF16),
            ("d_v2", [128, 16 * 256], BF16), ("d_attn", [128, 2048], BF16),
            ("d_scores", [128, 2048], F32), ("d_stats", [128, 8], F32),
            ("d_maskb", [128, 2048], BF16),
            ("d_stat12", [128, 12], F32), ("d_hpre", [128, 2048], F32),
            ("d_attnT", [128, 2048], BF16), ("d_av", [128, 128], F32),
        ):
            dbg[nm] = nc.dram_tensor(nm, shp, dt, kind="ExternalOutput").ap()

    ops = {e: [] for e in ENGINES}
    semh, cnt = {}, {}
    waited = {e: {} for e in ENGINES}
    es = contextlib.ExitStack()
    live = {}

    def mksem(*names):
        for n in names:
            semh[n] = es.enter_context(nc.semaphore(n))
            cnt[n] = 0

    def wait(eng, sname, val=None):
        v = cnt[sname] if val is None else val
        if v <= 0 or waited[eng].get(sname, -1) >= v:
            return
        waited[eng][sname] = v
        reg = REGION[0]

        def run(h, s=semh[sname], vv=v, eng=eng, sname=sname, reg=reg):
            ins = h.wait_ge(s, vv)
            try:
                WAITMAP[ins.ins.name] = (eng, sname, vv, reg)
            except Exception:
                pass
        ops[eng].append(run)

    def waitd(eng, deps):
        best = {}
        for d in deps:
            if d is None:
                continue
            s, v = d
            if best.get(s, -1) < v:
                best[s] = v
        for s, v in best.items():
            wait(eng, s, v)

    def emit(eng, fn, inc=None, by=1):
        tgt = None
        if inc is not None:
            cnt[inc] += by
            tgt = cnt[inc]

        def run(h, fn=fn, inc=inc, by=by):
            r = fn(h)
            if inc is not None:
                r.then_inc(semh[inc], by)
        ops[eng].append(run)
        return (inc, tgt) if inc else None

    def dma(eng, out_ap, in_ap, s):
        return emit(eng, lambda h, o=out_ap, i=in_ap: h.dma_start(out=o, in_=i), inc=s, by=16)

    SB_LO, SB_HI = 16512, 229344
    free_list = [[SB_LO, SB_HI]]
    sizes = {}

    def sb(name, shape, dtype):
        nbytes = shape[1] * (2 if dtype == BF16 else 4)
        nbytes = (nbytes + 255) & ~255
        for blk in free_list:
            if blk[1] - blk[0] >= nbytes:
                off = blk[0]
                blk[0] += nbytes
                sizes[name] = (off, nbytes)
                return nc.alloc_sbuf_tensor_at(name, shape, dtype, offset=off)
        raise RuntimeError(f"SBUF OOM for {name} ({nbytes}B); free={free_list}")

    def psT(name, shape, dtype=F32):
        g = nc.psum_tensor(name, shape, dtype)
        h = g.__enter__()
        live[name] = g
        return h

    def free(*names):
        for n in names:
            if n in live:
                live.pop(n).__exit__(None, None, None)
                continue
            off, nbytes = sizes.pop(n)
            free_list.append([off, off + nbytes])
            free_list.sort()
            merged = [free_list[0]]
            for lo, hi in free_list[1:]:
                if lo == merged[-1][1]:
                    merged[-1][1] = hi
                else:
                    merged.append([lo, hi])
            free_list[:] = merged

    def vsync():
        tok = emit("vector", lambda h: h.engine_nop(), inc="tVE")
        wait("vector", "tVE", cnt["tVE"])
        return tok

    def ssync():
        tok = emit("scalar", lambda h: h.activation(ssync_sb[:], identb_sb[:, 0:1], AF.Copy), inc="tSC")
        wait("scalar", "tSC", cnt["tSC"])
        return tok

    _regc = {}

    def bcreg(h, v):
        if v not in _regc:
            _regc[v] = h.to_reg(v)
        return _regc[v]

    def fence(engines=ENGINES):
        for e in engines:
            for s in ("tPE", "tVE", "tSC", "tGP"):
                wait(e, s, cnt[s])

    mksem("tPE", "tVE", "tSC", "tGP",
          "dINIT", "dWkv", "dWkr0", "dWkr1", "dXSa", "dXSb",
          "dXSc", "dXSd", "dXSe", "dXSf", "dXSg", "dXSh",
          "dWqd", "dWqu", "dWqr", "dWku", "dWvu", "dWo", "dRW",
          "dXO0", "dXO1", "dWg0", "dWg1", "dWu0", "dWu1", "dWd0", "dWd1", "dOUT",
          "dIDB", "dH1S", "dCWS", "dGPF", "dGIX", "dGA0", "dGA1", "dGI0", "dGI1",
          "dCG0", "dCG1", "dSCT", "dH2Z", "dH2L0", "dH2L1", "dIOT", "dPC")

    EV = {"vector": "tVE", "scalar": "tSC", "gpsimd": "tGP"}

    def copy_on(eng, out_ap, in_ap):
        if eng == "scalar":
            return emit(eng, lambda h, o=out_ap, i=in_ap: h.copy(o, i), inc=EV[eng])
        return emit(eng, lambda h, o=out_ap, i=in_ap: h.tensor_copy(o, i), inc=EV[eng])

    def act_copy(out_ap, in_ap, scale=1.0):
        return emit("scalar", lambda h, o=out_ap, i=in_ap, sc=scale:
                    h.activation(o, i, AF.Copy, scale=sc), inc="tSC")

    def mm(psl, wl, rl, stl, spl):
        n = len(psl)
        tok = None
        for i in range(n):
            if i == n - 1:
                tok = emit("tensor", lambda h, p=psl[i], w=wl[i], r=rl[i], st=stl[i], sp=spl[i]:
                           h.matmul(p, w, r, start=st, stop=sp), inc="tPE")
            else:
                emit("tensor", lambda h, p=psl[i], w=wl[i], r=rl[i], st=stl[i], sp=spl[i]:
                     h.matmul(p, w, r, start=st, stop=sp))
        return tok

    ssync_sb = sb("ssync_sb", [128, 1], F32)
    identb_sb = sb("identb_sb", [128, 128], BF16)
    identf_sb = sb("identf_sb", [128, 128], F32)
    maskb_sb = sb("maskb_sb", [128, 8 * 256], BF16)
    cosq_sb = sb("cosq_sb", [128, TOK], F32)
    sinq_sb = sb("sinq_sb", [128, TOK], F32)
    cosk_sb = sb("cosk_sb", [128, S], F32)
    sink_sb = sb("sink_sb", [128, S], F32)
    dma("sync", identb_sb[:], identb[:], "dIDB")
    dma("sync", identf_sb[:], identf[:], "dINIT")
    dma("sync", maskb_sb.rearrange("p (j n) -> p j n", j=8), maskb.rearrange("j p n -> p j n"), "dINIT")
    dma("sync", cosq_sb[:], cosq[:], "dINIT")
    dma("sync", sinq_sb[:], sinq[:], "dINIT")
    dma("sync", cosk_sb[:], cosk[:], "dINIT")
    dma("sync", sink_sb[:], sink[:], "dINIT")
    iot_sb = sb("iot_sb", [128, 8], mybir.dt.int32)
    ebase_sb = sb("ebase_sb", [8, 1], F32)
    dma("sync", iot_sb[:], iotatok[:], "dIOT")
    dma("sync", ebase_sb[:], ebase[:], "dIOT")
    dma("sync", gidx_dram[:], oobfill[:], "dGPF")

    Wkv_sb = sb("Wkv_sb", [128, 16 * KVR], BF16)
    dma("gpsimd", Wkv_sb[:], Wkv[:], "dWkv")
    Wkr_sb = [sb("Wkr_sb0", [128, 16 * 512], BF16), sb("Wkr_sb1", [128, 16 * 512], BF16)]
    for hh in range(2):
        dma("gpsimd", Wkr_sb[hh][:], Wkr[hh], f"dWkr{hh}")

    if debug:
        wait("sync", "dINIT")
        dma("sync", dbg["d_maskb"][:], maskb_sb[:], "dOUT")
    xT_kv = sb("xT_kv", [128, 16 * S], BF16)
    xslab = [sb(f"xslab{i}", [128, D], BF16) for i in range(8)]
    XSEMS = ["dXSa", "dXSb", "dXSc", "dXSd", "dXSe", "dXSf", "dXSg", "dXSh"]
    ptr = [psT("ptr0", [128, 512], BF16), psT("ptr1", [128, 512], BF16)]
    ptr_dep = [[], []]
    slab_dep = [[] for _ in range(8)]
    wait("tensor", "dIDB")

    def transpose_in(n_slab, dst_sb, dst_span, src_dram, sem_ld, wdma=()):
        deps = []
        for gs in range(n_slab // 4):
            for q in range(4):
                s = gs * 4 + q
                buf = s % 8
                waitd("gpsimd", slab_dep[buf])
                dma("gpsimd", xslab[buf][:], src_dram[s * 128:(s + 1) * 128, :], XSEMS[buf])
                if s < len(wdma):
                    wsb, wsrc, wsem = wdma[s]
                    dma("gpsimd", wsb[:], wsrc[:], wsem)
            for q in range(4):
                sem = XSEMS[(gs * 4 + q) % 8]
                wait("tensor", sem, cnt[sem])
            for f in range(16):
                pidx = f % 2
                waitd("tensor", ptr_dep[pidx])
                tp = None
                for q in range(4):
                    buf = (gs * 4 + q) % 8
                    tp = emit("tensor", lambda h, o=ptr[pidx][:, q * 128:(q + 1) * 128],
                              i=xslab[buf][:, f * 128:(f + 1) * 128]:
                              h.transpose(o, i, identb_sb[:]), inc="tPE")
                    if f == 15:
                        slab_dep[buf] = [tp]
                eng = "vector" if f % 2 == 0 else "scalar"
                waitd(eng, [tp])
                ev = copy_on(eng, dst_sb[:, f * dst_span + gs * 512: f * dst_span + gs * 512 + 512],
                             ptr[pidx][:])
                ptr_dep[pidx] = [ev]
                deps.append(ev)
        return deps

    REGION[0] = "A1-xkv"
    xTkv_dep = transpose_in(16, xT_kv, S, x_kv, None)
    free(*[f"xslab{i}" for i in range(8)])

    REGION[0] = "A2-kvlat"
    kv_latT = sb("kv_latT", [128, 4 * S], BF16)
    psA = [psT("psA0", [128, 512], F32), psT("psA1", [128, 512], F32)]
    psA_dep = [[], []]
    kvlat_dep = []
    wait("tensor", "dWkv")
    waitd("tensor", xTkv_dep)
    ti = 0
    for oc in range(4):
        for tg in range(4):
            pidx = ti % 2
            ti += 1
            waitd("tensor", psA_dep[pidx])
            tok = mm([psA[pidx][:]] * 16,
                     [Wkv_sb[:, kc * KVR + oc * 128: kc * KVR + oc * 128 + 128] for kc in range(16)],
                     [xT_kv[:, kc * S + tg * 512: kc * S + tg * 512 + 512] for kc in range(16)],
                     [kc == 0 for kc in range(16)], [kc == 15 for kc in range(16)])
            eng = "vector" if ti % 2 == 0 else "scalar"
            waitd(eng, [tok])
            ev = copy_on(eng, kv_latT[:, oc * S + tg * 512: oc * S + tg * 512 + 512], psA[pidx][:])
            psA_dep[pidx] = [ev]
            kvlat_dep.append(ev)

    REGION[0] = "A3-kpe"
    kpe1 = sb("kpe1", [128, 4 * S], BF16)
    kpe2 = sb("kpe2", [128, 4 * S], BF16)
    tmpA = [sb("tmpA0", [128, 512], F32), sb("tmpA1", [128, 512], F32)]
    tmpB = [sb("tmpB0", [128, 512], F32), sb("tmpB1", [128, 512], F32)]
    tmp_rd = []
    kpe_dep = []
    wait("vector", "dINIT")
    wait("gpsimd", "dINIT")
    psA2 = [psT("psA2", [128, 512], F32), psT("psA3", [128, 512], F32)]
    psA2_dep = [[], []]

    def rope6(ps1, ps2, cos_ap, sin_ap, d1, d2, w):
        waitd("vector", tmp_rd)
        t1c = emit("vector", lambda h, o=tmpA[0][:, 0:w], i=ps1, c=cos_ap: h.tensor_mul(o, i, c), inc="tVE")
        t1s = emit("vector", lambda h, o=tmpA[1][:, 0:w], i=ps1, c=sin_ap: h.tensor_mul(o, i, c), inc="tVE")
        t2c = emit("vector", lambda h, o=tmpB[0][:, 0:w], i=ps2, c=cos_ap: h.tensor_mul(o, i, c), inc="tVE")
        t2s = emit("vector", lambda h, o=tmpB[1][:, 0:w], i=ps2, c=sin_ap: h.tensor_mul(o, i, c), inc="tVE")
        waitd("gpsimd", [t1c, t1s, t2c, t2s])
        o1 = emit("gpsimd", lambda h, o=d1: h.tensor_sub(o, tmpA[0][:, 0:w], tmpB[1][:, 0:w]), inc="tGP")
        o2 = emit("gpsimd", lambda h, o=d2: h.tensor_add(o, tmpA[1][:, 0:w], tmpB[0][:, 0:w]), inc="tGP")
        tmp_rd[:] = [o1, o2]
        return [o1, o2], [t1c, t1s], [t2c, t2s]

    for qd in range(4):
        wait("tensor", f"dWkr{qd // 2}")
        for tg in range(4):
            wb = Wkr_sb[qd // 2]
            woff = (qd % 2) * 256
            waitd("tensor", psA2_dep[0])
            tok1 = mm([psA2[0][:]] * 16,
                      [wb[:, kc * 512 + woff: kc * 512 + woff + 128] for kc in range(16)],
                      [xT_kv[:, kc * S + tg * 512: kc * S + tg * 512 + 512] for kc in range(16)],
                      [kc == 0 for kc in range(16)], [kc == 15 for kc in range(16)])
            waitd("tensor", psA2_dep[1])
            tok2 = mm([psA2[1][:]] * 16,
                      [wb[:, kc * 512 + woff + 128: kc * 512 + woff + 256] for kc in range(16)],
                      [xT_kv[:, kc * S + tg * 512: kc * S + tg * 512 + 512] for kc in range(16)],
                      [kc == 0 for kc in range(16)], [kc == 15 for kc in range(16)])
            waitd("vector", [tok1, tok2])
            col = qd * S + tg * 512
            deps, f1, f2 = rope6(psA2[0][:], psA2[1][:],
                                 cosk_sb[:, tg * 512: tg * 512 + 512],
                                 sink_sb[:, tg * 512: tg * 512 + 512],
                                 kpe1[:, col:col + 512], kpe2[:, col:col + 512], 512)
            psA2_dep[0] = f1
            psA2_dep[1] = f2
            kpe_dep += deps

    free("Wkv_sb", "cosk_sb", "sink_sb", "Wkr_sb0", "Wkr_sb1")

    if debug:
        waitd("sync", kpe_dep)
        dma("sync", dbg["d_kpe"][:, 0:4 * S], kpe1[:], "dOUT")
        dma("sync", dbg["d_kpe"][:, 4 * S:8 * S], kpe2[:], "dOUT")
        waitd("sync", kvlat_dep)
        dma("sync", dbg["d_kvlat"][:], kv_latT[:], "dOUT")

    free("xT_kv")
    fence(("gpsimd", "sync"))
    Wqd_sb = sb("Wqd_sb", [128, 16 * QR], BF16)
    Wqu_sb = sb("Wqu_sb", [128, 8 * H * ND], BF16)
    Wqr_sb = sb("Wqr_sb", [128, 8 * H * RD], BF16)

    xT_own = sb("xT_own", [128, 16 * TOK], BF16)
    wait("vector", "tPE", cnt["tPE"])
    wait("scalar", "tPE", cnt["tPE"])
    REGION[0] = "A5-xown"
    xslab[:] = [sb(f"xslabB{i}", [128, D], BF16) for i in range(8)]
    for i in range(8):
        slab_dep[i] = []
    xTown_dep = transpose_in(8, xT_own, TOK, x_own, None, wdma=[
        (Wqd_sb, Wqd, "dWqd"), (Wqu_sb, Wqu, "dWqu"), (Wqr_sb, Wqr, "dWqr")])

    free(*[f"xslabB{i}" for i in range(8)])
    REGION[0] = "A6-qlat"
    q_latT = sb("q_latT", [128, 8 * TOK], BF16)
    qlat_dep = []
    wait("tensor", "dWqd")
    waitd("tensor", xTown_dep)
    ti = 0
    for oc in range(8):
        for tg in range(2):
            pidx = ti % 2
            ti += 1
            waitd("tensor", psA_dep[pidx])
            tok = mm([psA[pidx][:]] * 16,
                     [Wqd_sb[:, kc * QR + oc * 128: kc * QR + oc * 128 + 128] for kc in range(16)],
                     [xT_own[:, kc * TOK + tg * 512: kc * TOK + tg * 512 + 512] for kc in range(16)],
                     [kc == 0 for kc in range(16)], [kc == 15 for kc in range(16)])
            eng = "vector" if ti % 2 == 0 else "scalar"
            waitd(eng, [tok])
            ev = copy_on(eng, q_latT[:, oc * TOK + tg * 512: oc * TOK + tg * 512 + 512], psA[pidx][:])
            psA_dep[pidx] = [ev]
            qlat_dep.append(ev)

    free("Wqd_sb", "xT_own")
    REGION[0] = "A7-qcomb"
    q_comb = sb("q_comb", [128, H * TOK], BF16)
    qcomb_dep = []
    wait("tensor", "dWqu")
    waitd("tensor", qlat_dep)
    ti = 0
    for hp in range(8):
        for tg in range(2):
            pidx = ti % 2
            ti += 1
            waitd("tensor", psA_dep[pidx])
            tok = mm([psA[pidx][:]] * 8,
                     [Wqu_sb[:, kc * H * ND + hp * 128: kc * H * ND + hp * 128 + 128] for kc in range(8)],
                     [q_latT[:, kc * TOK + tg * 512: kc * TOK + tg * 512 + 512] for kc in range(8)],
                     [kc == 0 for kc in range(8)], [kc == 7 for kc in range(8)])
            waitd("scalar", [tok])
            e1 = act_copy(q_comb[0:64, (2 * hp) * TOK + tg * 512: (2 * hp) * TOK + tg * 512 + 512],
                          psA[pidx][0:64, :], scale=SCALE)
            e2 = act_copy(q_comb[0:64, (2 * hp + 1) * TOK + tg * 512: (2 * hp + 1) * TOK + tg * 512 + 512],
                          psA[pidx][64:128, :], scale=SCALE)
            psA_dep[pidx] = [e2]
            qcomb_dep += [e1, e2]
    wait("tensor", "dWqr")
    qr1 = sb("qr1", [128, 512], BF16)
    qr2 = sb("qr2", [128, 512], BF16)
    qr_rd = []
    for qd in range(4):
        for tg in range(2):
            waitd("tensor", psA2_dep[0])
            tok1 = mm([psA2[0][:]] * 8,
                      [Wqr_sb[:, kc * H * RD + qd * 256: kc * H * RD + qd * 256 + 128] for kc in range(8)],
                      [q_latT[:, kc * TOK + tg * 512: kc * TOK + tg * 512 + 512] for kc in range(8)],
                      [kc == 0 for kc in range(8)], [kc == 7 for kc in range(8)])
            waitd("tensor", psA2_dep[1])
            tok2 = mm([psA2[1][:]] * 8,
                      [Wqr_sb[:, kc * H * RD + qd * 256 + 128: kc * H * RD + qd * 256 + 256] for kc in range(8)],
                      [q_latT[:, kc * TOK + tg * 512: kc * TOK + tg * 512 + 512] for kc in range(8)],
                      [kc == 0 for kc in range(8)], [kc == 7 for kc in range(8)])
            waitd("vector", [tok1, tok2])
            waitd("vector", qr_rd)
            waitd("gpsimd", qr_rd)
            deps, f1, f2 = rope6(psA2[0][:], psA2[1][:],
                                 cosq_sb[:, tg * 512: tg * 512 + 512],
                                 sinq_sb[:, tg * 512: tg * 512 + 512],
                                 qr1[:], qr2[:], 512)
            psA2_dep[0] = f1
            psA2_dep[1] = f2
            qr_rd = []
            for i_ in range(4):
                hh2 = 4 * qd + i_
                eng = "vector" if i_ % 2 == 0 else "scalar"
                waitd(eng, deps)
                c1 = copy_on(eng, q_comb[64:96, hh2 * TOK + tg * 512: hh2 * TOK + tg * 512 + 512],
                             qr1[i_ * 32:(i_ + 1) * 32, :])
                c2 = copy_on(eng, q_comb[96:128, hh2 * TOK + tg * 512: hh2 * TOK + tg * 512 + 512],
                             qr2[i_ * 32:(i_ + 1) * 32, :])
                qr_rd += [c1, c2]
                qcomb_dep += [c1, c2]
    free("qr1", "qr2")
    free("Wqu_sb", "Wqr_sb", "q_latT", "cosq_sb", "sinq_sb",
         "tmpA0", "tmpA1", "tmpB0", "tmpB1")

    if debug:
        waitd("sync", qcomb_dep)
        dma("sync", dbg["d_qcomb"][:], q_comb[:], "dOUT")

    fence(("gpsimd", "sync"))
    Wku_b = [sb("Wku_b0", [128, 512], BF16), sb("Wku_b1", [128, 512], BF16)]
    Wvu_b = [sb("Wvu_b0", [128, 1024], BF16), sb("Wvu_b1", [128, 1024], BF16)]
    wkv_rd = [[], []]
    dma("gpsimd", Wku_b[0][:], Wku[0], "dWku")
    dma("gpsimd", Wvu_b[0][:], Wvu[0], "dWvu")
    Wo_sbA = sb("Wo_sbA", [128, 8 * D], BF16)
    dma("gpsimd", Wo_sbA[:], Wo[0], "dWo")

    free("psA3", "psA2", "psA1", "psA0")
    psS = psT("psS0", [128, 2048], F32)
    psS_dep = []
    pav = [psT("pav0", [128, 128], F32), psT("pav1", [128, 128], F32)]
    pav_dep = [[], []]

    avT = sb("avT", [128, H * TOK], BF16)
    k_comb2 = [sb("k_comb_a", [128, 2 * S], BF16), sb("k_comb_b", [128, 2 * S], BF16)]
    v2b = [sb("v2_a", [128, 16 * 256], BF16), sb("v2_b", [128, 16 * 256], BF16)]
    attn2 = [sb("attn_sb0", [128, 2048], BF16), sb("attn_sb1", [128, 2048], BF16)]
    attnT2 = [sb("attnT_sb0", [128, 2048], BF16), sb("attnT_sb1", [128, 2048], BF16)]
    nmx = sb("nmx", [128, 2], F32)
    den = sb("den", [128, 2], F32)
    rden = sb("rden", [128, 2], F32)
    avT_dep = []
    nmx_r = [[], []]

    slot_ctr = [0]
    bank_pos = [0]
    bank_dep = [[], [], [], []]
    pend = []
    attn_free = [[], []]
    attnT_free = [[], []]

    def flush_slot(p):
        E2, bp2, s2, hh3, j3 = p["E"], p["bp"], p["s_"], p["hh"], p["j"]
        REGION[0] = f"ATT-flush-j{j3}"
        vb = v2b[p["pp"]]
        waitd("tensor", [p["nrm"]])
        cop_toks = []
        ngrp = (E2 + 3) // 4
        for g in range(ngrp):
            c0, c1 = g * 4, min(g * 4 + 4, E2)
            pidx2 = g % 2
            waitd("tensor", ptr_dep[pidx2])
            tp = None
            for c in range(c0, c1):
                tp = emit("tensor", lambda h, o=ptr[pidx2][:, (c - c0) * 128:(c - c0) * 128 + 128],
                          i=attn2[bp2][:, c * 128: c * 128 + 128]:
                          h.transpose(o, i, identb_sb[:]), inc="tPE")
            if g == ngrp - 1:
                attn_free[bp2] = [tp]
            eng = "vector" if g % 2 == 0 else "scalar"
            waitd(eng, [tp])
            waitd(eng, attnT_free[bp2])
            ev = copy_on(eng, attnT2[bp2][:, c0 * 128: c1 * 128], ptr[pidx2][:, 0:(c1 - c0) * 128])
            ptr_dep[pidx2] = [ev]
            cop_toks.append((ev, c0, c1))
        waitd("tensor", pav_dep[bp2])
        av_tok = None
        for ev, c0, c1 in cop_toks:
            waitd("tensor", [ev])
            for c in range(c0, c1):
                av_tok = mm([pav[bp2][:]],
                            [vb[:, c * 256 + s2 * 128: c * 256 + s2 * 128 + 128]],
                            [attnT2[bp2][:, c * 128: c * 128 + 128]],
                            [c == 0], [c == E2 - 1])
        attnT_free[bp2] = [av_tok]
        pair_av[p["pp"]] = pair_av[p["pp"]] + [av_tok]
        eng = "vector" if j3 % 2 == 0 else "scalar"
        waitd(eng, [av_tok])
        ev = copy_on(eng, avT[:, hh3 * TOK + j3 * 128: hh3 * TOK + j3 * 128 + 128], pav[bp2][:])
        pav_dep[bp2] = [ev]
        avT_dep.append(ev)

    waitd("tensor", kvlat_dep)
    ti = 0
    pair_av = [[], []]
    kc_ready_p = [[], []]
    REGION[0] = "ATT"
    pctasks = []
    for e in range(NE):
        for sl4 in range(4):
            pctasks.append((WGb[e, sl4], WG[e, sl4]))
            pctasks.append((WUb[e, sl4], WU[e, sl4]))
        for dg in range(8):
            pctasks.append((WDb[e, dg], WD[e, dg]))
    for hp in range(8):
        pp = hp % 2
        k_comb = k_comb2[pp]
        v2 = v2b[pp]
        asm_dep = [[], [], [], []]
        wait("tensor", "dWku")
        wait("tensor", "dWvu")
        for tg in range(4):
            pidx = ti % 4
            ti += 1
            waitd("tensor", bank_dep[pidx])
            waitd("tensor", asm_dep[pidx])
            sl = psS[:, pidx * 512:(pidx + 1) * 512]
            tok = mm([sl] * 4,
                     [Wku_b[pp][:, kc * 128:(kc + 1) * 128] for kc in range(4)],
                     [kv_latT[:, kc * S + tg * 512: kc * S + tg * 512 + 512] for kc in range(4)],
                     [kc == 0 for kc in range(4)], [kc == 3 for kc in range(4)])
            eng = "vector" if tg % 2 == 0 else "scalar"
            waitd(eng, [tok])
            waitd(eng, pair_av[pp])
            e1 = copy_on(eng, k_comb[0:64, 0 * S + tg * 512: 0 * S + tg * 512 + 512], sl[0:64, :])
            e2 = copy_on(eng, k_comb[0:64, 1 * S + tg * 512: 1 * S + tg * 512 + 512], sl[64:128, :])
            asm_dep[pidx] = [e2]
        for s_ in range(2):
            hh2 = 2 * hp + s_
            qd2, i2 = hh2 // 4, hh2 % 4
            eng = "vector" if s_ == 0 else "scalar"
            waitd(eng, kpe_dep)
            waitd(eng, pair_av[pp])
            copy_on(eng, k_comb[64:96, s_ * S: s_ * S + S],
                    kpe1[i2 * 32:(i2 + 1) * 32, qd2 * S: qd2 * S + S])
            copy_on(eng, k_comb[96:128, s_ * S: s_ * S + S],
                    kpe2[i2 * 32:(i2 + 1) * 32, qd2 * S: qd2 * S + S])
        kc_ready = [("tVE", cnt["tVE"]), ("tSC", cnt["tSC"])]
        for tc in range(16):
            pidx = ti % 4
            ti += 1
            waitd("tensor", asm_dep[pidx])
            waitd("tensor", bank_dep[pidx])
            sl = psS[:, pidx * 512: pidx * 512 + 256]
            tok = mm([sl] * 4,
                     [kv_latT[:, kc * S + tc * 128: kc * S + tc * 128 + 128] for kc in range(4)],
                     [Wvu_b[pp][:, kc * 256:(kc + 1) * 256] for kc in range(4)],
                     [kc == 0 for kc in range(4)], [kc == 3 for kc in range(4)])
            eng = "vector" if tc % 2 == 0 else "scalar"
            waitd(eng, [tok])
            waitd(eng, pair_av[pp])
            ev = copy_on(eng, v2[:, tc * 256: tc * 256 + 256], sl)
            asm_dep[pidx] = [ev]
        for b in range(4):
            bank_dep[b] = bank_dep[b] + asm_dep[b]
        bank_pos[0] = 0
        v_ready = [("tVE", cnt["tVE"]), ("tSC", cnt["tSC"])]
        wkv_rd[pp] = [("tPE", cnt["tPE"])]
        if hp < 7:
            waitd("gpsimd", wkv_rd[(hp + 1) % 2])
            dma("gpsimd", Wku_b[(hp + 1) % 2][:], Wku[hp + 1], "dWku")
            dma("gpsimd", Wvu_b[(hp + 1) % 2][:], Wvu[hp + 1], "dWvu")

        if debug and hp == 7:
            waitd("sync", kc_ready)
            dma("sync", dbg["d_kcomb"][:], k_comb[:], "dOUT")
            waitd("sync", v_ready)
            dma("sync", dbg["d_v2"][:], v2[:], "dOUT")

        for j in range(8):
            for s_ in range(2):
                hh = 2 * hp + s_
                E = EXT[j]
                cols = E * 128
                nbank = (cols + 511) // 512
                bp = slot_ctr[0] % 2
                slot_ctr[0] += 1
                if bank_pos[0] + nbank > 4:
                    bank_pos[0] = 0
                b0 = bank_pos[0]
                bank_pos[0] += nbank
                waitd("tensor", sum((bank_dep[b] for b in range(b0, b0 + nbank)), []))
                base = b0 * 512
                qsl = q_comb[:, hh * TOK + j * 128: hh * TOK + j * 128 + 128]
                REGION[0] = f"ATT-score-j{j}"
                waitd("tensor", qcomb_dep)
                waitd("tensor", kc_ready)
                psl, wl, rl, stl, spl = [], [], [], [], []
                off = 0
                while off < cols:
                    n = min(512, cols - off)
                    psl.append(psS[:, base + off:base + off + n])
                    wl.append(qsl)
                    rl.append(k_comb[:, s_ * S + off: s_ * S + off + n])
                    stl.append(True)
                    spl.append(off + n < cols)
                    off += n
                psl.append(psS[:, base + cols - 256:base + cols])
                wl.append(identb_sb[:])
                rl.append(maskb_sb[:, j * 256: j * 256 + 256])
                stl.append(False)
                spl.append(True)
                score_tok = mm(psl, wl, rl, stl, spl)
                if len(pend) >= 2:
                    flush_slot(pend.pop(0))
                REGION[0] = f"ATT-smax-j{j}"
                waitd("vector", [score_tok])
                waitd("vector", nmx_r[bp])
                emit("vector", lambda h, o=nmx[:, bp:bp + 1], i=psS[:, base:base + cols]:
                     h.reduce_max(o, i, axis=AX.X, negate=True), inc="tVE")
                nm_done = [("tVE", cnt["tVE"])]
                waitd("scalar", nm_done)
                waitd("scalar", attn_free[bp])
                exp_tok = emit("scalar",
                    lambda h, o=attn2[bp][:, 0:cols], i=psS[:, base:base + cols], b2=nmx[:, bp:bp + 1],
                    a2=den[:, bp:bp + 1]:
                    h.activation(o, i, AF.Exp, bias=b2, scale=1.0, accum_out=a2), inc="tSC")
                for b in range(b0, b0 + nbank):
                    bank_dep[b] = [exp_tok]
                nmx_r[bp] = [exp_tok]
                waitd("vector", [exp_tok])
                emit("vector", lambda h, o=rden[:, bp:bp + 1], i=den[:, bp:bp + 1]:
                     h.reciprocal(o, i), inc="tVE")
                vsync()
                nrm = emit("vector", lambda h, w=cols, b2=bp: h.tensor_single_scalar(
                    attn2[b2][:, 0:w], attn2[b2][:, 0:w], rden[:, b2:b2 + 1], ALU.mult), inc="tVE")
                pend.append(dict(E=E, cols=cols, s_=s_, hh=hh, j=j, bp=bp, nrm=nrm, pp=pp))
        for o_, i_ in pctasks[hp * 16:(hp + 1) * 16]:
            dma("gpsimd", o_, i_, "dPC")
    while pend:
        flush_slot(pend.pop(0))

    if debug:
        waitd("sync", avT_dep)
        dma("sync", dbg["d_avT"][:], avT[:], "dOUT")


    free("k_comb_a", "k_comb_b", "v2_a", "v2_b", "attn_sb0", "attn_sb1", "attnT_sb0", "attnT_sb1", "q_comb", "kpe1", "kpe2", "kv_latT",
         "Wku_b0", "Wku_b1", "Wvu_b0", "Wvu_b1", "maskb_sb", "nmx", "den", "rden",
         "pav1", "pav0", "psS0")
    fence()
    Wo_sbB = sb("Wo_sbB", [128, 8 * D], BF16)
    dma("gpsimd", Wo_sbB[:], Wo[1], "dWo")
    REGION[0] = "LN1"
    po = psT("po", [128, 2048], F32)
    po_dep = []
    po_dep_a = []
    xo = [sb("xo0", [128, D], F32), sb("xo1", [128, D], F32)]
    xo_dep = [[], []]
    hpre = [sb("hpre0", [128, D], F32), sb("hpre1", [128, D], F32)]
    hpre_dep = [[], []]
    sqs = sb("sqs", [128, 512], F32)
    stat = sb("stat", [128, 12], F32)
    if debug:
        dstat12_sb = sb("dstat12_sb", [128, 12], F32)
        dhpre_sb = sb("dhpre_sb", [128, 2048], F32)
    h1b = sb("h1b", [128, 8 * D], BF16)
    h1T0 = sb("h1T0", [128, 8 * TOK], BF16)
    h1T1 = sb("h1T1", [128, 8 * TOK], BF16)

    def h1T_ap(fc, a, b):
        t = h1T0 if fc < 8 else h1T1
        return t[:, (fc % 8) * TOK + a: (fc % 8) * TOK + b]
    h1b_dep, h1T_dep = [], []
    stat_r, sq_r = [], []

    wait("tensor", "dWo")
    waitd("tensor", avT_dep)

    def ln_stats(bi, rs):
        waitd("scalar", [rs])
        waitd("scalar", sq_r)
        for qq in range(4):
            wait("scalar", "tSC", cnt["tSC"])
            emit("scalar", lambda h, i=hpre[bi][:, qq * 512:(qq + 1) * 512], ac=stat[:, 1 + qq:2 + qq]:
                 h.activation(sqs[:], i, AF.Square, accum_out=ac), inc="tSC")
        ssync()
        sq_done = [("tSC", cnt["tSC"])]
        waitd("vector", sq_done)
        emit("vector", lambda h: h.reduce_sum(stat[:, 5:6], stat[:, 1:5], axis=AX.X), inc="tVE")
        emit("vector", lambda h: h.tensor_scalar_mul(stat[:, 6:7], stat[:, 0:1], 1.0 / D), inc="tVE")
        vsync()
        emit("vector", lambda h: h.tensor_scalar(stat[:, 7:8], stat[:, 6:7], stat[:, 6:7][:, 0:1], EPS,
                                                 ALU.mult, ALU.subtract), inc="tVE")
        vsync()
        emit("vector", lambda h: h.scalar_tensor_tensor(stat[:, 8:9], stat[:, 5:6], 1.0 / D, stat[:, 7:8],
                                                        ALU.mult, ALU.subtract), inc="tVE")
        var_done = [("tVE", cnt["tVE"])]
        waitd("scalar", var_done)
        emit("scalar", lambda h: h.activation(stat[:, 9:10], stat[:, 8:9], AF.Ln), inc="tSC")
        ssync()
        return emit("scalar", lambda h: h.activation(stat[:, 10:11], stat[:, 9:10], AF.Exp, scale=-0.5), inc="tSC")

    for tc in range(8):
        bi = tc % 2
        waitd("sync", xo_dep[bi])
        dma("sync", xo[bi][:], x_own[tc * 128:(tc + 1) * 128, :], f"dXO{bi}")
        tok = tokh = None
        for dq in range(4):
            waitd("tensor", po_dep_a if dq < 2 else po_dep)
            tok = mm([po[:, dq * 512:(dq + 1) * 512]] * 16,
                     [avT[:, kc * TOK + tc * 128: kc * TOK + tc * 128 + 128] for kc in range(16)],
                     [(Wo_sbA if kc < 8 else Wo_sbB)[:, (kc % 8) * D + dq * 512: (kc % 8) * D + dq * 512 + 512] for kc in range(16)],
                     [kc == 0 for kc in range(16)], [kc == 15 for kc in range(16)])
            if dq == 1:
                tokh = tok
        waitd("vector", [tokh])
        wait("vector", f"dXO{bi}")
        waitd("vector", hpre_dep[bi])
        waitd("vector", stat_r)
        rs_a = emit("vector", lambda h, o=hpre[bi][:, 0:1024], p=po[:, 0:1024], x=xo[bi][:, 0:1024]:
                    h.scalar_tensor_tensor(o, p, 1.0, x, ALU.mult, ALU.add, accum_out=stat[:, 0:1]), inc="tVE")
        waitd("vector", [tok])
        rs_b = emit("vector", lambda h, o=hpre[bi][:, 1024:2048], p=po[:, 1024:2048], x=xo[bi][:, 1024:2048]:
                    h.scalar_tensor_tensor(o, p, 1.0, x, ALU.mult, ALU.add, accum_out=stat[:, 11:12]), inc="tVE")
        vsync()
        rs = emit("vector", lambda h: h.tensor_add(stat[:, 0:1], stat[:, 0:1], stat[:, 11:12]), inc="tVE")
        po_dep_a = [rs_a]
        po_dep = [rs_b]
        xo_dep[bi] = [rs_b]
        rstd_tok = ln_stats(bi, rs)
        sq_r = [rstd_tok]
        waitd("vector", [rstd_tok])
        vsync()
        nh = emit("vector", lambda h, o=h1b[:, tc * D: tc * D + D], i=hpre[bi][:]:
                  h.tensor_scalar(o, i, stat[:, 6:7], stat[:, 10:11], ALU.subtract, ALU.mult), inc="tVE")
        hpre_dep[bi] = [nh]
        stat_r = [nh]
        h1b_dep.append(nh)
        waitd("sync", [nh])
        dma("sync", h1_dram[tc * 128:(tc + 1) * 128, :], h1b[:, tc * D: tc * D + D], "dH1S")
        if debug and tc == 7:
            c1 = copy_on("vector", dstat12_sb[:], stat[:])
            c2 = copy_on("vector", dhpre_sb[:], hpre[0][:])
            waitd("sync", [c1, c2])
            dma("sync", dbg["d_stat12"][:], dstat12_sb[:], "dOUT")
            dma("sync", dbg["d_hpre"][:], dhpre_sb[:], "dOUT")
        for f in range(16):
            pidx = f % 2
            waitd("tensor", ptr_dep[pidx])
            waitd("tensor", [nh])
            tp = emit("tensor", lambda h, o=ptr[pidx][:, 0:128], i=h1b[:, tc * D + f * 128: tc * D + f * 128 + 128]:
                      h.transpose(o, i, identb_sb[:]), inc="tPE")
            eng = "vector" if f % 2 == 0 else "scalar"
            waitd(eng, [tp])
            ev = copy_on(eng, h1T_ap(f, tc * 128, tc * 128 + 128), ptr[pidx][:, 0:128])
            ptr_dep[pidx] = [ev]
            h1T_dep.append(ev)

    if debug:
        free("dstat12_sb", "dhpre_sb")
    free("avT", "Wo_sbA", "Wo_sbB", "xo0", "xo1", "po")

    fence(("gpsimd", "sync"))
    REGION[0] = "ROUTER"
    rw_sb = sb("rw_sb", [128, 16 * NE], BF16)
    dma("gpsimd", rw_sb[:], RW[:], "dRW")
    ps_r = psT("ps_r", [8, 512], F32)
    ptrf = psT("ptrf", [128, 128], F32)
    sc_sb = sb("sc_sb", [8, TOK], F32)
    sc_tok = sb("sc_tok", [128, 64], F32)
    p_sb = sb("p_sb", [128, 64], F32)
    t8a = sb("t8a", [128, 64], F32)
    t8b = sb("t8b", [128, 64], F32)
    t8c = sb("t8c", [128, 64], F32)
    rst8 = sb("rst8", [128, 48], F32)
    cw_sb = sb("cw_sb", [128, 8 * NE], F32)
    ind_sb = sb("ind_sb", [128, 8 * NE], F32)
    msk1_sb = sb("msk1_sb", [128, 8 * NE], F32)
    msk2_sb = sb("msk2_sb", [128, 8 * NE], F32)
    oi12 = sb("oi12", [128, 16], mybir.dt.int32)
    oi12f = sb("oi12f", [128, 16], F32)
    tmp16 = sb("tmp16", [128, 16], F32)
    offs_tok = sb("offs_tok", [128, 8 * NE], F32)
    indT = sb("indT", [8, TOK], F32)
    posA = sb("posA", [8, TOK], F32)
    posB = sb("posB", [8, TOK], F32)
    cw_dep = []
    wait("tensor", "dRW")
    waitd("tensor", h1T_dep)
    scsb_dep = []
    prev_ev = None
    for g in range(2):
        if prev_ev is not None:
            waitd("tensor", [prev_ev])
        tok = mm([ps_r[:]] * 16,
                 [rw_sb[:, kc * NE: kc * NE + NE] for kc in range(16)],
                 [h1T_ap(kc, g * 512, g * 512 + 512) for kc in range(16)],
                 [kc == 0 for kc in range(16)], [kc == 15 for kc in range(16)])
        waitd("scalar", [tok])
        prev_ev = act_copy(sc_sb[:, g * 512:(g + 1) * 512], ps_r[:])
        scsb_dep.append(prev_ev)
    ptrf_rd = None
    sc_tokw = sc_tok
    trtoks = []
    for tc in range(8):
        waitd("tensor", scsb_dep)
        if ptrf_rd is not None:
            waitd("tensor", [ptrf_rd])
        tp = emit("tensor", lambda h, o=ptrf[:, 0:8], i=sc_sb[:, tc * 128: tc * 128 + 128]:
                  h.transpose(o, i, identf_sb[0:8, 0:8]), inc="tPE")
        waitd("vector", [tp])
        ptrf_rd = emit("vector", lambda h, o=sc_tokw[:, tc * 8:(tc + 1) * 8]:
                       h.tensor_copy(o, ptrf[:, 0:8]), inc="tVE")
        trtoks.append(ptrf_rd)

    def step(fn_of_tc, eng="vector"):
        vsync()
        for tc in range(8):
            emit(eng, fn_of_tc(tc), inc=EV[eng])

    def SL(t, tc):
        return t[:, tc * 8:(tc + 1) * 8]

    def RC(tc, i):
        return rst8[:, tc * 6 + i: tc * 6 + i + 1]

    waitd("vector", trtoks)
    step(lambda tc: (lambda h: h.reduce_max(RC(tc, 0), SL(sc_tokw, tc), axis=AX.X, negate=True)))
    sm_tok = [("tVE", cnt["tVE"])]
    waitd("scalar", sm_tok)
    e1 = None
    for tc in range(8):
        e1 = emit("scalar", lambda h, o=SL(p_sb, tc), i=SL(sc_tokw, tc), b=RC(tc, 0):
                  h.activation(o, i, AF.Exp, bias=b), inc="tSC")
    waitd("vector", [e1])
    step(lambda tc: (lambda h: h.reduce_max(RC(tc, 1), SL(p_sb, tc), axis=AX.X)))
    step(lambda tc: (lambda h: h.tensor_single_scalar(SL(t8a, tc), SL(p_sb, tc), RC(tc, 1), ALU.is_ge)))
    step(lambda tc: (lambda h: h.tensor_copy(SL(t8b, tc), SL(t8a, tc))))
    step(lambda tc: (lambda h: h.tensor_add(t8b[:, tc * 8 + 1:tc * 8 + 8], t8a[:, tc * 8 + 1:tc * 8 + 8], t8a[:, tc * 8:tc * 8 + 7])))
    step(lambda tc: (lambda h: h.tensor_copy(SL(t8c, tc), SL(t8b, tc))))
    step(lambda tc: (lambda h: h.tensor_add(t8c[:, tc * 8 + 2:tc * 8 + 8], t8b[:, tc * 8 + 2:tc * 8 + 8], t8b[:, tc * 8:tc * 8 + 6])))
    step(lambda tc: (lambda h: h.tensor_copy(SL(t8b, tc), SL(t8c, tc))))
    step(lambda tc: (lambda h: h.tensor_add(t8b[:, tc * 8 + 4:tc * 8 + 8], t8c[:, tc * 8 + 4:tc * 8 + 8], t8c[:, tc * 8:tc * 8 + 4])))
    step(lambda tc: (lambda h: h.tensor_single_scalar(SL(t8c, tc), SL(t8b, tc), 1.5, ALU.is_le)))
    step(lambda tc: (lambda h: h.tensor_mul(SL(t8c, tc), SL(t8c, tc), SL(t8a, tc))))
    step(lambda tc: (lambda h: h.tensor_scalar(SL(t8a, tc), SL(t8c, tc), -1.0, 1.0, ALU.mult, ALU.add)))
    step(lambda tc: (lambda h: h.tensor_mul(SL(t8a, tc), SL(t8a, tc), SL(p_sb, tc))))
    step(lambda tc: (lambda h: h.reduce_max(RC(tc, 2), SL(t8a, tc), axis=AX.X)))
    step(lambda tc: (lambda h: h.tensor_single_scalar(SL(t8b, tc), SL(t8a, tc), RC(tc, 2), ALU.is_ge)))
    step(lambda tc: (lambda h: h.tensor_copy(SL(sc_tokw, tc), SL(t8b, tc))))
    step(lambda tc: (lambda h: h.tensor_add(sc_tokw[:, tc * 8 + 1:tc * 8 + 8], t8b[:, tc * 8 + 1:tc * 8 + 8], t8b[:, tc * 8:tc * 8 + 7])))
    step(lambda tc: (lambda h: h.tensor_copy(SL(t8a, tc), SL(sc_tokw, tc))))
    step(lambda tc: (lambda h: h.tensor_add(t8a[:, tc * 8 + 2:tc * 8 + 8], sc_tokw[:, tc * 8 + 2:tc * 8 + 8], sc_tokw[:, tc * 8:tc * 8 + 6])))
    step(lambda tc: (lambda h: h.tensor_copy(SL(sc_tokw, tc), SL(t8a, tc))))
    step(lambda tc: (lambda h: h.tensor_add(sc_tokw[:, tc * 8 + 4:tc * 8 + 8], t8a[:, tc * 8 + 4:tc * 8 + 8], t8a[:, tc * 8:tc * 8 + 4])))
    step(lambda tc: (lambda h: h.tensor_single_scalar(SL(t8a, tc), SL(sc_tokw, tc), 1.5, ALU.is_le)))
    step(lambda tc: (lambda h: h.tensor_mul(SL(t8b, tc), SL(t8a, tc), SL(t8b, tc))))
    step(lambda tc: (lambda h: h.tensor_add(SL(ind_sb, tc), SL(t8c, tc), SL(t8b, tc))))
    step(lambda tc: (lambda h: h.tensor_copy(SL(msk1_sb, tc), SL(t8c, tc))))
    step(lambda tc: (lambda h: h.tensor_copy(SL(msk2_sb, tc), SL(t8b, tc))))
    step(lambda tc: (lambda h: h.tensor_add(RC(tc, 3), RC(tc, 1), RC(tc, 2))))
    step(lambda tc: (lambda h: h.reciprocal(RC(tc, 4), RC(tc, 3))))
    step(lambda tc: (lambda h: h.tensor_mul(RC(tc, 1), RC(tc, 1), RC(tc, 4))))
    step(lambda tc: (lambda h: h.tensor_mul(RC(tc, 2), RC(tc, 2), RC(tc, 4))))
    step(lambda tc: (lambda h: h.tensor_single_scalar(SL(t8c, tc), SL(t8c, tc), RC(tc, 1), ALU.mult)))
    step(lambda tc: (lambda h: h.tensor_single_scalar(SL(t8b, tc), SL(t8b, tc), RC(tc, 2), ALU.mult)))
    vsync()
    cw_dep = []
    for tc in range(8):
        cwt = emit("vector", lambda h, o=SL(cw_sb, tc), a=SL(t8c, tc), b=SL(t8b, tc):
                   h.tensor_add(o, a, b), inc="tVE")
        cw_dep.append(cwt)
        waitd("sync", [cwt])
        dma("sync", cw_dram[tc * 128:(tc + 1) * 128, :], cw_sb[:, tc * NE: tc * NE + NE], "dCWS")

    BIGV = 1.0e6
    trT = []
    for tc in range(8):
        waitd("tensor", cw_dep)
        if trT:
            waitd("tensor", trT[-1:])
        tp = emit("tensor", lambda h, i=ind_sb[:, tc * NE: tc * NE + NE]:
                  h.transpose(ptrf[0:8, 0:128], i, identf_sb[:]), inc="tPE")
        waitd("vector", [tp])
        trT.append(emit("vector", lambda h, o=indT[:, tc * 128:(tc + 1) * 128]:
                        h.tensor_copy(o, ptrf[0:8, 0:128]), inc="tVE"))
    cur, oth = posA, posB
    waitd("vector", trT)
    emit("vector", lambda h: h.tensor_copy(cur[:], indT[:]), inc="tVE")
    sh = 1
    while sh < TOK:
        vsync()
        emit("vector", lambda h, a=cur, b=oth, s=sh: h.tensor_copy(b[:, 0:s], a[:, 0:s]), inc="tVE")
        emit("vector", lambda h, a=cur, b=oth, s=sh: h.tensor_add(b[:, s:TOK], a[:, s:TOK], a[:, 0:TOK - s]), inc="tVE")
        cur, oth = oth, cur
        sh *= 2
    vsync()
    emit("vector", lambda h, a=cur, b=oth: h.tensor_mul(b[:], a[:], indT[:]), inc="tVE")
    vsync()
    emit("vector", lambda h, b=oth: h.tensor_scalar(indT[:], indT[:], -BIGV, BIGV - 1.0, ALU.mult, ALU.add), inc="tVE")
    vsync()
    emit("vector", lambda h, b=oth: h.tensor_add(b[:], b[:], indT[:]), inc="tVE")
    vsync()
    wait("vector", "dIOT")
    emit("vector", lambda h, b=oth: h.tensor_single_scalar(b[:], b[:], ebase_sb[:, 0:1], ALU.add), inc="tVE")
    offs_done = [("tVE", cnt["tVE"])]
    trT2 = []
    for tc in range(8):
        waitd("tensor", offs_done)
        if trT2:
            waitd("tensor", trT2[-1:])
        tp = emit("tensor", lambda h, i=oth[0:8, tc * 128:(tc + 1) * 128]: h.transpose(
            ptrf[:, 8:16], i, identf_sb[0:8, 0:8]), inc="tPE")
        waitd("vector", [tp])
        trT2.append(emit("vector", lambda h, o=offs_tok[:, tc * NE: tc * NE + NE]:
                         h.tensor_copy(o, ptrf[:, 8:16]), inc="tVE"))
    waitd("vector", trT2)
    for tc in range(8):
        for pk in range(2):
            mk = msk1_sb if pk == 0 else msk2_sb
            vsync()
            emit("vector", lambda h, a=offs_tok[:, tc * NE: tc * NE + NE],
                 b=mk[:, tc * NE: tc * NE + NE]: h.tensor_mul(tmp16[:, 0:NE], a, b), inc="tVE")
            vsync()
            emit("vector", lambda h, o=oi12f[:, tc * 2 + pk: tc * 2 + pk + 1]:
                 h.reduce_sum(o, tmp16[:, 0:NE], axis=AX.X), inc="tVE")
    vsync()
    ocast = emit("vector", lambda h: h.tensor_copy(oi12[:], oi12f[:]), inc="tVE")
    wait("gpsimd", "dGPF")
    wait("gpsimd", "dIOT")
    waitd("gpsimd", [ocast])
    for tc in range(8):
        for pk in range(2):
            emit("gpsimd", lambda h, tc=tc, pk=pk: h.indirect_dma_start(
                gidx_dram.rearrange("a b -> (a b)").rearrange("(a b) -> a b", b=1),
                bass.IndirectOffsetOnAxis(oi12[:, tc * 2 + pk: tc * 2 + pk + 1], 0),
                iot_sb[:, tc:tc + 1], None,
                bounds_check=bcreg(h, NE * 512 - 1), oob_is_err=False), inc="dGIX", by=16)
    free("sc_sb", "rw_sb", "sc_tok", "p_sb", "t8a", "t8b", "t8c", "rst8",
         "ind_sb", "indT", "posA", "posB", "msk1_sb", "msk2_sb", "offs_tok",
         "oi12f", "tmp16", "identf_sb", "ptrf", "ps_r")

    if debug:
        waitd("sync", cw_dep)
        dma("sync", dbg["d_cw"][:], cw_sb[:], "dOUT")
        waitd("sync", h1b_dep)
        dma("sync", dbg["d_h1"][:], h1b[:], "dOUT")

    fence(("gpsimd", "sync"))
    free("h1T0", "h1T1")
    C = 384
    REGION[0] = "FFN"
    psG = psT("psG", [128, C], F32)
    psU = psT("psU", [128, C], F32)
    psD = [psT("psD0", [128, 256], F32), psT("psD1", [128, 256], F32)]
    ptr2 = [ptr[0], ptr[1]]
    ptr2_dep = [ptr_dep[0], ptr_dep[1]]
    psD_dep = [[], []]
    sct = [sb("sct0", [128, 3 * D], F32), sb("sct1", [128, 3 * D], F32)]
    gathT = [sb("gathT0", [128, 16 * C], BF16), sb("gathT1", [128, 16 * C], BF16)]
    graw = [sb(f"graw{i}", [128, D], BF16) for i in range(4)]
    Wg_buf = [sb("Wg_buf0", [128, 16 * 256], BF16), sb("Wg_buf1", [128, 16 * 256], BF16)]
    Wu_buf = [sb("Wu_buf0", [128, 16 * 256], BF16), sb("Wu_buf1", [128, 16 * 256], BF16)]
    Wd_buf = [sb(f"Wd_buf{i}", [128, 8 * 256], BF16) for i in range(4)]
    sg_bf = sb("sg_bf", [128, C], BF16)
    actT = [sb("actT0", [128, 8 * C], BF16), sb("actT1", [128, 8 * C], BF16)]
    cwg = [sb("cwg0", [128, 24], F32), sb("cwg1", [128, 24], F32)]
    gidxs = [sb("gidxs0", [128, 3], mybir.dt.int32), sb("gidxs1", [128, 3], mybir.dt.int32)]
    emit("vector", lambda h: h.memzero(sct[0][:, 0:D]), inc="tVE")
    for gi in range(4):
        emit("vector", lambda h, gi=gi: h.memzero(graw[gi][:]), inc="tVE")
    zv = ("tVE", cnt["tVE"])
    waitd("sync", [zv])
    for tc in range(8):
        dma("sync", h2_dram[tc * 128:(tc + 1) * 128, :], sct[0][:, 0:D], "dH2Z")

    wg_rd = [[], []]
    wu_rd = [[], []]
    wd_rd = [[], [], [], []]
    psG_dep, psU_dep = [], []
    sg_dep = []
    gidx_rd = [[], []]
    cwg_rd = [[], []]
    graw_rd = [[] for _ in range(4)]
    gathT_rd = [[], []]
    actT_rd = [[], []]
    sct_rd = [[("dH2Z", 128)], []]
    nsct = [0]

    def emit_gather_dma(e):
        be = e % 2
        wait("gpsimd", "dGIX")
        wait("gpsimd", "dH1S")
        wait("gpsimd", "dCWS")
        waitd("gpsimd", gidx_rd[be])
        for k in range(3):
            dma("gpsimd", gidxs[be][:, k:k + 1],
                gidx_dram[e][k * 128:(k + 1) * 128].rearrange("(a b) -> a b", b=1), f"dGI{be}")
        wait("gpsimd", f"dGI{be}")
        waitd("gpsimd", cwg_rd[be])
        for k in range(3):
            gslot = (e * 3 + k) % 4
            waitd("gpsimd", graw_rd[gslot])
            emit("gpsimd", lambda h, gslot=gslot, be=be, k=k: h.indirect_dma_start(
                graw[gslot][:], None, h1_dram[:],
                bass.IndirectOffsetOnAxis(gidxs[be][:, k:k + 1], 0),
                bounds_check=bcreg(h, TOK - 1), oob_is_err=False), inc=f"dGA{be}", by=16)
            emit("gpsimd", lambda h, be=be, k=k: h.indirect_dma_start(
                cwg[be][:, k * 8:(k + 1) * 8], None, cw_dram[:],
                bass.IndirectOffsetOnAxis(gidxs[be][:, k:k + 1], 0),
                bounds_check=bcreg(h, TOK - 1), oob_is_err=False), inc=f"dCG{be}", by=16)
        gidx_rd[be] = [(f"dGA{be}", cnt[f"dGA{be}"])]

    def emit_gather_tp(e):
        REGION[0] = "FFN-gathtp"
        be = e % 2
        wait("tensor", f"dGA{be}", cnt[f"dGA{be}"])
        tptoks = []
        lasttp = {}
        for kc in range(16):
            pidx = kc % 2
            waitd("tensor", ptr2_dep[pidx])
            tp = None
            for k in range(3):
                gslot = (e * 3 + k) % 4
                tp = emit("tensor", lambda h, o=ptr2[pidx][:, k * 128:(k + 1) * 128], gslot=gslot, kc=kc:
                          h.transpose(o, graw[gslot][:, kc * 128:(kc + 1) * 128], identb_sb[:]), inc="tPE")
                if kc == 15:
                    lasttp[gslot] = tp
            eng = "vector" if kc % 2 == 0 else "scalar"
            waitd(eng, [tp])
            waitd(eng, gathT_rd[be])
            ev = copy_on(eng, gathT[be][:, kc * C: kc * C + C], ptr2[pidx][:, 0:C])
            ptr2_dep[pidx] = [ev]
            tptoks.append(ev)
        for k in range(3):
            gslot = (e * 3 + k) % 4
            graw_rd[gslot] = [lasttp[gslot]]
        return tptoks[-2:] + [("tVE", cnt["tVE"]), ("tSC", cnt["tSC"])]

    def flush_scatter(e2, sc_toks2):
        be2 = e2 % 2
        wait("gpsimd", "dH2Z")
        wait("gpsimd", "dSCT", nsct[0] * 16)
        waitd("gpsimd", sc_toks2)
        for tk in range(3):
            emit("gpsimd", lambda h, be=be2, tk=tk: h.indirect_dma_start(
                h2_dram[:], bass.IndirectOffsetOnAxis(gidxs[be][:, tk:tk + 1], 0),
                sct[be][:, tk * D:(tk + 1) * D], None,
                bounds_check=bcreg(h, TOK - 1), oob_is_err=False,
                compute_op=ALU.add), inc="dSCT", by=16)
            nsct[0] += 1
        sct_rd[be2] = [("dSCT", nsct[0] * 16)]

    pend_sct = [None]
    gath_tp_next = [None]
    emit_gather_dma(0)
    for e in range(NE):
        be = e % 2
        gath_done = emit_gather_tp(e)
        REGION[0] = "FFN-gateup"
        if e == 0:
            wait("sync", "dPC", cnt["dPC"])
        for sl in range(4):
            gsem, usem = f"dWg{sl % 2}", f"dWu{sl % 2}"
            waitd("sync", wg_rd[sl % 2])
            dma("sync", Wg_buf[sl % 2][:], WGb[e, sl], gsem)
            waitd("sync", wu_rd[sl % 2])
            dma("sync", Wu_buf[sl % 2][:], WUb[e, sl], usem)
            for fc2 in range(2):
                fc = sl * 2 + fc2
                wait("tensor", gsem)
                wait("tensor", usem)
                waitd("tensor", gath_done)
                waitd("tensor", psG_dep)
                gmm = mm([psG[:]] * 16,
                         [Wg_buf[sl % 2][:, kc * 256 + fc2 * 128: kc * 256 + fc2 * 128 + 128] for kc in range(16)],
                         [gathT[be][:, kc * C: kc * C + C] for kc in range(16)],
                         [kc == 0 for kc in range(16)], [kc == 15 for kc in range(16)])
                waitd("tensor", psU_dep)
                umm = mm([psU[:]] * 16,
                         [Wu_buf[sl % 2][:, kc * 256 + fc2 * 128: kc * 256 + fc2 * 128 + 128] for kc in range(16)],
                         [gathT[be][:, kc * C: kc * C + C] for kc in range(16)],
                         [kc == 0 for kc in range(16)], [kc == 15 for kc in range(16)])
                wg_rd[sl % 2] = [gmm]
                wu_rd[sl % 2] = [umm]
                waitd("scalar", [gmm])
                waitd("scalar", sg_dep)
                sg = emit("scalar", lambda h: h.activation(sg_bf[:], psG[:], AF.Silu), inc="tSC")
                psG_dep = [sg]
                waitd("vector", [sg, umm])
                waitd("vector", actT_rd[be])
                aw = emit("vector", lambda h, o=actT[be][:, fc * C:(fc + 1) * C]:
                          h.tensor_mul(o, sg_bf[:], psU[:]), inc="tVE")
                psU_dep = [aw]
                sg_dep = [aw]
        act_done = [("tVE", cnt["tVE"])]
        gathT_rd[be] = [("tPE", cnt["tPE"])]
        if e < NE - 1:
            emit_gather_dma(e + 1)
        REGION[0] = "FFN-down"
        sc_toks = []
        dmm = None
        wd_tok = {}
        for dg in range(2):
            waitd("sync", wd_rd[dg % 4])
            wd_tok[dg] = dma("sync", Wd_buf[dg % 4][:], WDb[e, dg], f"dWd{dg % 2}")
        for dg in range(8):
            if dg + 2 < 8:
                waitd("sync", wd_rd[(dg + 2) % 4])
                wd_tok[dg + 2] = dma("sync", Wd_buf[(dg + 2) % 4][:], WDb[e, dg + 2], f"dWd{(dg + 2) % 2}")
            for tk in range(3):
                di = (dg * 3 + tk) % 2
                waitd("tensor", [wd_tok[dg]])
                waitd("tensor", act_done)
                waitd("tensor", psD_dep[di])
                dmm = mm([psD[di][:]] * 8,
                         [actT[be][:, kc * C + tk * 128: kc * C + tk * 128 + 128] for kc in range(8)],
                         [Wd_buf[dg % 4][:, kc * 256:(kc + 1) * 256] for kc in range(8)],
                         [kc == 0 for kc in range(8)], [kc == 7 for kc in range(8)])
                wd_rd[dg % 4] = [dmm]
                eng = "vector" if (dg + tk) % 2 == 0 else "scalar"
                waitd(eng, [dmm])
                wait(eng, f"dCG{be}", cnt[f"dCG{be}"])
                waitd(eng, sct_rd[be])
                o = sct[be][:, tk * D + dg * 256: tk * D + dg * 256 + 256]
                cwc = cwg[be][:, k * 8 + e: k * 8 + e + 1] if False else cwg[be][:, tk * 8 + e: tk * 8 + e + 1]
                if eng == "vector":
                    ev = emit(eng, lambda h, o=o, c=cwc, di=di:
                              h.tensor_single_scalar(o, psD[di][:], c, ALU.mult), inc="tVE")
                else:
                    ev = emit(eng, lambda h, o=o, c=cwc, di=di:
                              h.activation(o, psD[di][:], AF.Copy, scale=c), inc="tSC")
                psD_dep[di] = [ev]
                sc_toks.append(ev)
        actT_rd[be] = [("tPE", cnt["tPE"])]
        cwg_rd[be] = sc_toks[-2:]
        flush_scatter(e, sc_toks)

    free("Wg_buf0", "Wg_buf1", "Wu_buf0", "Wu_buf1", "Wd_buf0", "Wd_buf1",
         "sg_bf", "gathT0", "gathT1", "actT0", "actT1",
         "graw0", "graw1", "graw2", "graw3",
         "sct0", "sct1", "cwg0", "cwg1", "gidxs0", "gidxs1")
    REGION[0] = "LN2"
    ob = [sb("ob0", [128, D], F32), sb("ob1", [128, D], F32)]
    h2t = [sb("h2t0", [128, D], F32), sb("h2t1", [128, D], F32)]
    h2t_rd = [[], []]
    ob_dep = [[], []]
    stat_r, sq_r = [], []
    wait("scalar", "dSCT", nsct[0] * 16)
    for tc in range(8):
        bi = tc % 2
        waitd("scalar", h2t_rd[bi])
        dma("scalar", h2t[bi][:], h2_dram[tc * 128:(tc + 1) * 128, :], f"dH2L{bi}")
        wait("vector", f"dH2L{bi}", cnt[f"dH2L{bi}"])
        waitd("vector", h1b_dep)
        waitd("vector", hpre_dep[bi])
        waitd("vector", stat_r)
        rs = emit("vector", lambda h, o=hpre[bi][:], a=h1b[:, tc * D: tc * D + D],
                  b=h2t[bi][:]:
                  h.scalar_tensor_tensor(o, a, 1.0, b, ALU.mult, ALU.add, accum_out=stat[:, 0:1]), inc="tVE")
        h2t_rd[bi] = [rs]
        rstd_tok = ln_stats(bi, rs)
        sq_r = [rstd_tok]
        waitd("vector", [rstd_tok])
        waitd("vector", ob_dep[bi])
        vsync()
        nh = emit("vector", lambda h, o=ob[bi][:], i=hpre[bi][:]:
                  h.tensor_scalar(o, i, stat[:, 6:7], stat[:, 10:11], ALU.subtract, ALU.mult), inc="tVE")
        hpre_dep[bi] = [nh]
        stat_r = [nh]
        waitd("sync", [nh])
        st_tok = dma("sync", out[tc * 128:(tc + 1) * 128, :], ob[bi][:], "dOUT")
        ob_dep[bi] = [st_tok]

    wait("sync", "dOUT")

    with es:
        block = es.enter_context(nc.Block())
        for eng in ENGINES:
            getattr(block, eng)(lambda h, lst=ops[eng]: [f(h) for f in lst])
    return nc


def _sw(W, k):
    K2, n = W.shape
    assert K2 == k * 128
    return np.ascontiguousarray(
        W.reshape(k, 128, n).transpose(1, 0, 2).reshape(128, k * n))


def _host_weights(inputs):
    H_ = H
    perm = []
    for qd in range(H_ // 4):
        for b2 in (0, 1):
            for i_ in range(4):
                perm.extend((4 * qd + i_) * 64 + b2 * 32 + np.arange(32))
    perm = np.asarray(perm)
    Wqr = np.asarray(inputs["L0_attn_Wq_rope"])[:, perm]
    Wkr = np.asarray(inputs["L0_attn_Wk_rope"])[:, perm]
    Wo_ = np.asarray(inputs["L0_attn_Wo"])
    G = np.asarray(inputs["L0_ffn_gate_proj"])
    U = np.asarray(inputs["L0_ffn_up_proj"])
    Dn = np.asarray(inputs["L0_ffn_down_proj"])
    d = {
        "Wq_down": _sw(np.asarray(inputs["L0_attn_Wq_down"]), 16),
        "Wq_up": _sw(np.asarray(inputs["L0_attn_Wq_up"]), 8),
        "Wq_rope": _sw(Wqr, 8),
        "Wkv_down": _sw(np.asarray(inputs["L0_attn_Wkv_down"]), 16),
        "Wk_up": np.stack([_sw(np.ascontiguousarray(
            np.asarray(inputs["L0_attn_Wk_up"])[:, hp * 128:(hp + 1) * 128]), 4) for hp in range(8)]),
        "Wk_rope": np.stack([_sw(np.ascontiguousarray(Wkr[:, hh * 512:(hh + 1) * 512]), 16)
                             for hh in range(2)]),
        "Wv_up": np.stack([_sw(np.ascontiguousarray(
            np.asarray(inputs["L0_attn_Wv_up"])[:, hp * 256:(hp + 1) * 256]), 4) for hp in range(8)]),
        "Wo": np.stack([_sw(np.ascontiguousarray(Wo_[hh * 1024:(hh + 1) * 1024, :]), 8)
                        for hh in range(2)]),
        "router_w": _sw(np.asarray(inputs["L0_ffn_router_w"]), 16),
        "gate_proj": np.stack([np.stack([_sw(np.ascontiguousarray(G[e][:, sl * 256:(sl + 1) * 256]), 16)
                                         for sl in range(4)]) for e in range(NE)]),
        "up_proj": np.stack([np.stack([_sw(np.ascontiguousarray(U[e][:, sl * 256:(sl + 1) * 256]), 16)
                                       for sl in range(4)]) for e in range(NE)]),
        "down_proj": np.stack([np.stack([_sw(np.ascontiguousarray(Dn[e][:, dg * 256:(dg + 1) * 256]), 8)
                                         for dg in range(8)]) for e in range(NE)]),
        "identb": np.eye(128, dtype=ml_dtypes.bfloat16),
        "identf": np.eye(128, dtype=np.float32),
        "iotatok": (np.arange(128)[:, None] + 128 * np.arange(8)[None, :]).astype(np.int32),
        "oobfill": np.full((NE, 512), 10 ** 6, np.int32),
        "ebase": (np.arange(8)[:, None] * 512).astype(np.float32),
    }
    return d


def _host_inputs(inputs, core):
    b, sub = core // 2, core % 2
    tiles = TILES[sub]
    x = np.asarray(inputs["L0_attn_x"])
    x_kv = np.ascontiguousarray(x[b])
    x_own = np.ascontiguousarray(
        np.concatenate([x[b, g * 128:(g + 1) * 128, :] for g in tiles], axis=0))
    m = np.zeros((8, 128, 256), np.float32)
    for j, g in enumerate(tiles):
        E = EXT[j]
        rows = g * 128 + np.arange(128)
        cols = (E - 2) * 128 + np.arange(256)
        m[j] = np.where(cols[None, :] <= rows[:, None], 0.0, NEG)
    half = RD // 2
    inv = 1.0 / (10000.0 ** (np.arange(half, dtype=np.float64) / half))

    def tables(pos, scale):
        ang = pos[None, :] * inv[:, None]
        c = (np.cos(ang) * scale).astype(np.float32)
        s = (np.sin(ang) * scale).astype(np.float32)
        return np.tile(c, (4, 1)), np.tile(s, (4, 1))

    posq = np.concatenate([g * 128 + np.arange(128) for g in tiles]).astype(np.float64)
    posk = np.arange(S, dtype=np.float64)
    cq, sq = tables(posq, SCALE)
    ck, sk = tables(posk, 1.0)
    return {
        "x_kv": x_kv, "x_own": x_own,
        "maskb": m.astype(ml_dtypes.bfloat16),
        "cosq": cq, "sinq": sq, "cosk": ck, "sink": sk,
    }


def kernel(**inputs):
    if "nc" not in _cache:
        _cache["nc"] = build_program(debug=False)
    nc = _cache["nc"]
    shared = _host_weights(inputs)
    in_maps = []
    for c in range(8):
        d = dict(shared)
        d.update(_host_inputs(inputs, c))
        in_maps.append(d)
    res = run_bass_kernel_spmd(nc, in_maps, core_ids=list(range(8)))
    y = np.empty((B, S, D), np.float32)
    for c in range(8):
        b, sub = c // 2, c % 2
        o = res.results[c]["out"]
        for j, g in enumerate(TILES[sub]):
            y[b, g * 128:(g + 1) * 128, :] = o[j * 128:(j + 1) * 128, :]
    return y
